# version 11
# baseline (speedup 1.0000x reference)
import numpy as np
import ml_dtypes
from contextlib import ExitStack

import concourse.bass as bass
import concourse.mybir as mybir
from concourse.bass_utils import run_bass_kernel_spmd

F32 = mybir.dt.float32
BF16 = mybir.dt.bfloat16
AF = mybir.ActivationFunctionType
ALU = mybir.AluOpType
AX = mybir.AxisListType

NCORES = 8
D = 2048
BATCH = 4
SEQ = 2048
T = 1024
DC = D // 128
TTI = T // 128


class Prog:
    ENGS = ("tensor", "vector", "scalar", "gpsimd", "sync")
    DMA_ENGS = ("gpsimd", "sync", "scalar")

    STRICT_SAME_ENGINE = True

    def __init__(self, nc, ctx):
        self.nc = nc
        self.ctx = ctx
        self.qsem = {e: ctx.enter_context(nc.semaphore(f"q_{e}")) for e in self.ENGS}
        self.qc = {e: 0 for e in self.ENGS}
        self.dsem = {}
        self.queue = {e: [] for e in self.ENGS}
        self.wr = {}
        self.rd = {}
        self.clock = {e: {} for e in self.ENGS}

    def _semh(self, kk):
        return self.qsem[kk[1]] if kk[0] == "q" else self.dsem[kk[1]][0]

    def _needs(self, eng, dma, reads, writes, extra):
        deps = []
        for k in reads:
            for ev in self.wr.get(k, ()):
                deps.append((ev, False))
        for k in writes:
            for ev in self.wr.get(k, ()):
                deps.append((ev, True))
            for rev in self.rd.get(k, {}).values():
                deps.append((rev, True))
        for ev in extra:
            deps.append((ev, False))
        waits = {}
        myclock = self.clock[eng]
        for (kk, cnt, evclock), order_only in deps:
            if (order_only and kk == ("q", eng) and not dma
                    and not self.STRICT_SAME_ENGINE):
                continue
            if myclock.get(kk, 0) >= cnt:
                continue
            waits[kk] = max(waits.get(kk, 0), cnt)
            for ck, cv in evclock.items():
                if myclock.get(ck, 0) < cv:
                    myclock[ck] = cv
            myclock[kk] = max(myclock.get(kk, 0), cnt)
        return waits

    def _record(self, eng, kk, cnt, reads, writes):
        evclock = dict(self.clock[eng])
        evclock[kk] = cnt
        ev = (kk, cnt, evclock)
        for k in writes:
            evs = [e for e in self.wr.get(k, ()) if e[0] != kk]
            evs.append(ev)
            self.wr[k] = evs
            self.rd[k] = {}
        for k in reads:
            self.rd.setdefault(k, {})[kk] = ev
        return ev

    def op(self, eng, fn, reads=(), writes=(), extra=()):
        waits = self._needs(eng, False, reads, writes, extra)
        self.qc[eng] += 1
        ev = self._record(eng, ("q", eng), self.qc[eng], reads, writes)
        self.queue[eng].append((waits, fn, ("q", eng), 1))
        return ev

    def dma(self, eng, fn, reads=(), writes=(), extra=(), n=1, sem=None):
        assert sem is not None, "dma ops need an explicit sem stream name"
        waits = self._needs(eng, True, reads, writes, extra)
        if sem not in self.dsem:
            self.dsem[sem] = [self.ctx.enter_context(self.nc.semaphore(f"dma_{sem}")), 0]
        self.dsem[sem][1] += 16 * n
        ev = self._record(eng, ("dma", sem), self.dsem[sem][1], reads, writes)
        self.queue[eng].append((waits, fn, ("dma", sem), n))
        return ev

    def coll(self, fn, reads=(), writes=(), extra=(), sem=None):
        assert sem is not None
        eng = "gpsimd"
        waits = self._needs(eng, True, reads, writes, extra)
        if sem not in self.dsem:
            self.dsem[sem] = [self.ctx.enter_context(self.nc.semaphore(f"dma_{sem}")), 0]
        self.dsem[sem][1] += 1
        ev = self._record(eng, ("dma", sem), self.dsem[sem][1], reads, writes)
        self.queue[eng].append((waits, fn, ("dmac", sem), 1))
        return ev

    def barrier_all(self, eng="sync"):
        waits = {}
        for e in self.ENGS:
            if self.qc[e] and e != eng:
                waits[("q", e)] = self.qc[e]
        for name, (h, cnt) in self.dsem.items():
            waits[("dma", name)] = cnt
        self.queue[eng].append((waits, None, ("q", eng), 0))

    def finalize(self):
        nc = self.nc
        with nc.Block() as block:
            for ename in self.ENGS:
                if not self.queue[ename]:
                    continue
                def make_body(ename):
                    def body(eng):
                        for waits, fn, kk, n in self.queue[ename]:
                            for wkk, v in sorted(waits.items()):
                                eng.wait_ge(self._semh(wkk), v)
                            if fn is None:
                                continue
                            insts = fn(eng)
                            if kk[0] == "dmac":
                                insts.then_inc(self.dsem[kk[1]][0], 1)
                            elif kk[0] == "dma":
                                if not isinstance(insts, (list, tuple)):
                                    insts = [insts]
                                assert len(insts) == n, f"{ename}: dma op declared n={n} got {len(insts)}"
                                for inst in insts:
                                    inst.then_inc(self.dsem[kk[1]][0], 16)
                            else:
                                if isinstance(insts, (list, tuple)):
                                    insts = insts[-1]
                                insts.then_inc(self.qsem[ename], 1)
                    return body
                getattr(block, ename)(make_body(ename))


def tile_w_out_in(w):
    O, K = w.shape
    return np.ascontiguousarray(
        w.reshape(O // 128, 128, K // 128, 128).transpose(0, 2, 3, 1))


def tile_w_in_out(w):
    K, O = w.shape
    return np.ascontiguousarray(
        w.reshape(K // 128, 128, O // 128, 128).transpose(2, 0, 1, 3))


def tile_w_out_in_wide(w, ow=512):
    O, K = w.shape
    return np.ascontiguousarray(
        w.reshape(O // ow, ow, K // 128, 128).transpose(0, 2, 3, 1))


def emit_ln_tm(p, xs, xkeys, out_ap, out_key, stats, i, sq, eps_col, n=D,
               sqk="ln.sq", pref="ln"):
    if isinstance(sqk, str):
        sqk = [sqk]
    sqw = sq.shape[1]
    """LayerNorm token-major: xs [128, n] fp32 -> out_ap (bf16 or f32).
    stats: dict of sbuf [128, >=ntiles] fp32 buffers m0,s0,m,var,rstd; i = stat column.
    sq: [128, n] f32 scratch."""
    m0 = stats["m0"]; s0 = stats["s0"]; m = stats["m"]; var = stats["var"]; rstd = stats["rstd"]
    p.op("vector", lambda v: v.reduce_sum(m0[:, i:i + 1], xs, axis=AX.X),
         reads=xkeys, writes=[f"{pref}.m0.{i}"])
    nh_ = (n + sqw - 1) // sqw
    for h_ in range(nh_):
        lo, hi = h_ * sqw, min((h_ + 1) * sqw, n)
        p.op("scalar", lambda s, lo=lo, hi=hi: s.activation(sq[:, 0:hi - lo], xs[:, lo:hi], AF.Square),
             reads=xkeys, writes=list(sqk))
        dstc = s0 if h_ == 0 else var
        p.op("vector", lambda v, dstc=dstc, hi=hi, lo=lo: v.reduce_sum(
                dstc[:, i:i + 1], sq[:, 0:hi - lo], axis=AX.X),
             reads=list(sqk), writes=[f"{pref}.s0.{i}" if dstc is s0 else f"{pref}.s0b.{i}"])
        if h_ > 0:
            p.op("vector", lambda v: v.tensor_add(s0[:, i:i + 1], s0[:, i:i + 1], var[:, i:i + 1]),
                 reads=[f"{pref}.s0.{i}", f"{pref}.s0b.{i}"], writes=[f"{pref}.s0.{i}"])
    p.op("vector", lambda v: v.tensor_scalar_mul(m[:, i:i + 1], m0[:, i:i + 1], 1.0 / n),
         reads=[f"{pref}.m0.{i}"], writes=[f"{pref}.m.{i}"])
    p.op("vector", lambda v: v.tensor_scalar(
            var[:, i:i + 1], s0[:, i:i + 1], 1.0 / n, None, ALU.mult),
         reads=[f"{pref}.s0.{i}"], writes=[f"{pref}.va.{i}"])
    p.op("vector", lambda v: v.scalar_tensor_tensor(
            var[:, i:i + 1], m[:, i:i + 1], m[:, i:i + 1], var[:, i:i + 1],
            ALU.mult, ALU.subtract),
         reads=[f"{pref}.m.{i}", f"{pref}.va.{i}"], writes=[f"{pref}.var.{i}"])
    p.op("scalar", lambda s: s.activation(rstd[:, i:i + 1], var[:, i:i + 1], AF.Sqrt,
                                          bias=eps_col, scale=-1.0),
         reads=[f"{pref}.var.{i}", "consts"], writes=[f"{pref}.std.{i}"])
    p.op("vector", lambda v: v.reciprocal(rstd[:, i:i + 1], rstd[:, i:i + 1]),
         reads=[f"{pref}.std.{i}"], writes=[f"{pref}.rstd.{i}"])
    p.op("vector", lambda v: v.tensor_scalar(
            out_ap, xs, m[:, i:i + 1], rstd[:, i:i + 1], ALU.subtract, ALU.mult),
         reads=list(xkeys) + [f"{pref}.m.{i}", f"{pref}.rstd.{i}"], writes=[out_key])


def emit_transpose_tm_to_fm(p, src_tm, src_keys, dst_fm, dst_key, ident, pbanks, pbank_keys,
                            n_ttiles, n_dchunks, t_per=1024, ident_key="identb"):
    grp = 8
    nb = len(pbanks)
    g = 0
    for i in range(n_ttiles):
        for kc0 in range(0, n_dchunks, grp):
            bank = pbanks[g % nb]
            bkey = pbank_keys[g % nb]
            kcs = list(range(kc0, min(kc0 + grp, n_dchunks)))
            def f_tr(t, i=i, kcs=kcs, bank=bank):
                insts = []
                for j, kc in enumerate(kcs):
                    insts.append(t.transpose(
                        bank[:, j * 128:(j + 1) * 128],
                        src_tm[:, i * (128 * n_dchunks) + kc * 128: i * (128 * n_dchunks) + (kc + 1) * 128],
                        ident[:, :]))
                return insts
            p.op("tensor", f_tr, reads=[src_keys(i), ident_key], writes=[bkey], extra=())
            def f_ev(v, i=i, kcs=kcs, bank=bank):
                dst3 = dst_fm[:, :].rearrange("p (kc t) -> p kc t", t=t_per)
                dst = dst3[:, kcs[0]:kcs[0] + len(kcs), i * 128:(i + 1) * 128]
                return v.tensor_copy(dst, bank[:, 0:len(kcs) * 128].rearrange("p (kc t) -> p kc t", t=128))
            p.op("vector", f_ev, reads=[bkey], writes=[f"{dst_key}.ev.{i}.{kc0}"])
            g += 1


def build_launch_A():
    nc = bass.Bass("TRN2", target_bir_lowering=False)
    xh = nc.dram_tensor("xh", [T, D], F32, kind="ExternalInput").ap()
    w_in = nc.dram_tensor("w_in", [64, DC, 128, 128], F32, kind="ExternalInput").ap()
    ident = nc.dram_tensor("ident", [128, 128], F32, kind="ExternalInput").ap()
    z_fm = nc.dram_tensor("z_fm", [D, T], BF16, kind="ExternalOutput").ap()
    xbc_fm = nc.dram_tensor("xbc_fm", [3 * D, T], BF16, kind="ExternalOutput").ap()
    xn0b = nc.dram_tensor("xn0b", [D, T], BF16, kind="ExternalOutput").ap()

    ctx = ExitStack()
    xt = ctx.enter_context(nc.sbuf_tensor("xt", [128, TTI * D], F32))
    xnb = ctx.enter_context(nc.sbuf_tensor("xnb", [128, TTI * D], BF16))
    xn_fm = ctx.enter_context(nc.sbuf_tensor("xn_fm", [128, DC * T], BF16))
    identb = ctx.enter_context(nc.sbuf_tensor("identb", [128, 128], BF16))
    sq = ctx.enter_context(nc.sbuf_tensor("sq", [128, D], F32))
    stats = {k: ctx.enter_context(nc.sbuf_tensor(f"st_{k}", [128, TTI], F32))
             for k in ("m0", "s0", "m", "var", "rstd")}
    wsl = [ctx.enter_context(nc.sbuf_tensor(f"wsl{j}", [128, DC * 128], BF16)) for j in range(2)]
    stage = [ctx.enter_context(nc.sbuf_tensor(f"stg{j}", [128, T], BF16)) for j in range(2)]
    ptr = [ctx.enter_context(nc.psum_tensor(f"ptr{j}", [128, 512], F32)) for j in range(2)]
    pmm = [ctx.enter_context(nc.psum_tensor(f"pmm{j}", [128, 512], F32)) for j in range(4)]

    consts = ctx.enter_context(nc.sbuf_tensor("consts", [128, 4], F32))

    p = Prog(nc, ctx)
    p.op("vector", lambda v: v.memset(consts[:, 0:1], 1e-5), writes=["consts"])

    p.dma("gpsimd", lambda g: g.dma_start(identb[:, :], ident[:, :]), writes=["identb"], sem="identb")
    p.dma("gpsimd", lambda g: g.dma_start(
        xt[:, :].rearrange("p (tt d) -> p tt d", d=D),
        xh.rearrange("(tt p) d -> p tt d", p=128)), writes=[f"xt.{i}" for i in range(TTI)], sem="xt")

    for i in range(TTI):
        emit_ln_tm(p, xt[:, i * D:(i + 1) * D], [f"xt.{i}"],
                   xnb[:, i * D:(i + 1) * D], f"ln.out.{i}", stats, i, sq,
                   eps_col=consts[:, 0:1])

    ptr_b = [b.bitcast(BF16) for b in ptr]
    emit_transpose_tm_to_fm(p, xnb, lambda i: f"ln.out.{i}", xn_fm, "xn_fm",
                            identb, ptr_b, ["ptrb0", "ptrb1"], TTI, DC, t_per=T)
    xn_fm_ready = [f"xn_fm.ev.{i}.{kc0}" for i in range(TTI) for kc0 in (0, 8)]

    p.dma("sync", lambda s: s.dma_start(
        xn0b.rearrange("(kc p) t -> p kc t", p=128),
        xn_fm[:, :].rearrange("p (kc t) -> p kc t", t=T)),
        reads=xn_fm_ready, writes=["xn0b_out"], sem="xn0b_out")

    for oc in range(64):
        par = oc % 2
        p.dma("gpsimd", lambda g, oc=oc, par=par: g.dma_start(
            wsl[par][:, :].rearrange("p (kc o) -> p kc o", o=128),
            w_in[oc].rearrange("kc k o -> k kc o")),
            reads=[], writes=[f"wsl.{par}"], sem=f"wsl{par}")
        pbs = [pmm[par * 2], pmm[par * 2 + 1]]
        pbks = [f"pmm.{par * 2}", f"pmm.{par * 2 + 1}"]
        def f_mm(t, oc=oc, par=par, pbs=pbs):
            for kc in range(DC):
                for nh in range(2):
                    inst = t.matmul(
                        pbs[nh][:, :],
                        wsl[par][:, kc * 128:(kc + 1) * 128],
                        xn_fm[:, kc * T + nh * 512: kc * T + (nh + 1) * 512],
                        start=(kc == 0), stop=(kc == DC - 1))
            return inst
        p.op("tensor", f_mm, reads=[f"wsl.{par}"] + xn_fm_ready, writes=pbks)
        for nh in range(2):
            p.op("vector", lambda v, par=par, nh=nh, pb=pbs[nh]: v.tensor_copy(
                stage[par][:, nh * 512:(nh + 1) * 512], pb[:, :]),
                reads=[pbks[nh]], writes=[f"stage.{par}.{nh}"])
        dst = z_fm if oc < 16 else xbc_fm
        row = oc * 128 if oc < 16 else (oc - 16) * 128
        p.dma("sync", lambda s, dst=dst, row=row, par=par: s.dma_start(
            dst[row:row + 128, :], stage[par][:, :]),
            reads=[f"stage.{par}.0", f"stage.{par}.1"], writes=[f"out.{oc}"], sem=f"st{par}")

    p.barrier_all("sync")
    p.finalize()
    ctx.close()
    return nc


def _run(nc, in_maps):
    return run_bass_kernel_spmd(nc, in_maps, core_ids=list(range(NCORES))).results


def build_launch_A2():
    nc = bass.Bass("TRN2", target_bir_lowering=False)
    xn0b = nc.dram_tensor("xn0b", [D, T], BF16, kind="ExternalInput").ap()
    w_gu = nc.dram_tensor("w_gu", [128, DC, 128, 128], F32, kind="ExternalInput").ap()
    w_d = nc.dram_tensor("w_d", [4, 64, 128, 512], F32, kind="ExternalInput").ap()
    g0 = nc.dram_tensor("g0", [T, D], BF16, kind="ExternalOutput").ap()

    ctx = ExitStack()
    xn_fm = ctx.enter_context(nc.sbuf_tensor("xn_fm", [128, DC * T], BF16))
    h = ctx.enter_context(nc.sbuf_tensor("h", [128, 64 * T], BF16))
    wsl = [ctx.enter_context(nc.sbuf_tensor(f"wsl{j}", [128, DC * 128], BF16)) for j in range(4)]
    wdsl = [ctx.enter_context(nc.sbuf_tensor(f"wdsl{j}", [128, 8 * 512], BF16)) for j in range(2)]
    sg = ctx.enter_context(nc.sbuf_tensor("sg", [128, T], F32))
    stage = [ctx.enter_context(nc.sbuf_tensor(f"stg{j}", [128, 512], BF16)) for j in range(8)]
    pmm = [ctx.enter_context(nc.psum_tensor(f"pmm{j}", [128, 512], F32)) for j in range(8)]

    p = Prog(nc, ctx)
    p.dma("gpsimd", lambda g: g.dma_start(
        xn_fm[:, :].rearrange("p (kc t) -> p kc t", t=T),
        xn0b.rearrange("(kc p) t -> p kc t", p=128)), writes=["xn_fm"], sem="xn_fm")

    for i in range(64):
        par = i % 2
        wg, wu = wsl[par * 2], wsl[par * 2 + 1]
        p.dma("gpsimd", lambda g, i=i, wg=wg: g.dma_start(
            wg[:, :].rearrange("p (kc o) -> p kc o", o=128),
            w_gu[i].rearrange("kc k o -> k kc o")), writes=[f"wg.{par}"], sem=f"wg{par}")
        p.dma("gpsimd", lambda g, i=i, wu=wu: g.dma_start(
            wu[:, :].rearrange("p (kc o) -> p kc o", o=128),
            w_gu[64 + i].rearrange("kc k o -> k kc o")), writes=[f"wu.{par}"], sem=f"wu{par}")
        pg = [pmm[par * 4], pmm[par * 4 + 1]]
        pu = [pmm[par * 4 + 2], pmm[par * 4 + 3]]
        pgk = [f"pmm.{par * 4}", f"pmm.{par * 4 + 1}"]
        puk = [f"pmm.{par * 4 + 2}", f"pmm.{par * 4 + 3}"]
        def f_mm(t, wg=wg, wu=wu, pg=pg, pu=pu):
            for kc in range(DC):
                for nh in range(2):
                    rhs = xn_fm[:, kc * T + nh * 512: kc * T + (nh + 1) * 512]
                    t.matmul(pg[nh][:, :], wg[:, kc * 128:(kc + 1) * 128], rhs,
                             start=(kc == 0), stop=(kc == DC - 1))
                    inst = t.matmul(pu[nh][:, :], wu[:, kc * 128:(kc + 1) * 128], rhs,
                                    start=(kc == 0), stop=(kc == DC - 1))
            return inst
        p.op("tensor", f_mm, reads=[f"wg.{par}", f"wu.{par}", "xn_fm"], writes=pgk + puk)
        for nh in range(2):
            p.op("scalar", lambda s, nh=nh, pg=pg: s.activation(
                sg[:, nh * 512:(nh + 1) * 512], pg[nh][:, :], AF.Silu),
                reads=[pgk[nh]], writes=[f"sg.{nh}"])
            p.op("vector", lambda v, i=i, nh=nh, pu=pu: v.tensor_mul(
                h[:, i * T + nh * 512: i * T + (nh + 1) * 512],
                sg[:, nh * 512:(nh + 1) * 512], pu[nh][:, :]),
                reads=[f"sg.{nh}", puk[nh]], writes=[f"h.{i}"])

    for oc in range(4):
        for icg in range(8):
            par = icg % 2
            p.dma("gpsimd", lambda g, oc=oc, icg=icg, par=par: g.dma_start(
                wdsl[par][:, :].rearrange("p (ic o) -> p ic o", o=512),
                w_d[oc, icg * 8:(icg + 1) * 8].rearrange("ic k o -> k ic o")),
                writes=[f"wd.{par}"], sem=f"wd{par}")
            def f_dn(t, oc=oc, icg=icg, par=par):
                for ici in range(8):
                    ic = icg * 8 + ici
                    for tt in range(TTI):
                        inst = t.matmul(
                            pmm[tt][:, :],
                            h[:, ic * T + tt * 128:(ic * T) + (tt + 1) * 128],
                            wdsl[par][:, ici * 512:(ici + 1) * 512],
                            start=(ic == 0), stop=(ic == 63))
                return inst
            p.op("tensor", f_dn,
                 reads=[f"wd.{par}"] + [f"h.{icg * 8 + j}" for j in range(8)],
                 writes=[f"pmm.{tt}" for tt in range(TTI)])
        for tt in range(TTI):
            p.op("scalar", lambda s, tt=tt: s.activation(
                stage[tt][:, :], pmm[tt][:, :], AF.Silu),
                reads=[f"pmm.{tt}"], writes=[f"stage.{tt}"])
            p.dma("sync", lambda s, oc=oc, tt=tt: s.dma_start(
                g0[tt * 128:(tt + 1) * 128, oc * 512:(oc + 1) * 512], stage[tt][:, :]),
                reads=[f"stage.{tt}"], writes=[f"g0out.{oc}.{tt}"], sem=f"g0st{tt}")

    p.barrier_all("sync")
    p.finalize()
    ctx.close()
    return nc


def prep_host(inputs):
    hp = {}
    hp["ident"] = np.eye(128, dtype=np.float32)
    hp["w_in"] = tile_w_out_in(np.ascontiguousarray(inputs["L0_attn_W_in"][:8192]))
    hp["w_gu"] = tile_w_out_in(inputs["L0_ffn_w_gate_up"])
    hp["w_d"] = tile_w_out_in_wide(inputs["L0_ffn_w_down"], ow=512)
    return hp


def run_phase1(inputs, upto="Z"):
    x = np.asarray(inputs["L0_attn_u"])
    hp = prep_host(inputs)
    inter = {}

    def core_tok(core):
        b, h = core // 2, core % 2
        return b, slice(h * 1024, (h + 1) * 1024)

    nc = build_launch_A()
    im = []
    for core in range(NCORES):
        b, sl = core_tok(core)
        im.append({"xh": np.ascontiguousarray(x[b, sl]), "w_in": hp["w_in"],
                   "ident": hp["ident"]})
    ra = _run(nc, im)
    inter["A"] = ra
    if upto == "A":
        return inter

    nc = build_launch_A2()
    im = []
    for core in range(NCORES):
        im.append({"xn0b": np.ascontiguousarray(ra[core]["xn0b"]),
                   "w_gu": hp["w_gu"], "w_d": hp["w_d"]})
    ra2 = _run(nc, im)
    inter["A2"] = ra2
    if upto == "A2":
        return inter

    A_log = np.asarray(inputs["L0_attn_A_log"], np.float32)
    conv_w = np.asarray(inputs["L0_attn_conv_w"], np.float32)
    conv_b = np.asarray(inputs["L0_attn_conv_b"], np.float32)
    A_log_rep = np.repeat(A_log, 64)
    mask_ssd = np.vstack([np.triu(np.ones((64, 64), np.float32))] * 2)
    xbc_b = [np.concatenate([ra[2 * b]["xbc_fm"], ra[2 * b + 1]["xbc_fm"]], axis=1)
             for b in range(BATCH)]
    nc = build_launch_B()
    im = []
    for core in range(NCORES):
        b, hh = core // 2, core % 2
        ch = np.concatenate([np.arange(hh * 1024, hh * 1024 + 1024),
                             2048 + np.arange(hh * 1024, hh * 1024 + 1024),
                             4096 + np.arange(hh * 1024, hh * 1024 + 1024)])
        cw_dev = np.ascontiguousarray(
            conv_w[ch, 0, :].reshape(24, 128, 3).transpose(1, 0, 2).reshape(128, 72))
        cb_dev = np.ascontiguousarray(conv_b[ch].reshape(24, 128).T)
        alog_dev = np.ascontiguousarray(
            A_log_rep[hh * 1024:(hh + 1) * 1024].reshape(8, 128).T)
        im.append({
            "xp": np.ascontiguousarray(xbc_b[b][hh * 1024:(hh + 1) * 1024]),
            "Bp": np.ascontiguousarray(xbc_b[b][2048 + hh * 1024: 2048 + (hh + 1) * 1024]),
            "Cp": np.ascontiguousarray(xbc_b[b][4096 + hh * 1024: 4096 + (hh + 1) * 1024]),
            "cw": cw_dev, "cb": cb_dev, "alog": alog_dev,
            "ident": hp["ident"], "mask": mask_ssd,
        })
    rb = _run(nc, im)
    inter["B"] = rb
    if upto == "B":
        return inter

    w_out_t = tile_w_out_in_wide(np.asarray(inputs["L0_attn_W_out"], np.float32), 512)
    rmsw_dev = np.ascontiguousarray(
        np.asarray(inputs["L0_attn_rms_w"], np.float32).reshape(DC, 128).T)
    y_full = [np.concatenate([rb[2 * b]["y_fm"], rb[2 * b + 1]["y_fm"]], axis=0)
              for b in range(BATCH)]
    nc = build_launch_C()
    im = []
    for core in range(NCORES):
        b, h = core // 2, core % 2
        sl = slice(h * 1024, (h + 1) * 1024)
        im.append({
            "y_sl": np.ascontiguousarray(y_full[b][:, sl]),
            "z_fm": np.ascontiguousarray(ra[core]["z_fm"]),
            "xh": np.ascontiguousarray(x[b, sl]),
            "g0": np.ascontiguousarray(ra2[core]["g0"]),
            "w_out": w_out_t, "rmsw": rmsw_dev, "ident": hp["ident"],
        })
    rc = _run(nc, im)
    inter["C"] = rc
    if upto == "C":
        return inter

    lnw = np.asarray(inputs["L1_ffn_ln_weight"], np.float32)
    lnb = np.asarray(inputs["L1_ffn_ln_bias"], np.float32)
    lnwb_dev = np.ascontiguousarray(np.concatenate(
        [lnw.reshape(DC, 128).T, lnb.reshape(DC, 128).T], 1))
    gamrow = np.concatenate([np.asarray(inputs["L1_ffn_gamma"], np.float32),
                             np.asarray(inputs["L1_ffn_beta"], np.float32)])[None, :]
    w1t = tile_w_out_in_wide(np.asarray(inputs["L1_ffn_weight"], np.float32), 512)
    nc = build_launch_C2()
    im = [{"xn1b_fm": np.ascontiguousarray(rc[core]["xn1b_fm"]), "w1": w1t,
           "lnwb": lnwb_dev, "gamrow": gamrow, "ident": hp["ident"]}
          for core in range(NCORES)]
    rc2 = _run(nc, im)
    inter["C2"] = rc2
    if upto == "C2":
        return inter

    w_qkv = np.asarray(inputs["L1_attn_w_qkv"], np.float32)
    tiles = np.ascontiguousarray(w_qkv.reshape(DC, 128, 48, 128).transpose(2, 0, 1, 3))
    ln_s = np.asarray(inputs["L1_attn_ln_scale"], np.float32)
    ln_b = np.asarray(inputs["L1_attn_ln_bias"], np.float32)
    inv = 1.0 / 10000.0 ** (np.arange(0, 128, 2) / 128)
    tgrid = np.arange(SEQ)[:, None] * inv[None, :]
    cos2 = np.vstack([np.cos(tgrid).T] * 2).astype(np.float32)
    sin2 = np.vstack([np.sin(tgrid).T] * 2).astype(np.float32)
    cs2 = np.ascontiguousarray(np.hstack([cos2, sin2]))
    mask4 = np.zeros((128, 4 * 512), np.float32)
    for off in range(4):
        for pz in range(128):
            mask4[pz, off * 512 + off * 128 + pz:(off + 1) * 512] = 1.0
    xn1_full = [np.concatenate([rc[2 * b]["xn1b_fm"], rc[2 * b + 1]["xn1b_fm"]], axis=1)
                for b in range(BATCH)]
    nc = build_launch_D()
    im = []
    for core in range(NCORES):
        b, hh = core // 2, core % 2
        wsel = np.stack([np.stack([tiles[hh * 8 + hl], tiles[16 + hh * 8 + hl],
                                   tiles[32 + hh * 8 + hl]]) for hl in range(8)])
        im.append({"xn1_fm": np.ascontiguousarray(xn1_full[b]),
                   "wqkv": np.ascontiguousarray(wsel),
                   "lns": np.ascontiguousarray(np.stack([ln_s, ln_b], 1)),
                   "cs2": cs2, "mask4": mask4,
                   "swp": np.roll(np.eye(128, dtype=np.float32), 64, axis=1)})
    rd = _run(nc, im)
    inter["D"] = rd
    if upto == "D":
        return inter

    wout1t = tile_w_in_out_wide(np.asarray(inputs["L1_attn_w_out"], np.float32), 512)
    attn_full = [np.concatenate([rd[2 * b]["attn"], rd[2 * b + 1]["attn"]], axis=0)
                 for b in range(BATCH)]
    nc = build_launch_E()
    im = []
    for core in range(NCORES):
        b, h = core // 2, core % 2
        sl = slice(h * 1024, (h + 1) * 1024)
        im.append({"attn_sl": np.ascontiguousarray(attn_full[b][:, sl]),
                   "w_out1": wout1t,
                   "x1": np.ascontiguousarray(rc[core]["x1"]),
                   "g1": np.ascontiguousarray(rc2[core]["g1"])})
    re_ = _run(nc, im)
    inter["E"] = re_
    return inter


def kernel(**inputs):
    return kernel_fused(**inputs)


NT_B = 8
NCH = 32
CH = 64


def build_launch_B(debug=False, stage=3):
    nc = bass.Bass("TRN2", target_bir_lowering=False)
    xp = nc.dram_tensor("xp", [1024, SEQ], BF16, kind="ExternalInput").ap()
    Bp = nc.dram_tensor("Bp", [1024, SEQ], BF16, kind="ExternalInput").ap()
    Cp = nc.dram_tensor("Cp", [1024, SEQ], BF16, kind="ExternalInput").ap()
    cw = nc.dram_tensor("cw", [128, 24 * 3], F32, kind="ExternalInput").ap()
    cb = nc.dram_tensor("cb", [128, 24], F32, kind="ExternalInput").ap()
    alog = nc.dram_tensor("alog", [128, NT_B], F32, kind="ExternalInput").ap()
    ident = nc.dram_tensor("ident", [128, 128], F32, kind="ExternalInput").ap()
    mask = nc.dram_tensor("mask", [128, 64], F32, kind="ExternalInput").ap()
    y_fm = nc.dram_tensor("y_fm", [1024, SEQ], BF16, kind="ExternalOutput").ap()
    if debug:
        xs_dbg = nc.dram_tensor("xs_dbg", [1024, SEQ], BF16, kind="ExternalOutput").ap()
        bs_dbg = nc.dram_tensor("bs_dbg", [1024, SEQ], BF16, kind="ExternalOutput").ap()
        cs_dbg = nc.dram_tensor("cs_dbg", [128, NT_B * NCH], F32, kind="ExternalOutput").ap()
        prev_dbg = nc.dram_tensor("prev_dbg", [128, NT_B * (NCH + 1)], F32, kind="ExternalOutput").ap()
        xt_dbg = nc.dram_tensor("xt_dbg", [128, NT_B * SEQ], BF16, kind="ExternalOutput").ap()

    ctx = ExitStack()
    xs = ctx.enter_context(nc.sbuf_tensor("xs", [128, NT_B * SEQ], BF16))
    Bs = ctx.enter_context(nc.sbuf_tensor("Bs", [128, NT_B * SEQ], BF16))
    Cs = ctx.enter_context(nc.sbuf_tensor("Cs", [128, NT_B * SEQ], BF16))
    xT = ctx.enter_context(nc.sbuf_tensor("xT", [128, NT_B * SEQ], BF16))
    cws = ctx.enter_context(nc.sbuf_tensor("cws", [128, 24 * 3], F32))
    cbs = ctx.enter_context(nc.sbuf_tensor("cbs", [128, 24], F32))
    alogs = ctx.enter_context(nc.sbuf_tensor("alogs", [128, NT_B], F32))
    decay = ctx.enter_context(nc.sbuf_tensor("decay", [128, NT_B], F32))
    identb = ctx.enter_context(nc.sbuf_tensor("identb", [128, 128], BF16))
    maskb = ctx.enter_context(nc.sbuf_tensor("maskb", [128, 64], F32))
    acc_a = ctx.enter_context(nc.sbuf_tensor("acc_a", [128, SEQ], F32))
    acc_b = ctx.enter_context(nc.sbuf_tensor("acc_b", [128, SEQ], F32))
    prod = [ctx.enter_context(nc.sbuf_tensor(f"prod{j}", [128, SEQ], BF16)) for j in range(2)]
    cs_all = ctx.enter_context(nc.sbuf_tensor("cs_all", [128, NT_B * NCH], F32))
    prevb = ctx.enter_context(nc.sbuf_tensor("prevb", [128, NT_B * (NCH + 1)], F32))
    onesb = ctx.enter_context(nc.sbuf_tensor("onesb", [128, 1], BF16))
    onesrow = ctx.enter_context(nc.sbuf_tensor("onesrow", [1, 128], BF16))
    gt_sb = [ctx.enter_context(nc.sbuf_tensor(f"gt{j}", [128, 512], BF16)) for j in range(2)]
    yoff_sb = [ctx.enter_context(nc.sbuf_tensor(f"yoff{j}", [1, 512], BF16)) for j in range(4)]
    y_st = [ctx.enter_context(nc.sbuf_tensor(f"yst{j}", [128, 512], BF16)) for j in range(2)]
    pb = [ctx.enter_context(nc.psum_tensor(f"pb{j}", [128, 512], F32)) for j in range(8)]

    p = Prog(nc, ctx)

    for name, dst, src in (("xs", xs, xp), ("Bs", Bs, Bp), ("Cs", Cs, Cp)):
        p.dma("gpsimd", lambda g, dst=dst, src=src: g.dma_start(
            dst[:, :].rearrange("p (t s) -> p t s", s=SEQ),
            src.rearrange("(t p) s -> p t s", p=128)),
            writes=[f"{name}.pre.{t}" for t in range(NT_B)], sem=f"in_{name}")
    p.dma("gpsimd", lambda g: g.dma_start(cws[:, :], cw[:, :]), writes=["cws"], sem="cws")
    p.dma("gpsimd", lambda g: g.dma_start(cbs[:, :], cb[:, :]), writes=["cbs"], sem="cbs")
    p.dma("gpsimd", lambda g: g.dma_start(alogs[:, :], alog[:, :]), writes=["alogs"], sem="alogs")
    p.dma("gpsimd", lambda g: g.dma_start(identb[:, :], ident[:, :]), writes=["identb"], sem="identb")
    p.dma("gpsimd", lambda g: g.dma_start(maskb[:, :], mask[:, :]), writes=["maskb"], sem="maskb")
    p.op("vector", lambda v: v.memset(onesb[:, :], 1.0), writes=["onesb"])
    p.op("vector", lambda v: v.memset(onesrow[:, :], 1.0), writes=["onesrow"])
    p.op("vector", lambda v: v.memset(pb[0][:, :], 0.0), writes=["pb.0"])
    p.op("vector", lambda v: v.memset(pb[1][:, :], 0.0), writes=["pb.1"])
    p.op("vector", lambda v: v.memset(prevb[:, :], 0.0), writes=["prevb.z"])
    p.op("scalar", lambda s: s.activation(decay[:, :], alogs[:, :], AF.Exp, scale=63.0),
         reads=["alogs"], writes=["decay"])

    def conv_tile(part, buf, key, t, gidx):
        src = buf[:, t * SEQ:(t + 1) * SEQ]
        w0 = cws[:, gidx * 3 + 0: gidx * 3 + 1]
        w1 = cws[:, gidx * 3 + 1: gidx * 3 + 2]
        w2 = cws[:, gidx * 3 + 2: gidx * 3 + 3]
        bcol = cbs[:, gidx:gidx + 1]
        kpre = f"{key}.pre.{t}"
        p.op("vector", lambda v: v.memset(acc_a[:, 0:1], 0.0), writes=["acc_a"])
        p.op("vector", lambda v: v.tensor_scalar_mul(acc_a[:, 1:SEQ], src[:, 0:SEQ - 1], w0),
             reads=[kpre, "cws"], writes=["acc_a"])
        p.op("vector", lambda v: v.scalar_tensor_tensor(
                acc_b[:, :], src, w1, acc_a[:, :], ALU.mult, ALU.add),
             reads=[kpre, "cws", "acc_a"], writes=["acc_b"])
        p.op("vector", lambda v: v.tensor_copy(acc_a[:, SEQ - 1:SEQ], acc_b[:, SEQ - 1:SEQ]),
             reads=["acc_b"], writes=["acc_a"])
        p.op("vector", lambda v: v.scalar_tensor_tensor(
                acc_a[:, 0:SEQ - 1], src[:, 1:SEQ], w2, acc_b[:, 0:SEQ - 1],
                ALU.mult, ALU.add),
             reads=[kpre, "cws", "acc_b"], writes=["acc_a"])
        p.op("scalar", lambda s: s.activation(src, acc_a[:, :], AF.Silu, bias=bcol),
             reads=["acc_a", "cbs", kpre], writes=[f"{key}.{t}"])

    for t in range(NT_B):
        conv_tile("x", xs, "xs", t, t)
    for t in range(NT_B):
        conv_tile("B", Bs, "Bs", t, 8 + t)
    for t in range(NT_B):
        conv_tile("C", Cs, "Cs", t, 16 + t)

    for j in range(NT_B):
        pj = prod[j % 2]
        p.op("vector", lambda v, j=j, pj=pj: v.tensor_tensor(
                pj[:, :], xs[:, j * SEQ:(j + 1) * SEQ], Bs[:, j * SEQ:(j + 1) * SEQ], ALU.mult),
             reads=[f"xs.{j}", f"Bs.{j}"], writes=[f"prodbuf.{j % 2}"])
        p.op("vector", lambda v, j=j, pj=pj: v.reduce_sum(
                cs_all[:, j * NCH:(j + 1) * NCH].unsqueeze(2),
                pj[:, :].rearrange("p (c s) -> p c s", s=CH), axis=AX.X),
             reads=[f"prodbuf.{j % 2}"], writes=[f"cs.{j}"])
        p.op("vector", lambda v, j=j: v.tensor_tensor_scan(
                prevb[:, j * (NCH + 1) + 1: j * (NCH + 1) + 1 + NCH],
                decay[:, j:j + 1].to_broadcast([128, NCH]),
                cs_all[:, j * NCH:(j + 1) * NCH],
                0.0, ALU.mult, ALU.add),
             reads=[f"cs.{j}", "decay", "prevb.z"], writes=[f"prev.{j}"])

    grp = 8
    for j in range(NT_B):
        for sb0 in (0, 8):
            bank = pb[6 + (sb0 // 8) % 2].bitcast(BF16)
            bkey = f"pb.{6 + (sb0 // 8) % 2}"
            def f_tr(t, j=j, sb0=sb0, bank=bank):
                insts = []
                for i in range(grp):
                    sb = sb0 + i
                    insts.append(t.transpose(
                        bank[:, i * 128:(i + 1) * 128],
                        xs[:, j * SEQ + sb * 128: j * SEQ + (sb + 1) * 128],
                        identb[:, :]))
                return insts
            p.op("tensor", f_tr, reads=[f"xs.{j}", "identb"], writes=[bkey])
            def f_ev(v, j=j, sb0=sb0, bank=bank):
                dst = xT[:, :].rearrange("p (b ch) -> p b ch", ch=128)[:, j * 16 + sb0: j * 16 + sb0 + grp, :]
                return v.tensor_copy(dst, bank[:, :].rearrange("p (b ch) -> p b ch", ch=128))
            p.op("vector", f_ev, reads=[bkey], writes=[f"xT.{j}.{sb0}"])

    for j in range(NT_B if stage >= 3 else 0):
        pj = prod[j % 2]
        prev_j = prevb[:, j * (NCH + 1): j * (NCH + 1) + NCH]
        p.op("vector", lambda v, j=j, pj=pj, prev_j=prev_j: v.tensor_tensor(
                pj[:, :].rearrange("p (c s) -> p c s", s=CH),
                prev_j.unsqueeze(2).to_broadcast([128, NCH, CH]),
                Cs[:, j * SEQ:(j + 1) * SEQ].rearrange("p (c s) -> p c s", s=CH),
                ALU.mult),
             reads=[f"prev.{j}", f"Cs.{j}", "prevb.z"], writes=[f"prodbuf.{j % 2}"])
        for g in range(4):
            gpar = (j * 4 + g) % 2
            yb = pb[2 + gpar]
            ybk = f"pb.{2 + gpar}"
            ysb = y_st[gpar]
            for e in range(2):
                ob = pb[4 + e]
                obk = f"pb.{4 + e}"
                yo = yoff_sb[gpar * 2 + e]
                yok = f"yoff.{gpar * 2 + e}"
                p.op("tensor", lambda t, j=j, g=g, e=e, ob=ob: t.matmul(
                        ob[0:1, :], onesb[e * 64:e * 64 + 64, :],
                        prod[j % 2][e * 64:e * 64 + 64, g * 512:(g + 1) * 512],
                        start=True, stop=True),
                     reads=[f"prodbuf.{j % 2}", "onesb"], writes=[obk])
                p.op("vector", lambda v, yo=yo, ob=ob: v.tensor_copy(yo[:, :], ob[0:1, :]),
                     reads=[obk], writes=[yok])
                gb = pb[e]
                gbk = f"pb.{e}"
                def f_g(t, j=j, g=g, e=e, gb=gb):
                    for ci in range(8):
                        c = g * 8 + ci
                        rh = (c % 2) * 64
                        inst = t.matmul(
                            gb[rh:rh + 64, ci * 64:(ci + 1) * 64],
                            Bs[e * 64:e * 64 + 64, j * SEQ + c * CH: j * SEQ + (c + 1) * CH],
                            Cs[e * 64:e * 64 + 64, j * SEQ + c * CH: j * SEQ + (c + 1) * CH],
                            start=True, stop=True)
                    return inst
                p.op("tensor", f_g, reads=[f"Bs.{j}", f"Cs.{j}"], writes=[gbk])
                p.op("vector", lambda v, e=e, gb=gb: v.tensor_tensor(
                        gt_sb[e][:, :].rearrange("p (ci l) -> p ci l", l=64),
                        gb[:, :].rearrange("p (ci l) -> p ci l", l=64),
                        maskb[:, :].unsqueeze(1).to_broadcast([128, 8, 64]),
                        ALU.mult),
                     reads=[gbk, "maskb"], writes=[f"gts.{e}"])
            def f_y(t, j=j, g=g, gpar=gpar, yb=yb):
                for ci in range(8):
                    c = g * 8 + ci
                    sb = c // 2
                    mrow = (c % 2) * 64
                    for e in range(2):
                        t.matmul(
                            yb[e * 64:e * 64 + 64, ci * 64:(ci + 1) * 64],
                            xT[mrow:mrow + 64, (j * 16 + sb) * 128 + e * 64: (j * 16 + sb) * 128 + e * 64 + 64],
                            gt_sb[e][mrow:mrow + 64, ci * 64:(ci + 1) * 64],
                            start=True, stop=False)
                        inst = t.matmul(
                            yb[e * 64:e * 64 + 64, ci * 64:(ci + 1) * 64],
                            onesrow[0:1, 0:64],
                            yoff_sb[gpar * 2 + e][0:1, ci * 64:(ci + 1) * 64],
                            start=False, stop=True)
                return inst
            p.op("tensor", f_y,
                 reads=["gts.0", "gts.1", f"xT.{j}.0", f"xT.{j}.8", "onesrow",
                        f"yoff.{gpar * 2}", f"yoff.{gpar * 2 + 1}"],
                 writes=[ybk])
            p.op("vector", lambda v, yb=yb, ysb=ysb: v.tensor_copy(ysb[:, :], yb[:, :]),
                 reads=[ybk], writes=[f"yst.{gpar}"])
            p.dma("sync", lambda s, j=j, g=g, ysb=ysb: s.dma_start(
                    y_fm[j * 128:(j + 1) * 128, g * 512:(g + 1) * 512], ysb[:, :]),
                reads=[f"yst.{gpar}"], writes=[f"yout.{j}.{g}"], sem=f"yout{gpar}")

    if debug:
        p.dma("sync", lambda s: s.dma_start(
            xs_dbg.rearrange("(t p) s -> p t s", p=128),
            xs[:, :].rearrange("p (t s) -> p t s", s=SEQ)),
            reads=[f"xs.{t}" for t in range(NT_B)], writes=["dbg1"], sem="dbg")
        p.dma("sync", lambda s: s.dma_start(
            bs_dbg.rearrange("(t p) s -> p t s", p=128),
            Bs[:, :].rearrange("p (t s) -> p t s", s=SEQ)),
            reads=[f"Bs.{t}" for t in range(NT_B)], writes=["dbg1b"], sem="dbg")
        p.dma("sync", lambda s: s.dma_start(cs_dbg, cs_all[:, :]),
            reads=[f"cs.{t}" for t in range(NT_B)], writes=["dbg2"], sem="dbg")
        p.dma("sync", lambda s: s.dma_start(prev_dbg, prevb[:, :]),
            reads=[f"prev.{t}" for t in range(NT_B)], writes=["dbg3"], sem="dbg")
        p.dma("sync", lambda s: s.dma_start(xt_dbg, xT[:, :]),
            reads=[f"xT.{t}.{s0}" for t in range(NT_B) for s0 in (0, 8)], writes=["dbg4"], sem="dbg")

    p.barrier_all("sync")
    p.finalize()
    ctx.close()
    return nc


def build_launch_C():
    nc = bass.Bass("TRN2", target_bir_lowering=False)
    y_sl = nc.dram_tensor("y_sl", [D, T], BF16, kind="ExternalInput").ap()
    z_fm = nc.dram_tensor("z_fm", [D, T], BF16, kind="ExternalInput").ap()
    xh = nc.dram_tensor("xh", [T, D], F32, kind="ExternalInput").ap()
    g0 = nc.dram_tensor("g0", [T, D], BF16, kind="ExternalInput").ap()
    w_out = nc.dram_tensor("w_out", [4, DC, 128, 512], F32, kind="ExternalInput").ap()
    rmsw = nc.dram_tensor("rmsw", [128, DC], F32, kind="ExternalInput").ap()
    ident = nc.dram_tensor("ident", [128, 128], F32, kind="ExternalInput").ap()
    x1 = nc.dram_tensor("x1", [T, D], F32, kind="ExternalOutput").ap()
    xn1b_fm = nc.dram_tensor("xn1b_fm", [D, T], BF16, kind="ExternalOutput").ap()

    ctx = ExitStack()
    ybuf = ctx.enter_context(nc.sbuf_tensor("ybuf", [128, DC * T], BF16))
    zbuf = ctx.enter_context(nc.sbuf_tensor("zbuf", [128, DC * T], BF16))
    xt = ctx.enter_context(nc.sbuf_tensor("xt", [128, TTI * D], F32))
    g0sl = [ctx.enter_context(nc.sbuf_tensor(f"g0sl{j}", [128, TTI * 512], BF16)) for j in range(2)]
    wsl = [ctx.enter_context(nc.sbuf_tensor(f"wsl{j}", [128, 8 * 512], BF16)) for j in range(2)]
    zs = ctx.enter_context(nc.sbuf_tensor("zs", [128, T], F32))
    sqb = [ctx.enter_context(nc.sbuf_tensor(f"sqb{j}", [128, T], BF16)) for j in range(2)]
    ones128b = ctx.enter_context(nc.sbuf_tensor("ones128b", [128, 1], BF16))
    onesrow = ctx.enter_context(nc.sbuf_tensor("onesrow", [1, 128], F32))
    rrow = ctx.enter_context(nc.sbuf_tensor("ROWA", [1, T], F32))
    rb = ctx.enter_context(nc.sbuf_tensor("rb", [128, T], F32))
    rmsws = ctx.enter_context(nc.sbuf_tensor("rmsws", [128, DC], F32))
    identb = ctx.enter_context(nc.sbuf_tensor("identb", [128, 128], BF16))
    consts = ctx.enter_context(nc.sbuf_tensor("consts", [128, 4], F32))
    tmp = ctx.enter_context(nc.sbuf_tensor("tmp", [128, 512], F32))
    xnb_s = ctx.enter_context(nc.sbuf_tensor("xnb_s", [128, D], BF16))
    xn1st = ctx.enter_context(nc.sbuf_tensor("xn1st", [128, D], BF16))
    sq = ctx.enter_context(nc.sbuf_tensor("sq", [128, D], F32))
    stats = {k: ctx.enter_context(nc.sbuf_tensor(f"st_{k}", [128, TTI], F32))
             for k in ("m0", "s0", "m", "var", "rstd")}
    pb = [ctx.enter_context(nc.psum_tensor(f"pb{j}", [128, 512], F32)) for j in range(8)]

    p = Prog(nc, ctx)
    p.op("vector", lambda v: v.memset(consts[:, 0:1], 1.1920929e-07), writes=["consts"])
    p.op("vector", lambda v: v.memset(consts[:, 1:2], 1e-5), writes=["consts"])
    p.op("vector", lambda v: v.memset(ones128b[:, :], 1.0), writes=["ones128b"])
    p.op("vector", lambda v: v.memset(onesrow[:, :], 1.0), writes=["onesrow"])
    p.dma("gpsimd", lambda g: g.dma_start(
        ybuf[:, :].rearrange("p (kc t) -> p kc t", t=T),
        y_sl.rearrange("(kc p) t -> p kc t", p=128)),
        writes=[f"y.{kc}" for kc in range(DC)], sem="y")
    p.dma("gpsimd", lambda g: g.dma_start(
        zbuf[:, :].rearrange("p (kc t) -> p kc t", t=T),
        z_fm.rearrange("(kc p) t -> p kc t", p=128)),
        writes=[f"z.{kc}" for kc in range(DC)], sem="z")
    p.dma("gpsimd", lambda g: g.dma_start(
        xt[:, :].rearrange("p (tt d) -> p tt d", d=D),
        xh.rearrange("(tt p) d -> p tt d", p=128)),
        writes=[f"xt.{tt}" for tt in range(TTI)], sem="xt")
    p.dma("gpsimd", lambda g: g.dma_start(rmsws[:, :], rmsw[:, :]), writes=["rmsws"], sem="rmsws")
    p.dma("gpsimd", lambda g: g.dma_start(identb[:, :], ident[:, :]), writes=["identb"], sem="identb")

    for kc in range(DC):
        par = kc % 2
        p.op("scalar", lambda s, kc=kc: s.activation(
                zs[:, :], zbuf[:, kc * T:(kc + 1) * T], AF.Silu),
             reads=[f"z.{kc}"], writes=["zs"])
        p.op("vector", lambda v, kc=kc: v.tensor_mul(
                zbuf[:, kc * T:(kc + 1) * T], ybuf[:, kc * T:(kc + 1) * T], zs[:, :]),
             reads=[f"y.{kc}", "zs"], writes=[f"z.{kc}"])
        p.op("scalar", lambda s, kc=kc, par=par: s.activation(
                sqb[par][:, :], zbuf[:, kc * T:(kc + 1) * T], AF.Square),
             reads=[f"z.{kc}"], writes=[f"sqb.{par}"])
        def f_st(t, kc=kc, par=par):
            insts = []
            for nh in range(2):
                insts.append(t.matmul(
                    pb[nh][0:1, :], ones128b[:, :], sqb[par][:, nh * 512:(nh + 1) * 512],
                    start=(kc == 0), stop=(kc == DC - 1)))
            return insts[-1]
        p.op("tensor", f_st, reads=[f"sqb.{par}", "ones128b"], writes=["pb.0", "pb.1"])
    for nh in range(2):
        p.op("vector", lambda v, nh=nh: v.tensor_copy(rrow[:, nh * 512:(nh + 1) * 512], pb[nh][0:1, :]),
             reads=[f"pb.{nh}"], writes=["ROWA"])
    p.op("vector", lambda v: v.tensor_scalar_mul(rrow[:, :], rrow[:, :], 1.0 / D),
         reads=["ROWA"], writes=["ROWA"])
    p.op("scalar", lambda s: s.activation(rrow[:, :], rrow[:, :], AF.Sqrt, bias=consts[0:1, 0:1]),
         reads=["ROWA", "consts"], writes=["ROWA"])
    p.op("vector", lambda v: v.reciprocal(rrow[:, :], rrow[:, :]),
         reads=["ROWA"], writes=["ROWA"])
    def f_bc(t):
        for nh in range(2):
            inst = t.matmul(pb[2 + nh][:, :], onesrow[:, :], rrow[:, nh * 512:(nh + 1) * 512],
                            start=True, stop=True)
        return inst
    p.op("tensor", f_bc, reads=["ROWA", "onesrow"], writes=["pb.2", "pb.3"])
    for nh in range(2):
        p.op("vector", lambda v, nh=nh: v.tensor_copy(rb[:, nh * 512:(nh + 1) * 512], pb[2 + nh][:, :]),
             reads=[f"pb.{2 + nh}"], writes=["rb"])
    for kc in range(DC):
        p.op("vector", lambda v, kc=kc: v.scalar_tensor_tensor(
                ybuf[:, kc * T:(kc + 1) * T], zbuf[:, kc * T:(kc + 1) * T],
                rmsws[:, kc:kc + 1], rb[:, :], ALU.mult, ALU.mult),
             reads=[f"z.{kc}", "rmsws", "rb"], writes=[f"yn.{kc}"])

    for oc in range(4):
        gpar = oc % 2
        p.dma("gpsimd", lambda g, oc=oc, gpar=gpar: g.dma_start(
            g0sl[gpar][:, :].rearrange("p (tt o) -> p tt o", o=512),
            g0.rearrange("(tt p) d -> p tt d", p=128)[:, :, oc * 512:(oc + 1) * 512]),
            writes=[f"g0sl.{gpar}"], sem=f"g0sl{gpar}")
        for kcg in range(2):
            par = kcg % 2
            p.dma("gpsimd", lambda g, oc=oc, kcg=kcg, par=par: g.dma_start(
                wsl[par][:, :].rearrange("p (kc o) -> p kc o", o=512),
                w_out[oc, kcg * 8:(kcg + 1) * 8].rearrange("kc k o -> k kc o")),
                writes=[f"wsl.{par}"], sem=f"wsl{par}")
            def f_mm(t, oc=oc, kcg=kcg, par=par):
                for kci in range(8):
                    kc = kcg * 8 + kci
                    for tt in range(TTI):
                        inst = t.matmul(
                            pb[tt][:, :],
                            ybuf[:, kc * T + tt * 128: kc * T + (tt + 1) * 128],
                            wsl[par][:, kci * 512:(kci + 1) * 512],
                            start=(kc == 0), stop=(kc == DC - 1))
                return inst
            p.op("tensor", f_mm,
                 reads=[f"wsl.{par}"] + [f"yn.{kcg * 8 + j}" for j in range(8)],
                 writes=[f"pb.{tt}" for tt in range(TTI)])
        for tt in range(TTI):
            p.op("vector", lambda v, oc=oc, tt=tt: v.tensor_mul(
                    tmp[:, :], pb[tt][:, :],
                    g0sl[oc % 2][:, tt * 512:(tt + 1) * 512]),
                 reads=[f"pb.{tt}", f"g0sl.{oc % 2}"], writes=["tmp"])
            p.op("vector", lambda v, oc=oc, tt=tt: v.tensor_add(
                    xt[:, tt * D + oc * 512: tt * D + (oc + 1) * 512],
                    xt[:, tt * D + oc * 512: tt * D + (oc + 1) * 512], tmp[:, :]),
                 reads=["tmp", f"xt.{tt}"], writes=[f"xt.{tt}"])

    for tt in range(TTI):
        p.dma("sync", lambda s, tt=tt: s.dma_start(
            x1[tt * 128:(tt + 1) * 128, :], xt[:, tt * D:(tt + 1) * D]),
            reads=[f"xt.{tt}"], writes=[f"x1out.{tt}"], sem="x1out")
        emit_ln_tm(p, xt[:, tt * D:(tt + 1) * D], [f"xt.{tt}"],
                   xnb_s[:, :], "xnb_s", stats, tt, sq,
                   eps_col=consts[:, 1:2], pref="ln1")
        for half in range(2):
            bank = pb[half].bitcast(BF16)
            bkey = f"pb.{half}"
            def f_tr(t, tt=tt, half=half, bank=bank):
                insts = []
                for j in range(8):
                    kc = half * 8 + j
                    insts.append(t.transpose(
                        bank[:, j * 128:(j + 1) * 128],
                        xnb_s[:, kc * 128:(kc + 1) * 128], identb[:, :]))
                return insts
            p.op("tensor", f_tr, reads=["xnb_s", "identb"], writes=[bkey])
            p.op("vector", lambda v, half=half, bank=bank: v.tensor_copy(
                    xn1st[:, half * 1024:(half + 1) * 1024], bank[:, 0:1024]),
                 reads=[bkey], writes=[f"xn1st.{half}"])
        p.dma("sync", lambda s, tt=tt: s.dma_start(
            xn1b_fm.rearrange("(kc p) t -> p kc t", p=128)[:, :, tt * 128:(tt + 1) * 128],
            xn1st[:, :].rearrange("p (kc t) -> p kc t", t=128)),
            reads=["xn1st.0", "xn1st.1"], writes=[f"xn1out.{tt}"], sem="xn1out")

    p.barrier_all("sync")
    p.finalize()
    ctx.close()
    return nc


def tile_w_in_out_wide(w, ow=512):
    K, O = w.shape
    return np.ascontiguousarray(
        w.reshape(K // 128, 128, O // ow, ow).transpose(2, 0, 1, 3))


def build_launch_C2():
    nc = bass.Bass("TRN2", target_bir_lowering=False)
    xn1b_fm = nc.dram_tensor("xn1b_fm", [D, T], BF16, kind="ExternalInput").ap()
    w1 = nc.dram_tensor("w1", [4, DC, 128, 512], F32, kind="ExternalInput").ap()
    lnwb = nc.dram_tensor("lnwb", [128, 2 * DC], F32, kind="ExternalInput").ap()
    gamrow = nc.dram_tensor("gamrow", [1, 2 * D], F32, kind="ExternalInput").ap()
    g1 = nc.dram_tensor("g1", [T, D], BF16, kind="ExternalOutput").ap()

    ctx = ExitStack()
    xfm = ctx.enter_context(nc.sbuf_tensor("xfm", [128, DC * T], BF16))
    lnx = ctx.enter_context(nc.sbuf_tensor("lnx", [128, DC * T], BF16))
    wsl = [ctx.enter_context(nc.sbuf_tensor(f"wsl{j}", [128, 8 * 512], BF16)) for j in range(2)]
    ssl = [ctx.enter_context(nc.sbuf_tensor(f"ssl{j}", [128, 8 * 512], BF16)) for j in range(2)]
    lnwbs = ctx.enter_context(nc.sbuf_tensor("lnwbs", [128, 2 * DC], F32))
    grows = ctx.enter_context(nc.sbuf_tensor("grows", [1, 2 * D], F32))
    grep = ctx.enter_context(nc.sbuf_tensor("grep", [128, D], F32))
    brep = ctx.enter_context(nc.sbuf_tensor("brep", [128, D], F32))
    sq = ctx.enter_context(nc.sbuf_tensor("sq", [128, T], BF16))
    ones128b = ctx.enter_context(nc.sbuf_tensor("ones128b", [128, 1], BF16))
    onesrow = ctx.enter_context(nc.sbuf_tensor("onesrow", [1, 128], F32))
    srow = ctx.enter_context(nc.sbuf_tensor("srow", [1, 2 * T], F32))
    r2row = ctx.enter_context(nc.sbuf_tensor("ROWA", [1, T], F32))
    mrrow = ctx.enter_context(nc.sbuf_tensor("ROWA", [1, T], F32))
    r2b = ctx.enter_context(nc.sbuf_tensor("r2b", [128, T], F32))
    mrb = ctx.enter_context(nc.sbuf_tensor("mrb", [128, T], F32))
    amaxc = ctx.enter_context(nc.sbuf_tensor("amaxc", [128, 1], F32))
    amrow = ctx.enter_context(nc.sbuf_tensor("ROWA", [1, 128], F32))
    scal1 = ctx.enter_context(nc.sbuf_tensor("ROWA", [1, 1], F32))
    scalc = ctx.enter_context(nc.sbuf_tensor("scalc", [128, 1], F32))
    identf = ctx.enter_context(nc.sbuf_tensor("identf", [128, 128], F32))
    ident = nc.dram_tensor("ident", [128, 128], F32, kind="ExternalInput").ap()
    consts = ctx.enter_context(nc.sbuf_tensor("consts", [128, 4], F32))
    tmp = ctx.enter_context(nc.sbuf_tensor("tmp", [128, T], F32))
    tmp2 = ctx.enter_context(nc.sbuf_tensor("tmp2", [128, 512], F32))
    stage = ctx.enter_context(nc.sbuf_tensor("stage", [128, 512], BF16))
    pb = [ctx.enter_context(nc.psum_tensor(f"pb{j}", [128, 512], F32)) for j in range(8)]

    p = Prog(nc, ctx)
    p.op("vector", lambda v: v.memset(consts[:, 0:1], 1e-5), writes=["consts"])
    p.op("vector", lambda v: v.memset(ones128b[:, :], 1.0), writes=["ones128b"])
    p.op("vector", lambda v: v.memset(onesrow[:, :], 1.0), writes=["onesrow"])
    p.op("vector", lambda v: v.memset(amaxc[:, :], 0.0), writes=["amaxc"])
    p.dma("gpsimd", lambda g: g.dma_start(
        xfm[:, :].rearrange("p (kc t) -> p kc t", t=T),
        xn1b_fm.rearrange("(kc p) t -> p kc t", p=128)),
        writes=[f"x.{kc}" for kc in range(DC)], sem="x")
    p.dma("gpsimd", lambda g: g.dma_start(lnwbs[:, :], lnwb[:, :]), writes=["lnwbs"], sem="lnwbs")
    p.dma("gpsimd", lambda g: g.dma_start(grows[:, :], gamrow[:, :]), writes=["grows"], sem="grows")
    p.dma("gpsimd", lambda g: g.dma_start(identf[:, :], ident[:, :]), writes=["identf"], sem="identf")

    def f_grep(t):
        for i in range(4):
            t.matmul(pb[i % 4][:, :], onesrow[:, :], grows[0:1, i * 512:(i + 1) * 512],
                     start=True, stop=True)
            inst = t.matmul(pb[4 + i % 4][:, :], onesrow[:, :], grows[0:1, D + i * 512: D + (i + 1) * 512],
                            start=True, stop=True)
        return inst
    p.op("tensor", f_grep, reads=["grows", "onesrow"], writes=[f"pb.{j}" for j in range(8)])
    for i in range(4):
        p.op("vector", lambda v, i=i: v.tensor_copy(grep[:, i * 512:(i + 1) * 512], pb[i][:, :]),
             reads=[f"pb.{i}"], writes=["grep"])
        p.op("vector", lambda v, i=i: v.tensor_copy(brep[:, i * 512:(i + 1) * 512], pb[4 + i][:, :]),
             reads=[f"pb.{4 + i}"], writes=["brep"])

    for kc in range(DC):
        p.op("scalar", lambda s, kc=kc: s.activation(sq[:, :], xfm[:, kc * T:(kc + 1) * T], AF.Square),
             reads=[f"x.{kc}"], writes=["sq"])
        def f_st(t, kc=kc):
            for nh in range(2):
                t.matmul(pb[nh][0:1, :], ones128b[:, :],
                         xfm[:, kc * T + nh * 512: kc * T + (nh + 1) * 512],
                         start=(kc == 0), stop=(kc == DC - 1))
                inst = t.matmul(pb[2 + nh][0:1, :], ones128b[:, :], sq[:, nh * 512:(nh + 1) * 512],
                                start=(kc == 0), stop=(kc == DC - 1))
            return inst
        p.op("tensor", f_st, reads=[f"x.{kc}", "sq", "ones128b"],
             writes=["pb.0", "pb.1", "pb.2", "pb.3"])
    for nh in range(2):
        p.op("vector", lambda v, nh=nh: v.tensor_copy(srow[:, nh * 512:(nh + 1) * 512], pb[nh][0:1, :]),
             reads=[f"pb.{nh}"], writes=["srow"])
        p.op("vector", lambda v, nh=nh: v.tensor_copy(srow[:, T + nh * 512: T + (nh + 1) * 512], pb[2 + nh][0:1, :]),
             reads=[f"pb.{2 + nh}"], writes=["srow"])
    p.op("vector", lambda v: v.tensor_scalar_mul(srow[:, :], srow[:, :], 1.0 / D),
         reads=["srow"], writes=["srow"])
    p.op("vector", lambda v: v.tensor_mul(r2row[:, :], srow[:, 0:T], srow[:, 0:T]),
         reads=["srow"], writes=["ROWA"])
    p.op("vector", lambda v: v.tensor_sub(r2row[:, :], r2row[:, :], srow[:, T:2 * T]),
         reads=["srow", "ROWA"], writes=["ROWA"])
    p.op("scalar", lambda s: s.activation(r2row[:, :], r2row[:, :], AF.Sqrt,
                                          bias=consts[0:1, 0:1], scale=-1.0),
         reads=["ROWA", "consts"], writes=["ROWA"])
    p.op("vector", lambda v: v.reciprocal(r2row[:, :], r2row[:, :]),
         reads=["ROWA"], writes=["ROWA"])
    p.op("vector", lambda v: v.tensor_mul(mrrow[:, :], srow[:, 0:T], r2row[:, :]),
         reads=["srow", "ROWA"], writes=["ROWA"])
    def f_bc(t):
        for nh in range(2):
            t.matmul(pb[4 + nh][:, :], onesrow[:, :], r2row[0:1, nh * 512:(nh + 1) * 512],
                     start=True, stop=True)
            inst = t.matmul(pb[6 + nh][:, :], onesrow[:, :], mrrow[0:1, nh * 512:(nh + 1) * 512],
                            start=True, stop=True)
        return inst
    p.op("tensor", f_bc, reads=["ROWA", "ROWA", "onesrow"],
         writes=["pb.4", "pb.5", "pb.6", "pb.7"])
    for nh in range(2):
        p.op("vector", lambda v, nh=nh: v.tensor_copy(r2b[:, nh * 512:(nh + 1) * 512], pb[4 + nh][:, :]),
             reads=[f"pb.{4 + nh}"], writes=["r2b"])
        p.op("vector", lambda v, nh=nh: v.tensor_copy(mrb[:, nh * 512:(nh + 1) * 512], pb[6 + nh][:, :]),
             reads=[f"pb.{6 + nh}"], writes=["mrb"])

    for kc in range(DC):
        p.op("vector", lambda v, kc=kc: v.tensor_mul(tmp[:, :], xfm[:, kc * T:(kc + 1) * T], r2b[:, :]),
             reads=[f"x.{kc}", "r2b"], writes=["tmpA"])
        p.op("vector", lambda v, kc=kc: v.tensor_sub(tmp[:, :], tmp[:, :], mrb[:, :]),
             reads=["tmpA", "mrb"], writes=["tmpA"])
        p.op("vector", lambda v, kc=kc: v.tensor_scalar(
                lnx[:, kc * T:(kc + 1) * T], tmp[:, :],
                lnwbs[:, kc:kc + 1], lnwbs[:, DC + kc: DC + kc + 1], ALU.mult, ALU.add),
             reads=["tmpA", "lnwbs"], writes=[f"lnx.{kc}"])

    raw = ctx.enter_context(nc.sbuf_tensor("raw", [128, TTI * D], BF16))
    amx_t = ctx.enter_context(nc.sbuf_tensor("amx_t", [128, 1], F32))
    for oc in range(4):
        for kcg in range(2):
            par = kcg % 2
            p.dma("gpsimd", lambda g, oc=oc, kcg=kcg, par=par: g.dma_start(
                wsl[par][:, :].rearrange("p (kc o) -> p kc o", o=512),
                w1[oc, kcg * 8:(kcg + 1) * 8].rearrange("kc k o -> k kc o")),
                writes=[f"wsl.{par}"], sem=f"wsl{par}")
            p.op("vector", lambda v, par=par: v.reduce_max(
                    amx_t[:, :], wsl[par][:, :], axis=AX.X, apply_absolute_value=True),
                 reads=[f"wsl.{par}"], writes=["amx_t"])
            p.op("vector", lambda v: v.tensor_max(amaxc[:, :], amaxc[:, :], amx_t[:, :]),
                 reads=["amx_t", "amaxc"], writes=["amaxc"])
            p.op("scalar", lambda s, par=par: s.activation(ssl[par][:, :], wsl[par][:, :], AF.Sign),
                 reads=[f"wsl.{par}"], writes=[f"ssl.{par}"])
            def f_mm(t, oc=oc, kcg=kcg, par=par):
                for kci in range(8):
                    kc = kcg * 8 + kci
                    for tt in range(TTI):
                        inst = t.matmul(
                            pb[tt][:, :],
                            lnx[:, kc * T + tt * 128: kc * T + (tt + 1) * 128],
                            ssl[par][:, kci * 512:(kci + 1) * 512],
                            start=(kc == 0), stop=(kc == DC - 1))
                return inst
            p.op("tensor", f_mm,
                 reads=[f"ssl.{par}"] + [f"lnx.{kcg * 8 + j}" for j in range(8)],
                 writes=[f"pb.{tt}" for tt in range(TTI)])
        for tt in range(TTI):
            p.op("vector", lambda v, oc=oc, tt=tt: v.tensor_copy(
                    raw[:, tt * D + oc * 512: tt * D + (oc + 1) * 512], pb[tt][:, :]),
                 reads=[f"pb.{tt}"], writes=[f"raw.{oc}.{tt}"])

    p.op("tensor", lambda t: t.transpose(pb[0][0:1, 0:128], amaxc[:, 0:1], identf[:, :]),
         reads=["amaxc", "identf"], writes=["pb.0"])
    p.op("vector", lambda v: v.tensor_copy(amrow[:, :], pb[0][0:1, 0:128]),
         reads=["pb.0"], writes=["ROWA"])
    p.op("vector", lambda v: v.reduce_max(scal1[:, :], amrow[:, :], axis=AX.X),
         reads=["ROWA"], writes=["ROWA"])
    p.op("tensor", lambda t: t.matmul(pb[1][:, 0:1], onesrow[:, :], scal1[:, :], start=True, stop=True),
         reads=["ROWA", "onesrow"], writes=["pb.1"])
    p.op("vector", lambda v: v.tensor_copy(scalc[:, :], pb[1][:, 0:1]),
         reads=["pb.1"], writes=["scalc"])
    p.op("vector", lambda v: v.tensor_scalar_mul(grep[:, :], grep[:, :], scalc[:, 0:1]),
         reads=["grep", "scalc"], writes=["grep"])

    for tt in range(TTI):
        for oc in range(4):
            p.op("vector", lambda v, tt=tt, oc=oc: v.tensor_mul(
                    tmp2[:, :], raw[:, tt * D + oc * 512: tt * D + (oc + 1) * 512],
                    grep[:, oc * 512:(oc + 1) * 512]),
                 reads=[f"raw.{oc}.{tt}", "grep"], writes=["tmp2"])
            p.op("vector", lambda v, oc=oc: v.tensor_add(
                    tmp2[:, :], tmp2[:, :], brep[:, oc * 512:(oc + 1) * 512]),
                 reads=["tmp2", "brep"], writes=["tmp2"])
            p.op("scalar", lambda s: s.activation(stage[:, :], tmp2[:, :], AF.Silu),
                 reads=["tmp2"], writes=["stage"])
            p.dma("sync", lambda s, tt=tt, oc=oc: s.dma_start(
                g1[tt * 128:(tt + 1) * 128, oc * 512:(oc + 1) * 512], stage[:, :]),
                reads=["stage"], writes=[f"g1out.{tt}.{oc}"], sem="g1out")

    p.barrier_all("sync")
    p.finalize()
    ctx.close()
    return nc


def build_launch_D():
    nc = bass.Bass("TRN2", target_bir_lowering=False)
    xn1_fm = nc.dram_tensor("xn1_fm", [D, SEQ], BF16, kind="ExternalInput").ap()
    wqkv = nc.dram_tensor("wqkv", [8, 3, DC, 128, 128], F32, kind="ExternalInput").ap()
    lns_i = nc.dram_tensor("lns", [128, 2], F32, kind="ExternalInput").ap()
    cs2 = nc.dram_tensor("cs2", [128, 2 * SEQ], F32, kind="ExternalInput").ap()
    mask4 = nc.dram_tensor("mask4", [128, 4 * 512], F32, kind="ExternalInput").ap()
    swp = nc.dram_tensor("swp", [128, 128], F32, kind="ExternalInput").ap()
    attn = nc.dram_tensor("attn", [1024, SEQ], BF16, kind="ExternalOutput").ap()

    ctx = ExitStack()
    xfm = ctx.enter_context(nc.sbuf_tensor("xfm", [128, DC * SEQ], BF16))
    v_all = ctx.enter_context(nc.sbuf_tensor("v_all", [128, 16 * 8 * 128], BF16))
    qb = ctx.enter_context(nc.sbuf_tensor("qb", [128, SEQ], BF16))
    kb = ctx.enter_context(nc.sbuf_tensor("kb", [128, SEQ], BF16))
    wslb = [ctx.enter_context(nc.sbuf_tensor(f"wslb{j}", [128, 3 * DC * 128], BF16)) for j in range(2)]
    cs2s = ctx.enter_context(nc.sbuf_tensor("cs2s", [128, 2 * SEQ], F32))
    mask4s = ctx.enter_context(nc.sbuf_tensor("mask4s", [128, 4 * 512], F32))
    lnss = ctx.enter_context(nc.sbuf_tensor("lnss", [128, 2], F32))
    ra = ctx.enter_context(nc.sbuf_tensor("ra", [128, 512], F32))
    rb_ = ctx.enter_context(nc.sbuf_tensor("rb_", [128, 512], BF16))
    swps = ctx.enter_context(nc.sbuf_tensor("swps", [128, 128], BF16))
    cbf = ctx.enter_context(nc.sbuf_tensor("cbf", [128, 512], F32))
    cbb = ctx.enter_context(nc.sbuf_tensor("cbb", [128, 512], BF16))
    sqh = ctx.enter_context(nc.sbuf_tensor("sqh", [128, 512], BF16))
    srow2 = ctx.enter_context(nc.sbuf_tensor("ROWA", [1, 1024], F32))
    rrow2 = ctx.enter_context(nc.sbuf_tensor("ROWA", [1, 512], F32))
    mrrow2 = ctx.enter_context(nc.sbuf_tensor("ROWA", [1, 512], F32))
    r2b = ctx.enter_context(nc.sbuf_tensor("r2b", [128, 512], F32))
    mrb = ctx.enter_context(nc.sbuf_tensor("mrb", [128, 512], F32))
    sa = ctx.enter_context(nc.sbuf_tensor("sa", [128, 512], F32))
    a4b = ctx.enter_context(nc.sbuf_tensor("a4b", [128, 512], BF16))
    drow = ctx.enter_context(nc.sbuf_tensor("ROWA", [1, 512], F32))
    rbq = ctx.enter_context(nc.sbuf_tensor("rbq", [128, 512], F32))
    attnb = ctx.enter_context(nc.sbuf_tensor("attnb", [128, SEQ], BF16))
    ones128b = ctx.enter_context(nc.sbuf_tensor("ones128b", [128, 1], BF16))
    onesrow = ctx.enter_context(nc.sbuf_tensor("onesrow", [1, 128], F32))
    consts = ctx.enter_context(nc.sbuf_tensor("consts", [128, 4], F32))
    pb = [ctx.enter_context(nc.psum_tensor(f"pb{j}", [128, 512], F32)) for j in range(8)]

    p = Prog(nc, ctx)
    p.op("vector", lambda v: v.memset(consts[:, 0:1], 1e-5), writes=["consts"])
    p.op("vector", lambda v: v.memset(ones128b[:, :], 1.0), writes=["ones128b"])
    p.op("vector", lambda v: v.memset(onesrow[:, :], 1.0), writes=["onesrow"])
    p.dma("gpsimd", lambda g: g.dma_start(
        xfm[:, :].rearrange("p (kc t) -> p kc t", t=SEQ),
        xn1_fm.rearrange("(kc p) t -> p kc t", p=128)),
        writes=[f"x.{kc}" for kc in range(DC)], sem="x")
    p.dma("gpsimd", lambda g: g.dma_start(cs2s[:, :], cs2[:, :]), writes=["cs2s"], sem="cs2s")
    p.dma("gpsimd", lambda g: g.dma_start(mask4s[:, :], mask4[:, :]), writes=["mask4s"], sem="mask4s")
    p.dma("gpsimd", lambda g: g.dma_start(lnss[:, :], lns_i[:, :]), writes=["lnss"], sem="lnss")
    p.dma("gpsimd", lambda g: g.dma_start(swps[:, :], swp[:, :]), writes=["swps"], sem="swps")

    for hl in range(8):
        par = hl % 2
        p.dma("gpsimd", lambda g, hl=hl, par=par: g.dma_start(
            wslb[par][:, :].rearrange("p (s kc o) -> p s kc o", kc=DC, o=128).rearrange("p s kc o -> p s kc o"),
            wqkv[hl].rearrange("s kc k o -> k s kc o")),
            writes=[f"wq.{par}"], sem=f"wq{par}")
        for tth in range(2):
            def f_v(t, hl=hl, par=par, tth=tth):
                for kc in range(DC):
                    for tt in range(TTI):
                        inst = t.matmul(
                            pb[tt][:, 0:128],
                            xfm[:, kc * SEQ + (tth * 8 + tt) * 128: kc * SEQ + (tth * 8 + tt + 1) * 128],
                            wslb[par][:, (2 * DC + kc) * 128:(2 * DC + kc + 1) * 128],
                            start=(kc == 0), stop=(kc == DC - 1))
                return inst
            p.op("tensor", f_v, reads=[f"wq.{par}"] + [f"x.{kc}" for kc in range(DC)],
                 writes=[f"pb.{tt}" for tt in range(TTI)])
            for tt in range(TTI):
                p.op("vector", lambda v, hl=hl, tth=tth, tt=tt: v.tensor_copy(
                        v_all[:, ((tth * 8 + tt) * 8 + hl) * 128: ((tth * 8 + tt) * 8 + hl + 1) * 128],
                        pb[tt][:, 0:128]),
                     reads=[f"pb.{tt}"], writes=[f"v.{hl}.{tth * 8 + tt}"])
        for side, dst in ((0, qb), (1, kb)):
            dk = "qb" if side == 0 else "kb"
            for nq in range(4):
                def f_qk(t, hl=hl, par=par, side=side, nq=nq):
                    for kc in range(DC):
                        inst = t.matmul(
                            pb[4][:, :],
                            wslb[par][:, (side * DC + kc) * 128:(side * DC + kc + 1) * 128],
                            xfm[:, kc * SEQ + nq * 512: kc * SEQ + (nq + 1) * 512],
                            start=(kc == 0), stop=(kc == DC - 1))
                    return inst
                p.op("tensor", f_qk, reads=[f"wq.{par}"] + [f"x.{kc}" for kc in range(DC)],
                     writes=["pb.4"])
                p.op("vector", lambda v, nq=nq: v.tensor_mul(
                        ra[:, :], pb[4][:, :], cs2s[:, nq * 512:(nq + 1) * 512]),
                     reads=["pb.4", "cs2s"], writes=["ra"])
                p.op("vector", lambda v, nq=nq: v.tensor_mul(
                        rb_[:, :], pb[4][:, :], cs2s[:, SEQ + nq * 512: SEQ + (nq + 1) * 512]),
                     reads=["pb.4", "cs2s"], writes=["rb_"])
                p.op("tensor", lambda t: t.matmul(pb[7][:, :], swps[:, :], rb_[:, :],
                                                  start=True, stop=True),
                     reads=["rb_", "swps"], writes=["pb.7"])
                p.op("vector", lambda v: v.tensor_sub(cbf[0:64, :], ra[0:64, :], pb[7][0:64, :]),
                     reads=["ra", "pb.7"], writes=["cbf"])
                p.op("vector", lambda v: v.tensor_add(cbf[64:128, :], ra[64:128, :], pb[7][64:128, :]),
                     reads=["ra", "pb.7"], writes=["cbf"])
                p.op("vector", lambda v: v.tensor_copy(cbb[:, :], cbf[:, :]),
                     reads=["cbf"], writes=["cbb"])
                p.op("scalar", lambda s: s.activation(sqh[:, :], cbf[:, :], AF.Square),
                     reads=["cbf"], writes=["sqh"])
                def f_st(t):
                    t.matmul(pb[5][0:1, :], ones128b[:, :], cbb[:, :], start=True, stop=True)
                    return t.matmul(pb[6][0:1, :], ones128b[:, :], sqh[:, :], start=True, stop=True)
                p.op("tensor", f_st, reads=["cbb", "sqh", "ones128b"], writes=["pb.5", "pb.6"])
                p.op("vector", lambda v: v.tensor_copy(srow2[:, 0:512], pb[5][0:1, :]),
                     reads=["pb.5"], writes=["ROWA"])
                p.op("vector", lambda v: v.tensor_copy(srow2[:, 512:1024], pb[6][0:1, :]),
                     reads=["pb.6"], writes=["ROWA"])
                p.op("vector", lambda v: v.tensor_scalar_mul(srow2[:, :], srow2[:, :], 1.0 / 128),
                     reads=["ROWA"], writes=["ROWA"])
                p.op("vector", lambda v: v.tensor_mul(rrow2[:, :], srow2[:, 0:512], srow2[:, 0:512]),
                     reads=["ROWA"], writes=["ROWA"])
                p.op("vector", lambda v: v.tensor_sub(rrow2[:, :], rrow2[:, :], srow2[:, 512:1024]),
                     reads=["ROWA", "ROWA"], writes=["ROWA"])
                p.op("scalar", lambda s: s.activation(rrow2[:, :], rrow2[:, :], AF.Sqrt,
                                                      bias=consts[0:1, 0:1], scale=-1.0),
                     reads=["ROWA", "consts"], writes=["ROWA"])
                p.op("vector", lambda v: v.reciprocal(rrow2[:, :], rrow2[:, :]),
                     reads=["ROWA"], writes=["ROWA"])
                p.op("vector", lambda v: v.tensor_mul(mrrow2[:, :], srow2[:, 0:512], rrow2[:, :]),
                     reads=["ROWA", "ROWA"], writes=["ROWA"])
                def f_bc(t):
                    t.matmul(pb[5][:, :], onesrow[:, :], rrow2[0:1, :], start=True, stop=True)
                    return t.matmul(pb[6][:, :], onesrow[:, :], mrrow2[0:1, :], start=True, stop=True)
                p.op("tensor", f_bc, reads=["ROWA", "ROWA", "onesrow"], writes=["pb.5", "pb.6"])
                p.op("vector", lambda v: v.tensor_copy(r2b[:, :], pb[5][:, :]),
                     reads=["pb.5"], writes=["r2b"])
                p.op("vector", lambda v: v.tensor_copy(mrb[:, :], pb[6][:, :]),
                     reads=["pb.6"], writes=["mrb"])
                p.op("vector", lambda v: v.tensor_mul(cbf[:, :], cbf[:, :], r2b[:, :]),
                     reads=["cbf", "r2b"], writes=["cbf"])
                p.op("vector", lambda v: v.tensor_sub(cbf[:, :], cbf[:, :], mrb[:, :]),
                     reads=["cbf", "mrb"], writes=["cbf"])
                p.op("vector", lambda v, dst=dst, nq=nq: v.tensor_scalar(
                        dst[:, nq * 512:(nq + 1) * 512], cbf[:, :],
                        lnss[:, 0:1], lnss[:, 1:2], ALU.mult, ALU.add),
                     reads=["cbf", "lnss"], writes=[f"{dk}.{nq}"])
        for qs in range(4):
            for kt in range(qs * 4 + 4):
                spar = kt % 2
                p.op("tensor", lambda t, hl=hl, qs=qs, kt=kt, spar=spar: t.matmul(
                        pb[spar][:, :], kb[:, kt * 128:(kt + 1) * 128],
                        qb[:, qs * 512:(qs + 1) * 512], start=True, stop=True),
                     reads=[f"kb.{kt // 4}", f"qb.{qs}"], writes=[f"pb.{spar}"])
                p.op("scalar", lambda s, spar=spar: s.activation(sa[:, :], pb[spar][:, :], AF.Square),
                     reads=[f"pb.{spar}"], writes=["sa"])
                if kt >= qs * 4:
                    off = kt - qs * 4
                    p.op("vector", lambda v, off=off: v.tensor_mul(
                            sa[:, :], sa[:, :], mask4s[:, off * 512:(off + 1) * 512]),
                         reads=["sa", "mask4s"], writes=["sa"])
                p.op("vector", lambda v: v.tensor_mul(a4b[:, :], sa[:, :], sa[:, :]),
                     reads=["sa"], writes=["a4b"])
                def f_acc(t, hl=hl, qs=qs, kt=kt):
                    t.matmul(pb[2][0:1, :], ones128b[:, :], a4b[:, :],
                             start=(kt == 0), stop=(kt == qs * 4 + 3))
                    return t.matmul(pb[3][:, :], v_all[:, (kt * 8 + hl) * 128:(kt * 8 + hl + 1) * 128],
                                    a4b[:, :], start=(kt == 0), stop=(kt == qs * 4 + 3))
                p.op("tensor", f_acc, reads=["a4b", "ones128b", f"v.{hl}.{kt}"],
                     writes=["pb.2", "pb.3"])
            p.op("vector", lambda v: v.tensor_copy(drow[:, :], pb[2][0:1, :]),
                 reads=["pb.2"], writes=["ROWA"])
            p.op("vector", lambda v: v.reciprocal(drow[:, :], drow[:, :]),
                 reads=["ROWA"], writes=["ROWA"])
            p.op("tensor", lambda t: t.matmul(pb[4][:, :], onesrow[:, :], drow[0:1, :],
                                              start=True, stop=True),
                 reads=["ROWA", "onesrow"], writes=["pb.4"])
            p.op("vector", lambda v: v.tensor_copy(rbq[:, :], pb[4][:, :]),
                 reads=["pb.4"], writes=["rbq"])
            p.op("vector", lambda v, qs=qs: v.tensor_mul(
                    attnb[:, qs * 512:(qs + 1) * 512], pb[3][:, :], rbq[:, :]),
                 reads=["pb.3", "rbq"], writes=[f"attnb.{qs}"])
        p.dma("sync", lambda s, hl=hl: s.dma_start(
            attn[hl * 128:(hl + 1) * 128, :], attnb[:, :]),
            reads=[f"attnb.{qs}" for qs in range(4)], writes=[f"attnout.{hl}"], sem="attnout")

    p.barrier_all("sync")
    p.finalize()
    ctx.close()
    return nc


def build_launch_E():
    nc = bass.Bass("TRN2", target_bir_lowering=False)
    attn_sl = nc.dram_tensor("attn_sl", [D, T], BF16, kind="ExternalInput").ap()
    w_out1 = nc.dram_tensor("w_out1", [4, DC, 128, 512], F32, kind="ExternalInput").ap()
    x1 = nc.dram_tensor("x1", [T, D], F32, kind="ExternalInput").ap()
    g1 = nc.dram_tensor("g1", [T, D], BF16, kind="ExternalInput").ap()
    x2 = nc.dram_tensor("x2", [T, D], F32, kind="ExternalOutput").ap()

    ctx = ExitStack()
    abuf = ctx.enter_context(nc.sbuf_tensor("abuf", [128, DC * T], BF16))
    xt = ctx.enter_context(nc.sbuf_tensor("xt", [128, TTI * D], F32))
    g1sl = [ctx.enter_context(nc.sbuf_tensor(f"g1sl{j}", [128, TTI * 512], BF16)) for j in range(2)]
    wsl = [ctx.enter_context(nc.sbuf_tensor(f"wsl{j}", [128, 8 * 512], BF16)) for j in range(2)]
    tmp = ctx.enter_context(nc.sbuf_tensor("tmp", [128, 512], F32))
    pb = [ctx.enter_context(nc.psum_tensor(f"pb{j}", [128, 512], F32)) for j in range(8)]

    p = Prog(nc, ctx)
    p.dma("gpsimd", lambda g: g.dma_start(
        abuf[:, :].rearrange("p (kc t) -> p kc t", t=T),
        attn_sl.rearrange("(kc p) t -> p kc t", p=128)),
        writes=[f"a.{kc}" for kc in range(DC)], sem="a")
    p.dma("gpsimd", lambda g: g.dma_start(
        xt[:, :].rearrange("p (tt d) -> p tt d", d=D),
        x1.rearrange("(tt p) d -> p tt d", p=128)),
        writes=[f"xt.{tt}" for tt in range(TTI)], sem="xt")

    for oc in range(4):
        gpar = oc % 2
        p.dma("gpsimd", lambda g, oc=oc, gpar=gpar: g.dma_start(
            g1sl[gpar][:, :].rearrange("p (tt o) -> p tt o", o=512),
            g1.rearrange("(tt p) d -> p tt d", p=128)[:, :, oc * 512:(oc + 1) * 512]),
            writes=[f"g1sl.{gpar}"], sem=f"g1sl{gpar}")
        for kcg in range(2):
            par = kcg % 2
            p.dma("gpsimd", lambda g, oc=oc, kcg=kcg, par=par: g.dma_start(
                wsl[par][:, :].rearrange("p (kc o) -> p kc o", o=512),
                w_out1[oc, kcg * 8:(kcg + 1) * 8].rearrange("kc k o -> k kc o")),
                writes=[f"wsl.{par}"], sem=f"wsl{par}")
            def f_mm(t, oc=oc, kcg=kcg, par=par):
                for kci in range(8):
                    kc = kcg * 8 + kci
                    for tt in range(TTI):
                        inst = t.matmul(
                            pb[tt][:, :],
                            abuf[:, kc * T + tt * 128: kc * T + (tt + 1) * 128],
                            wsl[par][:, kci * 512:(kci + 1) * 512],
                            start=(kc == 0), stop=(kc == DC - 1))
                return inst
            p.op("tensor", f_mm,
                 reads=[f"wsl.{par}"] + [f"a.{kcg * 8 + j}" for j in range(8)],
                 writes=[f"pb.{tt}" for tt in range(TTI)])
        for tt in range(TTI):
            p.op("vector", lambda v, oc=oc, tt=tt: v.tensor_mul(
                    tmp[:, :], pb[tt][:, :],
                    g1sl[oc % 2][:, tt * 512:(tt + 1) * 512]),
                 reads=[f"pb.{tt}", f"g1sl.{oc % 2}"], writes=["tmp"])
            p.op("vector", lambda v, oc=oc, tt=tt: v.tensor_add(
                    xt[:, tt * D + oc * 512: tt * D + (oc + 1) * 512],
                    xt[:, tt * D + oc * 512: tt * D + (oc + 1) * 512], tmp[:, :]),
                 reads=["tmp", f"xt.{tt}"], writes=[f"xt.{tt}"])
    for tt in range(TTI):
        p.dma("sync", lambda s, tt=tt: s.dma_start(
            x2[tt * 128:(tt + 1) * 128, :], xt[:, tt * D:(tt + 1) * D]),
            reads=[f"xt.{tt}"], writes=[f"x2out.{tt}"], sem="x2out")

    p.barrier_all("sync")
    p.finalize()
    ctx.close()
    return nc


NT = 16
NCHL = 16


def build_fused():
    nc = bass.Bass("TRN2", target_bir_lowering=False)
    I32 = mybir.dt.int32
    ap = lambda n, s, d=F32: nc.dram_tensor(n, s, d, kind="ExternalInput").ap()
    xh = ap("xh", [T, D])
    w_in = ap("w_in", [64, DC, 128, 128])
    w_gu = ap("w_gu", [128, DC, 128, 128])
    w_d = ap("w_d", [4, 64, 128, 512])
    w_out = ap("w_out", [4, DC, 128, 512])
    w1 = ap("w1", [4, DC, 128, 512])
    wqkv = ap("wqkv", [8, 2, DC, 128, 128])
    wv_b = ap("wv_b", [2, DC, 128, 512])
    w_out1 = ap("w_out1", [4, DC, 128, 512])
    cw = ap("cw", [128, NT * 3 * 3])
    cb = ap("cb", [128, NT * 3])
    alog = ap("alog", [128, NT])
    ident = ap("ident", [128, 128])
    mask_ssd = ap("mask_ssd", [128, 64])
    rmsw = ap("rmsw", [128, DC])
    lnwb = ap("lnwb", [128, 2 * DC])
    gamrow = ap("gamrow", [1, 2 * D])
    lns_i = ap("lns", [128, 2])
    cs2 = ap("cs2", [128, 2 * SEQ])
    mask4 = ap("mask4", [128, 4 * 512])
    swp = ap("swp", [128, 128])
    cidx = ap("cidx", [128, NCHL])
    bmask = ap("bmask", [128, 4])
    hhoff = nc.dram_tensor("hhoff", [1, 1], I32, kind="ExternalInput").ap()
    x2 = nc.dram_tensor("x2", [T, D], F32, kind="ExternalOutput").ap()
    idt = lambda n, s, d: nc.dram_tensor(n, s, d)
    hb_in = idt("hb_in", [2, 3 * D], BF16)
    hb_out = idt("hb_out", [4, 3 * D], BF16)
    st_in = idt("st_in", [128, NT], BF16)
    st_out = idt("st_out", [256, NT], BF16)
    xn_in = idt("xn_in", [D, T], BF16)
    xn_out = idt("xn_out", [2 * D, T], BF16)
    at_in = idt("at_in", [1024, SEQ], BF16)
    at_out = idt("at_out", [2048, SEQ], BF16)
    x1_sp = idt("x1_sp", [T, D], F32)
    s0_sp = idt("s0_sp", [T, D], BF16)
    g0_sp = idt("g0_sp", [T, D], BF16)
    g1_sp = idt("g1_sp", [T, D], BF16)
    RG = [[0, 1], [2, 3], [4, 5], [6, 7]]

    ctx = ExitStack()
    R1 = ctx.enter_context(nc.sbuf_tensor("R1", [128, 2 * DC * T], BF16))
    RA = ctx.enter_context(nc.sbuf_tensor("RA", [128, DC * T], BF16))
    RB = ctx.enter_context(nc.sbuf_tensor("RB", [128, DC * T], BF16))
    RZ = ctx.enter_context(nc.sbuf_tensor("RZ", [128, DC * T], BF16))
    WS = ctx.enter_context(nc.sbuf_tensor("WS", [128, 8192], BF16))
    SC = ctx.enter_context(nc.sbuf_tensor("SC", [128, 2048], F32))
    PR = ctx.enter_context(nc.sbuf_tensor("PR", [128, 2048], BF16))
    identb = ctx.enter_context(nc.sbuf_tensor("identb", [128, 128], BF16))
    maskb = ctx.enter_context(nc.sbuf_tensor("maskb", [128, 64], BF16))
    swps = ctx.enter_context(nc.sbuf_tensor("swps", [128, 128], BF16))
    cwsb = ctx.enter_context(nc.sbuf_tensor("cwsb", [128, NT * 9], F32))
    cbsb = ctx.enter_context(nc.sbuf_tensor("cbsb", [128, NT * 3], F32))
    alogs = ctx.enter_context(nc.sbuf_tensor("alogs", [128, NT], F32))
    decay = ctx.enter_context(nc.sbuf_tensor("decay", [128, NT], F32))
    cidxs = ctx.enter_context(nc.sbuf_tensor("cidxs", [128, NCHL], F32))
    bmasks = ctx.enter_context(nc.sbuf_tensor("bmasks", [128, 4], F32))
    hhoffs = ctx.enter_context(nc.sbuf_tensor("hhoffs", [1, 1], I32))
    rmsws = ctx.enter_context(nc.sbuf_tensor("rmsws", [128, DC], F32))
    lnwbs = ctx.enter_context(nc.sbuf_tensor("lnwbs", [128, 2 * DC], F32))
    lnss = ctx.enter_context(nc.sbuf_tensor("lnss", [128, 2], F32))
    ROWA = ctx.enter_context(nc.sbuf_tensor("ROWA", [1, 3072], F32))
    hlbuf = ctx.enter_context(nc.sbuf_tensor("hlbuf", [128, 2 * NT * 3], F32))
    hstage = ctx.enter_context(nc.sbuf_tensor("hstage", [128, 112], BF16))
    prevb = ctx.enter_context(nc.sbuf_tensor("prevb", [128, NT * (NCHL + 1)], F32))
    cs_all = ctx.enter_context(nc.sbuf_tensor("cs_all", [128, NT * NCHL], F32))
    s0f = ctx.enter_context(nc.sbuf_tensor("s0f", [128, NT], F32))
    powm = ctx.enter_context(nc.sbuf_tensor("powm", [128, NCHL], F32))
    ones128b = ctx.enter_context(nc.sbuf_tensor("ones128b", [128, 1], BF16))
    onesrowb = ctx.enter_context(nc.sbuf_tensor("onesrowb", [1, 128], BF16))
    invrowb = ctx.enter_context(nc.sbuf_tensor("invrowb", [1, 128], BF16))
    consts = ctx.enter_context(nc.sbuf_tensor("consts", [128, 2], F32))
    rrow = ROWA[:, 0:2048]
    r2row = ROWA[:, 2048:3072]
    mrrow = ROWA[:, 0:1024]
    srow2 = ROWA[:, 0:1024]
    rrow2 = ROWA[:, 1024:1536]
    mrrow2 = ROWA[:, 1536:2048]
    drow = ROWA[:, 2048:2560]
    amrow = ROWA[:, 0:128]
    scal1 = ROWA[:, 128:129]
    amaxc = ctx.enter_context(nc.sbuf_tensor("amaxc", [128, 1], BF16))
    amx_t = ctx.enter_context(nc.sbuf_tensor("amx_t", [128, 1], BF16))
    scalc = ctx.enter_context(nc.sbuf_tensor("scalc", [128, 1], F32))
    stats = {k: ctx.enter_context(nc.sbuf_tensor(f"st_{k}", [128, TTI], F32))
             for k in ("m0", "s0", "m", "var", "rstd")}
    sq = ctx.enter_context(nc.sbuf_tensor("sqs", [128, 1024], BF16))
    pb = [ctx.enter_context(nc.psum_tensor(f"pb{j}", [128, 512], F32)) for j in range(8)]
    R1b = R1

    SCb = SC.bitcast(BF16)
    gt_sb = [WS[:, 0:512], WS[:, 2048:2560]]
    r2b7 = R1[:, 3 * T:4 * T]
    r2bb = RZ[:, 0:T]
    mrbb = RZ[:, T:2 * T]
    ROWAb2 = ROWA.bitcast(BF16)
    yoff_sb = [ROWAb2[:, i * 512:(i + 1) * 512] for i in range(4)]

    p = Prog(nc, ctx)

    def kR1(j):  return f"R1.{j}"
    def kRA(j):  return f"RA.{j}"
    def kRB(j):  return f"RB.{j}"
    def kRZ(j):  return f"RZ.{j}"
    def kWS(j):  return f"WS.{j}"
    def r1t(j):  return R1[:, j * T:(j + 1) * T]
    def rat(j):  return RA[:, j * T:(j + 1) * T]
    def rbt(j):  return RB[:, j * T:(j + 1) * T]
    def rzt(j):  return RZ[:, j * T:(j + 1) * T]

    p.op("vector", lambda v: v.memset(consts[:, 0:1], 1e-5), writes=["consts"])
    p.op("vector", lambda v: v.memset(consts[:, 1:2], 1.1920929e-07), writes=["consts"])
    p.op("vector", lambda v: v.memset(ones128b[:, :], 1.0), writes=["ones128b"])
    p.op("vector", lambda v: v.memset(onesrowb[:, :], 1.0), writes=["onesrowb"])
    p.op("vector", lambda v: v.memset(invrowb[:, :], 1.0 / 128), writes=["invrowb"])
    p.op("vector", lambda v: v.memset(amaxc[:, :], 0.0), writes=["amaxc"])
    p.op("vector", lambda v: v.memset(prevb[:, :], 0.0), writes=["prevb.z"])
    p.op("vector", lambda v: v.memset(pb[0][:, :], 0.0), writes=["pb.0"])
    p.op("vector", lambda v: v.memset(pb[1][:, :], 0.0), writes=["pb.1"])
    for nm, dst, src in (("identb", identb, ident),
                         ("maskb", maskb, mask_ssd), ("swps", swps, swp),
                         ("cwsb", cwsb, cw), ("cbsb", cbsb, cb), ("alogs", alogs, alog),
                         ("cidxs", cidxs, cidx), ("bmasks", bmasks, bmask),
                         ("rmsws", rmsws, rmsw), ("lnwbs", lnwbs, lnwb),
                         ("lnss", lnss, lns_i),
                         ("hhoffs", hhoffs, hhoff)):
        p.dma("gpsimd", lambda g, dst=dst, src=src: g.dma_start(dst[:, :], src[:, :]),
              writes=[nm], sem=nm)
    p.op("scalar", lambda s: s.activation(decay[:, :], alogs[:, :], AF.Exp, scale=63.0),
         reads=["alogs"], writes=["decay"])

    for tt in range(TTI):
        slot = tt % 2
        xsl = SC[:, 0:2048]
        lnout = R1[:, slot * 2048:(slot + 1) * 2048]
        lnkeys = [kR1(slot * 2), kR1(slot * 2 + 1)]
        p.dma("gpsimd", lambda g, tt=tt, xsl=xsl: g.dma_start(
            xsl, xh[tt * 128:(tt + 1) * 128, :]),
            writes=["SC.0", "SC.1"], sem="xl0")
        emit_ln_tm(p, xsl, ["SC.0", "SC.1"], lnout, lnkeys[0], stats, tt, sq,
                   eps_col=consts[:, 0:1], pref="ln0", sqk=("sq.0", "sq.1"))
        p.wr[lnkeys[1]] = list(p.wr.get(lnkeys[0], ()))
        for half in range(2):
            bank = pb[half].bitcast(BF16)
            bkey = f"pb.{half}"
            def f_tr(t, tt=tt, half=half, bank=bank, lnout=lnout):
                insts = []
                for j in range(8):
                    kc = half * 8 + j
                    insts.append(t.transpose(
                        bank[:, j * 128:(j + 1) * 128],
                        lnout[:, kc * 128:(kc + 1) * 128], identb[:, :]))
                return insts
            p.op("tensor", f_tr, reads=lnkeys + ["identb"], writes=[bkey])
            def f_ev(v, tt=tt, half=half, bank=bank):
                dst = RA[:, :].rearrange("p (kc t) -> p kc t", t=T)[
                    :, half * 8:(half + 1) * 8, tt * 128:(tt + 1) * 128]
                return v.tensor_copy(dst, bank[:, 0:1024].rearrange("p (kc t) -> p kc t", t=128))
            p.op("vector", f_ev, reads=[bkey],
                 writes=[kRA(half * 8 + j) for j in range(8)])
    xn0_keys = [kRA(j) for j in range(DC)]

    def inproj_dst(oc):
        if oc < 16:   return rzt(oc), kRZ(oc)
        return (r1t(oc - 16), kR1(oc - 16)) if oc < 48 else (rbt(oc - 48), kRB(oc - 48))
    def f_gb(v):
        for r, c in ((0, 0), (1, T - 1)):
            v.tensor_copy(hstage[:, r * 48: r * 48 + 32].unsqueeze(2),
                          R1[:, 0:32 * T].rearrange("p (j t) -> p j t", t=T)[:, :, c:c + 1])
            inst = v.tensor_copy(hstage[:, r * 48 + 32: r * 48 + 48].unsqueeze(2),
                                 RB[:, :].rearrange("p (j t) -> p j t", t=T)[:, :, c:c + 1])
        return inst
    for oc in list(range(16, 64)) + list(range(16)):
        par = oc % 2
        wsl_ = WS[:, par * 2048:(par + 1) * 2048]
        p.dma("gpsimd", lambda g, oc=oc, wsl_=wsl_: g.dma_start(
            wsl_.rearrange("p (kc o) -> p kc o", o=128),
            w_in[oc].rearrange("kc k o -> k kc o")),
            writes=[kWS(par)], sem=f"ws{par}")
        pbs, pbks = [pb[par * 2], pb[par * 2 + 1]], [f"pb.{par * 2}", f"pb.{par * 2 + 1}"]
        def f_mm(t, oc=oc, par=par, wsl_=wsl_, pbs=pbs):
            for kc in range(DC):
                for nh in range(2):
                    inst = t.matmul(
                        pbs[nh][:, :], wsl_[:, kc * 128:(kc + 1) * 128],
                        RA[:, kc * T + nh * 512: kc * T + (nh + 1) * 512],
                        start=(kc == 0), stop=(kc == DC - 1))
            return inst
        p.op("tensor", f_mm, reads=[kWS(par)] + xn0_keys, writes=pbks)
        dst, dkey = inproj_dst(oc)
        for nh in range(2):
            p.op("vector", lambda v, dst=dst, nh=nh, pbs=pbs: v.tensor_copy(
                dst[:, nh * 512:(nh + 1) * 512], pbs[nh][:, :]),
                reads=[pbks[nh]], writes=[dkey])
        if oc == 63:
            p.op("vector", f_gb,
                 reads=[kR1(j) for j in range(32)] + [kRB(j) for j in range(16)],
                 writes=["hstage"])
            p.dma("sync", lambda g: g.dma_start(
                hb_in.ap().rearrange("r (p t) -> p r t", p=128),
                hstage[:, 0:96].rearrange("p (r t) -> p r t", t=48)),
                reads=["hstage"], writes=["hb_in"], sem="hbi")
    p.coll(lambda g: g.collective_compute(
        "AllGather", ALU.bypass, replica_groups=RG,
        ins=[hb_in.ap().opt()], outs=[hb_out.ap().opt()]),
        reads=["hb_in"], writes=["hb_out"], sem="cc1")
    p.dma("gpsimd", lambda g: g.dma_start(
        hlbuf[:, :].rearrange("p (r t) -> p r t", t=48),
        hb_out.ap()[1:3, :].rearrange("r (p t) -> p r t", p=128)),
        reads=["hb_out"], writes=["hlbuf"], sem="hbo")
    p.op("vector", lambda v: v.tensor_scalar_mul(hlbuf[:, 0:48], hlbuf[:, 0:48], bmasks[:, 0:1]),
         reads=["hlbuf", "bmasks"], writes=["hlbuf"])
    p.op("vector", lambda v: v.tensor_scalar_mul(hlbuf[:, 48:96], hlbuf[:, 48:96], bmasks[:, 1:2]),
         reads=["hlbuf", "bmasks"], writes=["hlbuf"])

    def conv_tile_f(buf_ap, key, gidx):
        par = gidx % 2
        a0, a1 = pb[4], pb[5]
        ak0, ak1 = "pb.4", "pb.5"
        acc_b = SC[:, par * 1024:(par + 1) * 1024]
        bk = f"SC.{par}"
        src = buf_ap
        w0 = cwsb[:, gidx * 3 + 0: gidx * 3 + 1]
        w1_ = cwsb[:, gidx * 3 + 1: gidx * 3 + 2]
        w2 = cwsb[:, gidx * 3 + 2: gidx * 3 + 3]
        bcol = cbsb[:, gidx:gidx + 1]
        hl = hlbuf[:, gidx:gidx + 1]
        hr = hlbuf[:, 48 + gidx: 48 + gidx + 1]
        p.op("vector", lambda v, a0=a0: v.tensor_scalar_mul(a0[:, 0:1], hl, w0),
             reads=["hlbuf", "cwsb"], writes=[ak0])
        p.op("vector", lambda v, a0=a0: v.tensor_scalar_mul(a0[:, 1:512], src[:, 0:511], w0),
             reads=[key, "cwsb"], writes=[ak0])
        p.op("vector", lambda v, a1=a1: v.tensor_scalar_mul(a1[:, :], src[:, 511:1023], w0),
             reads=[key, "cwsb"], writes=[ak1])
        p.op("vector", lambda v, a0=a0: v.scalar_tensor_tensor(
                a0[:, :], src[:, 0:512], w1_, a0[:, :], ALU.mult, ALU.add),
             reads=[key, "cwsb", ak0], writes=[ak0])
        p.op("vector", lambda v, a1=a1: v.scalar_tensor_tensor(
                a1[:, :], src[:, 512:1024], w1_, a1[:, :], ALU.mult, ALU.add),
             reads=[key, "cwsb", ak1], writes=[ak1])
        p.op("gpsimd", lambda g, acc_b=acc_b: g.tensor_scalar_mul(
                acc_b[:, 0:T - 1], src[:, 1:T], w2),
             reads=[key, "cwsb"], writes=[bk])
        p.op("gpsimd", lambda g, acc_b=acc_b: g.tensor_scalar_mul(
                acc_b[:, T - 1:T], hr, w2),
             reads=["hlbuf", "cwsb"], writes=[bk])
        p.op("vector", lambda v, a0=a0, acc_b=acc_b: v.tensor_add(
                acc_b[:, 0:512], acc_b[:, 0:512], a0[:, :]),
             reads=[ak0, bk], writes=[bk])
        p.op("vector", lambda v, a1=a1, acc_b=acc_b: v.tensor_add(
                acc_b[:, 512:1024], acc_b[:, 512:1024], a1[:, :]),
             reads=[ak1, bk], writes=[bk])
        p.op("scalar", lambda s, acc_b=acc_b: s.activation(src, acc_b[:, :], AF.Silu, bias=bcol),
             reads=[bk, "cbsb", key], writes=[key])

    for j in range(NT):
        conv_tile_f(r1t(j), kR1(j), j)
        conv_tile_f(r1t(16 + j), kR1(16 + j), 16 + j)
        pj = PR[:, (j % 2) * T:(j % 2 + 1) * T]
        pjk = f"PR.{j % 2}"
        p.op("vector", lambda v, j=j, pj=pj: v.tensor_tensor(
                pj, r1t(j), r1t(16 + j), ALU.mult),
             reads=[kR1(j), kR1(16 + j)], writes=[pjk])
        p.op("vector", lambda v, j=j, pj=pj: v.reduce_sum(
                cs_all[:, j * NCHL:(j + 1) * NCHL].unsqueeze(2),
                pj.rearrange("p (c s) -> p c s", s=CH), axis=AX.X),
             reads=[pjk], writes=[f"cs.{j}"])
        p.op("vector", lambda v, j=j: v.tensor_tensor_scan(
                prevb[:, j * (NCHL + 1) + 1: j * (NCHL + 1) + 1 + NCHL],
                decay[:, j:j + 1].to_broadcast([128, NCHL]),
                cs_all[:, j * NCHL:(j + 1) * NCHL], 0.0, ALU.mult, ALU.add),
             reads=[f"cs.{j}", "decay", "prevb.z"], writes=[f"prev.{j}"])
        bank = pb[6].bitcast(BF16)
        def f_tr(t, j=j, bank=bank):
            insts = []
            for i in range(8):
                insts.append(t.transpose(
                    bank[:, i * 128:(i + 1) * 128],
                    R1[:, j * T + i * 128: j * T + (i + 1) * 128], identb[:, :]))
            return insts
        p.op("tensor", f_tr, reads=[kR1(j), "identb", pjk], writes=["pb.6"])
        p.op("vector", lambda v, j=j, bank=bank: v.tensor_copy(r1t(j), bank[:, 0:1024]),
             reads=["pb.6"], writes=[kR1(j)])

    p.op("vector", lambda v: v.tensor_copy(
            hstage[:, 96:96 + NT].unsqueeze(2),
            prevb[:, :].rearrange("p (j c) -> p j c", c=NCHL + 1)[:, :, NCHL:NCHL + 1]),
         reads=[f"prev.{j}" for j in range(NT)], writes=["finst"])
    p.dma("sync", lambda g: g.dma_start(st_in.ap(), hstage[:, 96:96 + NT]),
          reads=["finst"], writes=["st_in"], sem="sti")
    p.coll(lambda g: g.collective_compute(
        "AllGather", ALU.bypass, replica_groups=RG,
        ins=[st_in.ap().opt()], outs=[st_out.ap().opt()]),
        reads=["st_in"], writes=["st_out"], sem="cc2")
    p.dma("gpsimd", lambda g: g.dma_start(s0f[:, :], st_out.ap()[0:128, :]),
          reads=["st_out"], writes=["s0f"], sem="sto")
    p.op("vector", lambda v: v.tensor_scalar_mul(s0f[:, :], s0f[:, :], bmasks[:, 0:1]),
         reads=["s0f", "bmasks"], writes=["s0f"])
    for j in range(NT):
        p.op("vector", lambda v, j=j: v.tensor_scalar_mul(
                powm[:, :], cidxs[:, :], alogs[:, j:j + 1]),
             reads=["cidxs", "alogs"], writes=["powm"])
        p.op("scalar", lambda s: s.activation(powm[:, :], powm[:, :], AF.Exp, scale=63.0),
             reads=["powm"], writes=["powm"])
        p.op("vector", lambda v, j=j: v.tensor_scalar_mul(
                powm[:, :], powm[:, :], s0f[:, j:j + 1]),
             reads=["powm", "s0f"], writes=["powm"])
        p.op("vector", lambda v, j=j: v.tensor_add(
                prevb[:, j * (NCHL + 1): j * (NCHL + 1) + NCHL],
                prevb[:, j * (NCHL + 1): j * (NCHL + 1) + NCHL], powm[:, :]),
             reads=["powm", f"prev.{j}"], writes=[f"prev.{j}"])

    for j in range(NT):
        conv_tile_f(rbt(j), kRB(j), 32 + j)
        p.op("scalar", lambda s, j=j: s.activation(SC[:, 0:1024], rzt(j), AF.Silu),
             reads=[kRZ(j)], writes=["SC.0"])
        pj = PR[:, (j % 2) * T:(j % 2 + 1) * T]
        pjk = f"PR.{j % 2}"
        prev_j = prevb[:, j * (NCHL + 1): j * (NCHL + 1) + NCHL]
        p.op("vector", lambda v, j=j, pj=pj, prev_j=prev_j: v.tensor_tensor(
                pj.rearrange("p (c s) -> p c s", s=CH),
                prev_j.unsqueeze(2).to_broadcast([128, NCHL, CH]),
                rbt(j).rearrange("p (c s) -> p c s", s=CH), ALU.mult),
             reads=[f"prev.{j}", kRB(j), "prevb.z"], writes=[pjk])
        for g_ in range(2):
            gpar = (j * 2 + g_) % 2
            yb = pb[2 + gpar]
            ybk = f"pb.{2 + gpar}"
            for e in range(2):
                ob = pb[4 + e]
                obk = f"pb.{4 + e}"
                yo = yoff_sb[gpar * 2 + e]
                p.op("tensor", lambda t, j=j, g_=g_, e=e, ob=ob, pj=pj: t.matmul(
                        ob[0:1, :], ones128b[e * 64:e * 64 + 64, :],
                        pj[e * 64:e * 64 + 64, g_ * 512:(g_ + 1) * 512],
                        start=True, stop=True),
                     reads=[pjk, "ones128b"], writes=[obk])
                p.op("vector", lambda v, yo=yo, ob=ob: v.tensor_copy(yo[:, :], ob[0:1, :]),
                     reads=[obk], writes=["ROWA"])
                gb = pb[e]
                gbk = f"pb.{e}"
                def f_g(t, j=j, g_=g_, e=e, gb=gb):
                    for ci in range(8):
                        c = g_ * 8 + ci
                        rh = (c % 2) * 64
                        inst = t.matmul(
                            gb[rh:rh + 64, ci * 64:(ci + 1) * 64],
                            R1[e * 64:e * 64 + 64, (16 + j) * T + c * CH: (16 + j) * T + (c + 1) * CH],
                            RB[e * 64:e * 64 + 64, j * T + c * CH: j * T + (c + 1) * CH],
                            start=True, stop=True)
                    return inst
                p.op("tensor", f_g, reads=[kR1(16 + j), kRB(j)], writes=[gbk])
                p.op("vector", lambda v, e=e, gb=gb: v.tensor_tensor(
                        gt_sb[e][:, :].rearrange("p (ci l) -> p ci l", l=64),
                        gb[:, :].rearrange("p (ci l) -> p ci l", l=64),
                        maskb[:, :].unsqueeze(1).to_broadcast([128, 8, 64]), ALU.mult),
                     reads=[gbk, "maskb", kWS(e)], writes=[f"gts.{e}", kWS(e)])
            def f_y(t, j=j, g_=g_, gpar=gpar, yb=yb):
                for ci in range(8):
                    c = g_ * 8 + ci
                    sb = c // 2
                    mrow = (c % 2) * 64
                    for e in range(2):
                        t.matmul(
                            yb[e * 64:e * 64 + 64, ci * 64:(ci + 1) * 64],
                            R1[mrow:mrow + 64, j * T + sb * 128 + e * 64: j * T + sb * 128 + e * 64 + 64],
                            gt_sb[e][mrow:mrow + 64, ci * 64:(ci + 1) * 64],
                            start=True, stop=False)
                        inst = t.matmul(
                            yb[e * 64:e * 64 + 64, ci * 64:(ci + 1) * 64],
                            onesrowb[0:1, 0:64],
                            yoff_sb[gpar * 2 + e][0:1, ci * 64:(ci + 1) * 64],
                            start=False, stop=True)
                return inst
            p.op("tensor", f_y,
                 reads=["gts.0", "gts.1", kWS(0), kWS(1), kR1(j), "onesrowb", "ROWA"],
                 writes=[ybk])
            p.op("vector", lambda v, j=j, g_=g_, yb=yb: v.tensor_mul(
                    rzt(j)[:, g_ * 512:(g_ + 1) * 512], yb[:, :],
                    SC[:, g_ * 512:(g_ + 1) * 512]),
                 reads=[ybk, "SC.0"], writes=[kRZ(j)])

    sqb = R1[:, 0:1024]
    for kc in range(DC):
        p.op("scalar", lambda s, kc=kc: s.activation(sqb[:, 0:1024], rzt(kc), AF.Square),
             reads=[kRZ(kc)], writes=[kR1(0)])
        def f_st(t, kc=kc):
            for nh in range(2):
                inst = t.matmul(pb[6 + nh][0:1, :], ones128b[:, :],
                                sqb[:, nh * 512:(nh + 1) * 512],
                                start=(kc == 0), stop=(kc == DC - 1))
            return inst
        p.op("tensor", f_st, reads=[kR1(0), "ones128b"], writes=["pb.6", "pb.7"])
    for nh in range(2):
        p.op("vector", lambda v, nh=nh: v.tensor_copy(rrow[:, nh * 512:(nh + 1) * 512], pb[6 + nh][0:1, :]),
             reads=[f"pb.{6 + nh}"], writes=["ROWA"])
    p.op("vector", lambda v: v.tensor_scalar_mul(rrow[:, 0:T], rrow[:, 0:T], 1.0 / D),
         reads=["ROWA"], writes=["ROWA"])
    p.op("scalar", lambda s: s.activation(rrow[:, 0:T], rrow[:, 0:T], AF.Sqrt, bias=consts[0:1, 1:2]),
         reads=["ROWA", "consts"], writes=["ROWA"])
    p.op("vector", lambda v: v.reciprocal(rrow[:, 0:T], rrow[:, 0:T]),
         reads=["ROWA"], writes=["ROWA"])
    ROWB7 = ROWA.bitcast(BF16)
    p.op("scalar", lambda s: s.activation(ROWB7[0:1, 4096:4096 + T], rrow[0:1, 0:T], AF.Copy),
         reads=["ROWA"], writes=["ROWA"])
    def f_bc(t):
        for nh in range(2):
            inst = t.matmul(pb[6 + nh][:, :], onesrowb[:, :],
                            ROWB7[0:1, 4096 + nh * 512: 4096 + (nh + 1) * 512],
                            start=True, stop=True)
        return inst
    p.op("tensor", f_bc, reads=["ROWA", "onesrowb"], writes=["pb.6", "pb.7"])
    for nh in range(2):
        p.op("vector", lambda v, nh=nh: v.tensor_copy(r2b7[:, nh * 512:(nh + 1) * 512], pb[6 + nh][:, :]),
             reads=[f"pb.{6 + nh}"], writes=[kR1(3)])
    for kc in range(DC):
        p.op("vector", lambda v, kc=kc: v.scalar_tensor_tensor(
                rzt(kc), rzt(kc), rmsws[:, kc:kc + 1], r2b7[:, :], ALU.mult, ALU.mult),
             reads=[kRZ(kc), "rmsws", kR1(3)], writes=[kRZ(kc)])
    for oc in range(4):
        for kcg in range(2):
            par = kcg % 2
            wsl_ = WS[:, par * 4096:(par + 1) * 4096]
            p.dma("gpsimd", lambda g, oc=oc, kcg=kcg, wsl_=wsl_: g.dma_start(
                wsl_.rearrange("p (kc o) -> p kc o", o=512),
                w_out[oc, kcg * 8:(kcg + 1) * 8].rearrange("kc k o -> k kc o")),
                writes=[kWS(par * 2), kWS(par * 2 + 1)], sem=f"ws{par}")
            def f_mm(t, oc=oc, kcg=kcg, wsl_=wsl_):
                for kci in range(8):
                    kc = kcg * 8 + kci
                    for tt in range(TTI):
                        inst = t.matmul(
                            pb[tt][:, :],
                            RZ[:, kc * T + tt * 128: kc * T + (tt + 1) * 128],
                            wsl_[:, kci * 512:(kci + 1) * 512],
                            start=(kc == 0), stop=(kc == DC - 1))
                return inst
            p.op("tensor", f_mm,
                 reads=[kWS(par * 2), kWS(par * 2 + 1)] + [kRZ(kcg * 8 + i) for i in range(8)],
                 writes=[f"pb.{tt}" for tt in range(TTI)])
        for tt in range(TTI):
            spar = tt % 2
            sbuf_st = R1[:, 1024 + spar * 1024: 1024 + spar * 1024 + 512]
            stk = kR1(1 + spar)
            p.op("vector", lambda v, tt=tt, sbuf_st=sbuf_st: v.tensor_copy(
                    sbuf_st, pb[tt][:, :]),
                 reads=[f"pb.{tt}"], writes=[stk])
            p.dma("sync", lambda s, oc=oc, tt=tt, sbuf_st=sbuf_st: s.dma_start(
                s0_sp.ap()[tt * 128:(tt + 1) * 128, oc * 512:(oc + 1) * 512], sbuf_st),
                reads=[stk], writes=[f"s0o.{oc}.{tt}"], sem=f"s0sp{spar}")
    def h_tile(ic):
        if ic < 32:  return r1t(ic), kR1(ic)
        if ic < 48:  return rzt(ic - 32), kRZ(ic - 32)
        return rbt(ic - 48), kRB(ic - 48)
    for i in range(64):
        par = i % 2
        wg = WS[:, par * 2048:(par + 1) * 2048]
        wu = WS[:, 4096 + par * 2048: 4096 + (par + 1) * 2048]
        p.dma("gpsimd", lambda g, i=i, wg=wg: g.dma_start(
            wg.rearrange("p (kc o) -> p kc o", o=128),
            w_gu[i].rearrange("kc k o -> k kc o")),
            writes=[kWS(par)], sem=f"ws{par}")
        p.dma("gpsimd", lambda g, i=i, wu=wu: g.dma_start(
            wu.rearrange("p (kc o) -> p kc o", o=128),
            w_gu[64 + i].rearrange("kc k o -> k kc o")),
            writes=[kWS(2 + par)], sem=f"wu{par}")
        pg = [pb[par * 4], pb[par * 4 + 1]]
        pu = [pb[par * 4 + 2], pb[par * 4 + 3]]
        pgk = [f"pb.{par * 4}", f"pb.{par * 4 + 1}"]
        puk = [f"pb.{par * 4 + 2}", f"pb.{par * 4 + 3}"]
        def f_gu(t, wg=wg, wu=wu, pg=pg, pu=pu):
            for kc in range(DC):
                for nh in range(2):
                    rhs = RA[:, kc * T + nh * 512: kc * T + (nh + 1) * 512]
                    t.matmul(pg[nh][:, :], wg[:, kc * 128:(kc + 1) * 128], rhs,
                             start=(kc == 0), stop=(kc == DC - 1))
                    inst = t.matmul(pu[nh][:, :], wu[:, kc * 128:(kc + 1) * 128], rhs,
                                    start=(kc == 0), stop=(kc == DC - 1))
            return inst
        p.op("tensor", f_gu, reads=[kWS(par), kWS(2 + par)] + xn0_keys, writes=pgk + puk)
        hdst, hkey = h_tile(i)
        for nh in range(2):
            p.op("scalar", lambda s, nh=nh, pg=pg: s.activation(
                    SC[:, nh * 512:(nh + 1) * 512], pg[nh][:, :], AF.Silu),
                 reads=[pgk[nh]], writes=["SC.0"])
            p.op("vector", lambda v, nh=nh, hdst=hdst, pu=pu: v.tensor_mul(
                    hdst[:, nh * 512:(nh + 1) * 512],
                    SC[:, nh * 512:(nh + 1) * 512], pu[nh][:, :]),
                 reads=["SC.0", puk[nh]], writes=[hkey])
    hkeys_all = [h_tile(i)[1] for i in range(64)]
    for oc in range(4):
        for icg in range(8):
            par = icg % 2
            wsl_ = WS[:, par * 4096:(par + 1) * 4096]
            p.dma("gpsimd", lambda g, oc=oc, icg=icg, wsl_=wsl_: g.dma_start(
                wsl_.rearrange("p (ic o) -> p ic o", o=512),
                w_d[oc, icg * 8:(icg + 1) * 8].rearrange("ic k o -> k ic o")),
                writes=[kWS(par * 2), kWS(par * 2 + 1)], sem=f"ws{par}")
            def f_dn(t, oc=oc, icg=icg, wsl_=wsl_):
                for ici in range(8):
                    ic = icg * 8 + ici
                    hsrc, _ = h_tile(ic)
                    for tt in range(TTI):
                        inst = t.matmul(
                            pb[tt][:, :], hsrc[:, tt * 128:(tt + 1) * 128],
                            wsl_[:, ici * 512:(ici + 1) * 512],
                            start=(ic == 0), stop=(ic == 63))
                return inst
            p.op("tensor", f_dn,
                 reads=[kWS(par * 2), kWS(par * 2 + 1)] + [h_tile(icg * 8 + k)[1] for k in range(8)],
                 writes=[f"pb.{tt}" for tt in range(TTI)])
        for tt in range(TTI):
            spar = tt % 2
            sbuf_st = SCb[:, spar * 2048: spar * 2048 + 512]
            stk = f"SC.{spar}"
            p.op("scalar", lambda s, tt=tt, sbuf_st=sbuf_st: s.activation(
                    sbuf_st, pb[tt][:, :], AF.Silu),
                 reads=[f"pb.{tt}"], writes=[stk])
            p.dma("sync", lambda s, oc=oc, tt=tt, sbuf_st=sbuf_st: s.dma_start(
                g0_sp.ap()[tt * 128:(tt + 1) * 128, oc * 512:(oc + 1) * 512], sbuf_st),
                reads=[stk], writes=[f"g0o.{oc}.{tt}"], sem=f"g0sp{spar}")

    for tt in range(TTI):
        slot = tt % 2
        xsl = SC[:, 0:2048]
        kx0, kx1 = "SC.0", "SC.1"
        s0b = R1[:, 4096 + slot * 2048: 4096 + (slot + 1) * 2048]
        ks0 = [kR1(4 + slot * 2), kR1(5 + slot * 2)]
        lnout = R1[:, slot * 2048:(slot + 1) * 2048]
        lnkeys = [kR1(slot * 2), kR1(slot * 2 + 1)]
        p.dma("gpsimd", lambda g, tt=tt, xsl=xsl: g.dma_start(
            xsl, xh[tt * 128:(tt + 1) * 128, :]),
            writes=[kx0, kx1], sem="xl0")
        p.dma("gpsimd", lambda g, tt=tt, s0b=s0b: g.dma_start(
            s0b, s0_sp.ap()[tt * 128:(tt + 1) * 128, :]),
            reads=[f"s0o.{oc}.{q}" for oc in range(4) for q in range(TTI)],
            writes=ks0, sem=f"s0l{slot}")
        p.dma("gpsimd", lambda g, tt=tt: g.dma_start(
            PR[:, :], g0_sp.ap()[tt * 128:(tt + 1) * 128, :]),
            reads=[f"g0o.{oc}.{q}" for oc in range(4) for q in range(TTI)],
            writes=["PR.0", "PR.1"], sem="g0l")
        for h_ in range(2):
            p.op("vector", lambda v, h_=h_, s0b=s0b: v.tensor_tensor(
                    sq[:, :], s0b[:, h_ * 1024:(h_ + 1) * 1024],
                    PR[:, h_ * 1024:(h_ + 1) * 1024], ALU.mult),
                 reads=[f"PR.{h_}", "sq.0", "sq.1"] + ks0, writes=["sq.0", "sq.1"])
            p.op("vector", lambda v, h_=h_, xsl=xsl: v.tensor_add(
                    xsl[:, h_ * 1024:(h_ + 1) * 1024],
                    xsl[:, h_ * 1024:(h_ + 1) * 1024], sq[:, :]),
                 reads=["sq.0", "sq.1", kx0, kx1], writes=[kx0, kx1])
        p.dma("sync", lambda s, tt=tt, xsl=xsl: s.dma_start(
            x1_sp.ap()[tt * 128:(tt + 1) * 128, :], xsl),
            reads=[kx0, kx1], writes=[f"x1o.{tt}"], sem="x1sp")
        emit_ln_tm(p, xsl, [kx0, kx1], lnout, lnkeys[0], stats, tt, sq,
                   eps_col=consts[:, 0:1], pref="ln1", sqk=("sq.0", "sq.1"))
        p.wr[lnkeys[1]] = list(p.wr.get(lnkeys[0], ()))
        for half in range(2):
            bank = pb[half].bitcast(BF16)
            bkey = f"pb.{half}"
            def f_tr(t, tt=tt, half=half, bank=bank, lnout=lnout):
                insts = []
                for j in range(8):
                    kc = half * 8 + j
                    insts.append(t.transpose(
                        bank[:, j * 128:(j + 1) * 128],
                        lnout[:, kc * 128:(kc + 1) * 128], identb[:, :]))
                return insts
            p.op("tensor", f_tr, reads=lnkeys + ["identb"], writes=[bkey])
            def f_ev(v, tt=tt, half=half, bank=bank):
                dst = RA[:, :].rearrange("p (kc t) -> p kc t", t=T)[
                    :, half * 8:(half + 1) * 8, tt * 128:(tt + 1) * 128]
                return v.tensor_copy(dst, bank[:, 0:1024].rearrange("p (kc t) -> p kc t", t=128))
            p.op("vector", f_ev, reads=[bkey],
                 writes=[kRA(half * 8 + j) for j in range(8)])
    xn1_keys = [kRA(j) for j in range(DC)]

    p.dma("sync", lambda s: s.dma_start(
        xn_in.ap().rearrange("(kc p) t -> p kc t", p=128),
        RA[:, :].rearrange("p (kc t) -> p kc t", t=T)),
        reads=xn1_keys, writes=["xn_in"], sem="xni")
    p.coll(lambda g: g.collective_compute(
        "AllGather", ALU.bypass, replica_groups=RG,
        ins=[xn_in.ap().opt()], outs=[xn_out.ap().opt()]),
        reads=["xn_in"], writes=["xn_out"], sem="cc3")
    for r_ in range(2):
        p.dma("gpsimd", lambda g, r_=r_: g.dma_start(
            R1[:, :].rearrange("p (kc rt) -> p kc rt", rt=2048)[:, :, r_ * T:(r_ + 1) * T],
            xn_out.ap()[r_ * D:(r_ + 1) * D, :].rearrange("(kc p) t -> p kc t", p=128)),
            reads=["xn_out"], writes=[kR1(j) for j in range(32)], sem="xno")
    xfm_keys = [kR1(j) for j in range(32)]

    for kc in range(DC):
        p.op("scalar", lambda s, kc=kc: s.activation(sq[:, 0:1024], rat(kc), AF.Square),
             reads=[kRA(kc)], writes=["sq.0", "sq.1"])
        def f_st(t, kc=kc):
            for nh in range(2):
                t.matmul(pb[2 + nh][0:1, :], ones128b[:, :],
                         RA[:, kc * T + nh * 512: kc * T + (nh + 1) * 512],
                         start=(kc == 0), stop=(kc == DC - 1))
                inst = t.matmul(pb[4 + nh][0:1, :], ones128b[:, :],
                                sq[:, nh * 512:(nh + 1) * 512],
                                start=(kc == 0), stop=(kc == DC - 1))
            return inst
        p.op("tensor", f_st, reads=[kRA(kc), "sq.0", "sq.1", "ones128b"],
             writes=["pb.2", "pb.3", "pb.4", "pb.5"])
    for nh in range(2):
        p.op("vector", lambda v, nh=nh: v.tensor_copy(rrow[:, nh * 512:(nh + 1) * 512], pb[2 + nh][0:1, :]),
             reads=[f"pb.{2 + nh}"], writes=["ROWA"])
        p.op("vector", lambda v, nh=nh: v.tensor_copy(rrow[:, T + nh * 512: T + (nh + 1) * 512], pb[4 + nh][0:1, :]),
             reads=[f"pb.{4 + nh}"], writes=["ROWA"])
    p.op("vector", lambda v: v.tensor_scalar_mul(rrow[:, :], rrow[:, :], 1.0 / D),
         reads=["ROWA"], writes=["ROWA"])
    p.op("vector", lambda v: v.tensor_mul(r2row[:, :], rrow[:, 0:T], rrow[:, 0:T]),
         reads=["ROWA"], writes=["ROWA"])
    p.op("vector", lambda v: v.tensor_sub(r2row[:, :], r2row[:, :], rrow[:, T:2 * T]),
         reads=["ROWA", "ROWA"], writes=["ROWA"])
    p.op("scalar", lambda s: s.activation(r2row[:, :], r2row[:, :], AF.Sqrt,
                                          bias=consts[0:1, 0:1], scale=-1.0),
         reads=["ROWA", "consts"], writes=["ROWA"])
    p.op("vector", lambda v: v.reciprocal(r2row[:, :], r2row[:, :]),
         reads=["ROWA"], writes=["ROWA"])
    p.op("vector", lambda v: v.tensor_mul(mrrow[:, :], rrow[:, 0:T], r2row[:, :]),
         reads=["ROWA", "ROWA"], writes=["ROWA"])
    ROWB11 = ROWA.bitcast(BF16)
    p.op("scalar", lambda s: s.activation(ROWB11[0:1, 4096:4096 + T], r2row[0:1, :], AF.Copy),
         reads=["ROWA"], writes=["ROWA"])
    p.op("scalar", lambda s: s.activation(ROWB11[0:1, 4096 + T:4096 + 2 * T], mrrow[0:1, :], AF.Copy),
         reads=["ROWA"], writes=["ROWA"])
    def f_bc2(t):
        for nh in range(2):
            t.matmul(pb[2 + nh][:, :], onesrowb[:, :],
                     ROWB11[0:1, 4096 + nh * 512: 4096 + (nh + 1) * 512],
                     start=True, stop=True)
            inst = t.matmul(pb[4 + nh][:, :], onesrowb[:, :],
                            ROWB11[0:1, 4096 + T + nh * 512: 4096 + T + (nh + 1) * 512],
                            start=True, stop=True)
        return inst
    p.op("tensor", f_bc2, reads=["ROWA", "onesrowb"],
         writes=["pb.2", "pb.3", "pb.4", "pb.5"])
    for nh in range(2):
        p.op("vector", lambda v, nh=nh: v.tensor_copy(r2bb[:, nh * 512:(nh + 1) * 512], pb[2 + nh][:, :]),
             reads=[f"pb.{2 + nh}"], writes=[kRZ(0)])
        p.op("vector", lambda v, nh=nh: v.tensor_copy(mrbb[:, nh * 512:(nh + 1) * 512], pb[4 + nh][:, :]),
             reads=[f"pb.{4 + nh}"], writes=[kRZ(1)])
    for kc in range(DC):
        p.op("vector", lambda v, kc=kc: v.tensor_mul(sq[:, 0:T], rat(kc), r2bb[:, :]),
             reads=[kRA(kc), kRZ(0), "sq.0", "sq.1"], writes=["sq.0", "sq.1"])
        p.op("vector", lambda v, kc=kc: v.tensor_sub(sq[:, 0:T], sq[:, 0:T], mrbb[:, :]),
             reads=[kRZ(1), "sq.0", "sq.1"], writes=["sq.0", "sq.1"])
        p.op("vector", lambda v, kc=kc: v.tensor_scalar(
                rbt(kc), sq[:, 0:T], lnwbs[:, kc:kc + 1], lnwbs[:, DC + kc: DC + kc + 1],
                ALU.mult, ALU.add),
             reads=["sq.0", "sq.1", "lnwbs"], writes=[kRB(kc)])
    lnx_keys = [kRB(j) for j in range(DC)]
    for oc in range(4):
        for kcg in range(2):
            par = kcg % 2
            wsl_ = WS[:, par * 4096:(par + 1) * 4096]
            p.dma("gpsimd", lambda g, oc=oc, kcg=kcg, wsl_=wsl_: g.dma_start(
                wsl_.rearrange("p (kc o) -> p kc o", o=512),
                w1[oc, kcg * 8:(kcg + 1) * 8].rearrange("kc k o -> k kc o")),
                writes=[kWS(par * 2), kWS(par * 2 + 1)], sem=f"ws{par}")
            p.op("vector", lambda v, wsl_=wsl_: v.reduce_max(
                    amx_t[:, :], wsl_, axis=AX.X, apply_absolute_value=True),
                 reads=[kWS(par * 2), kWS(par * 2 + 1)], writes=["amx_t"])
            p.op("vector", lambda v: v.tensor_max(amaxc[:, :], amaxc[:, :], amx_t[:, :]),
                 reads=["amx_t", "amaxc"], writes=["amaxc"])
            p.op("scalar", lambda s, wsl_=wsl_: s.activation(wsl_, wsl_, AF.Sign),
                 reads=[kWS(par * 2), kWS(par * 2 + 1)],
                 writes=[kWS(par * 2), kWS(par * 2 + 1)])
            def f_mm1(t, oc=oc, kcg=kcg, wsl_=wsl_):
                for kci in range(8):
                    kc = kcg * 8 + kci
                    for tt in range(TTI):
                        inst = t.matmul(
                            pb[tt][:, :],
                            RB[:, kc * T + tt * 128: kc * T + (tt + 1) * 128],
                            wsl_[:, kci * 512:(kci + 1) * 512],
                            start=(kc == 0), stop=(kc == DC - 1))
                return inst
            p.op("tensor", f_mm1,
                 reads=[kWS(par * 2), kWS(par * 2 + 1)] + [kRB(kcg * 8 + i) for i in range(8)],
                 writes=[f"pb.{tt}" for tt in range(TTI)])
        for tt in range(TTI):
            p.op("vector", lambda v, oc=oc, tt=tt: v.tensor_copy(
                    RZ[:, tt * D + oc * 512: tt * D + (oc + 1) * 512], pb[tt][:, :]),
                 reads=[f"pb.{tt}"], writes=[kRZ((tt * D + oc * 512) // T)])
    p.op("tensor", lambda t: t.transpose(pb[0].bitcast(BF16)[0:1, 0:128], amaxc[:, 0:1], identb[:, :]),
         reads=["amaxc", "identb"], writes=["pb.0"])
    p.op("vector", lambda v: v.tensor_copy(amrow[:, :], pb[0].bitcast(BF16)[0:1, 0:128]),
         reads=["pb.0"], writes=["ROWA"])
    p.op("vector", lambda v: v.reduce_max(scal1[:, :], amrow[:, :], axis=AX.X),
         reads=["ROWA"], writes=["ROWA"])
    p.op("scalar", lambda s: s.activation(ROWA.bitcast(BF16)[0:1, 0:1], scal1[:, :], AF.Copy),
         reads=["ROWA"], writes=["ROWA"])
    p.op("tensor", lambda t: t.matmul(pb[1][:, 0:1], onesrowb[:, :],
                                      ROWA.bitcast(BF16)[0:1, 0:1], start=True, stop=True),
         reads=["ROWA", "onesrowb"], writes=["pb.1"])
    p.op("vector", lambda v: v.tensor_copy(scalc[:, :], pb[1][:, 0:1]),
         reads=["pb.1"], writes=["scalc"])
    grepb = RB[:, 12 * T: 14 * T]
    brepb = RB[:, 14 * T: 16 * T]
    p.dma("gpsimd", lambda g: g.dma_start(
        grepb, gamrow[0:1, 0:D].to_broadcast([128, D])),
        writes=[kRB(12), kRB(13)], sem="grep")
    p.dma("gpsimd", lambda g: g.dma_start(
        brepb, gamrow[0:1, D:2 * D].to_broadcast([128, D])),
        writes=[kRB(14), kRB(15)], sem="brep")
    p.op("vector", lambda v: v.tensor_scalar_mul(grepb, grepb, scalc[:, 0:1]),
         reads=[kRB(12), kRB(13), "scalc"], writes=[kRB(12), kRB(13)])
    for tt in range(TTI):
        for oc in range(4):
            spar = (tt * 4 + oc) % 2
            sbuf_st = PR[:, spar * 1024: spar * 1024 + 512]
            stk = f"PR.{spar}"
            p.op("vector", lambda v, tt=tt, oc=oc: v.tensor_tensor(
                    sq[:, 0:512], RZ[:, tt * D + oc * 512: tt * D + (oc + 1) * 512],
                    grepb[:, oc * 512:(oc + 1) * 512], ALU.mult),
                 reads=[kRZ((tt * D + oc * 512) // T), kRB(12 + oc // 2), "sq.0"],
                 writes=["sq.0"])
            p.op("vector", lambda v, oc=oc: v.tensor_add(
                    sq[:, 0:512], sq[:, 0:512], brepb[:, oc * 512:(oc + 1) * 512]),
                 reads=[kRB(14 + oc // 2), "sq.0"], writes=["sq.0"])
            p.op("scalar", lambda s, sbuf_st=sbuf_st: s.activation(sbuf_st, sq[:, 0:512], AF.Silu),
                 reads=["sq.0"], writes=[stk])
            p.dma("sync", lambda s, tt=tt, oc=oc, sbuf_st=sbuf_st: s.dma_start(
                g1_sp.ap()[tt * 128:(tt + 1) * 128, oc * 512:(oc + 1) * 512], sbuf_st),
                reads=[stk], writes=[f"g1o.{oc}.{tt}"], sem=f"g1sp{spar}")

    qbD = RB[:, 0:2048]
    kbD = RB[:, 2048:4096]
    cs2D = RB[:, 4096:8192]
    mask4D = RB[:, 8192:10240]
    attnD = RB[:, 10240:12288]
    p.dma("gpsimd", lambda g: g.dma_start(cs2D, cs2[:, :]),
          writes=[kRB(4), kRB(5), kRB(6), kRB(7)], sem="dcs2")
    p.dma("gpsimd", lambda g: g.dma_start(mask4D, mask4[:, :]),
          writes=[kRB(8), kRB(9)], sem="dmask4")
    kq = [kRB(0), kRB(1)]
    kk = [kRB(2), kRB(3)]
    kcs = [kRB(4), kRB(5), kRB(6), kRB(7)]
    kmk = [kRB(8), kRB(9)]
    kat = [kRB(10), kRB(11)]
    wvA, wvB = WS[:, 0:4096], WS[:, 4096:8192]
    for grp in range(2):
        p.dma("gpsimd", lambda g, grp=grp: g.dma_start(
            wvA.rearrange("p (kc o) -> p kc o", o=256),
            wv_b[grp, :, :, 0:256].rearrange("kc k o -> k kc o")),
            writes=[kWS(0), kWS(1)], sem="wva")
        p.dma("gpsimd", lambda g, grp=grp: g.dma_start(
            wvB.rearrange("p (kc o) -> p kc o", o=256),
            wv_b[grp, :, :, 256:512].rearrange("kc k o -> k kc o")),
            writes=[kWS(2), kWS(3)], sem="wvb")
        for tth in range(2):
            def f_v(t, grp=grp, tth=tth):
                for kc in range(DC):
                    for tt in range(TTI):
                        t.matmul(
                            pb[tt][:, 0:256],
                            R1[:, kc * SEQ + (tth * 8 + tt) * 128: kc * SEQ + (tth * 8 + tt + 1) * 128],
                            wvA[:, kc * 256:(kc + 1) * 256],
                            start=(kc == 0), stop=False)
                        inst = t.matmul(
                            pb[tt][:, 256:512],
                            R1[:, kc * SEQ + (tth * 8 + tt) * 128: kc * SEQ + (tth * 8 + tt + 1) * 128],
                            wvB[:, kc * 256:(kc + 1) * 256],
                            start=False, stop=(kc == DC - 1))
                return inst
            p.op("tensor", f_v, reads=[kWS(0), kWS(1), kWS(2), kWS(3)] + xfm_keys,
                 writes=[f"pb.{tt}" for tt in range(TTI)])
            for tt in range(TTI):
                gt = tth * 8 + tt
                p.op("vector", lambda v, grp=grp, gt=gt, tt=tt: v.tensor_copy(
                        RA[:, (gt * 8 + grp * 4) * 128:(gt * 8 + grp * 4 + 4) * 128], pb[tt][:, :]),
                     reads=[f"pb.{tt}"], writes=[f"v.{grp}.{gt}"])
    for hl in range(8):
        wq_ = WS[:, 0:4096]
        kwq = [kWS(0), kWS(1)]
        p.dma("gpsimd", lambda g, hl=hl, wq_=wq_: g.dma_start(
            wq_.rearrange("p (s kc o) -> p s kc o", kc=DC, o=128),
            wqkv[hl].rearrange("s kc k o -> k s kc o")),
            writes=kwq, sem="wq0")
        ROWB = ROWA.bitcast(BF16)
        RBF = RB.bitcast(F32)
        def qk_res(side, nq):
            pc = nq % 2
            return dict(
                pc=pc, side=side, nq=nq,
                qkb=pb[4] if pc == 0 else pb[0], qkk="pb.4" if pc == 0 else "pb.0",
                swb=pb[7] if pc == 0 else pb[1], swk="pb.7" if pc == 0 else "pb.1",
                stA=pb[5] if pc == 0 else pb[2], stB=pb[6] if pc == 0 else pb[3],
                stAk="pb.5" if pc == 0 else "pb.2", stBk="pb.6" if pc == 0 else "pb.3",
                ra_=SC[:, 0:512] if pc == 0 else RBF[:, 6144:6656],
                rak="SC.0" if pc == 0 else kRB(12),
                cbf_=SC[:, 1024:1536] if pc == 0 else RBF[:, 6656:7168],
                cbk="SC.1" if pc == 0 else kRB(13),
                mbs_=PR[:, pc * 1024: pc * 1024 + 512],
                vr_=PR[:, pc * 1024 + 512: pc * 1024 + 1024],
                prk=f"PR.{pc}", rowo=pc * 512,
                dst=qbD if side == 0 else kbD, dks=kq if side == 0 else kk)
        def E_mm(r):
            def f_qk(t, r=r):
                for kc in range(DC):
                    inst = t.matmul(
                        r["qkb"][:, :],
                        wq_[:, (r["side"] * DC + kc) * 128:(r["side"] * DC + kc + 1) * 128],
                        R1[:, kc * SEQ + r["nq"] * 512: kc * SEQ + (r["nq"] + 1) * 512],
                        start=(kc == 0), stop=(kc == DC - 1))
                return inst
            p.op("tensor", f_qk, reads=kwq + xfm_keys, writes=[r["qkk"]])
        def E_rope(r):
            p.op("vector", lambda v, r=r: v.tensor_mul(
                    sq[:, 0:512], r["qkb"][:, :],
                    cs2D[:, SEQ + r["nq"] * 512: SEQ + (r["nq"] + 1) * 512]),
                 reads=[r["qkk"]] + kcs, writes=["sq.0"])
            p.op("tensor", lambda t, r=r: t.matmul(r["swb"][:, :], swps[:, :], sq[:, 0:512],
                                                   start=True, stop=True),
                 reads=["sq.0", "swps"], writes=[r["swk"]])
            p.op("vector", lambda v, r=r: v.tensor_mul(
                    r["ra_"], r["qkb"][:, :], cs2D[:, r["nq"] * 512:(r["nq"] + 1) * 512]),
                 reads=[r["qkk"]] + kcs, writes=[r["rak"]])
            p.op("vector", lambda v, r=r: v.tensor_sub(
                    r["cbf_"][0:64, :], r["ra_"][0:64, :], r["swb"][0:64, :]),
                 reads=[r["rak"], r["swk"]], writes=[r["cbk"]])
            p.op("vector", lambda v, r=r: v.tensor_add(
                    r["cbf_"][64:128, :], r["ra_"][64:128, :], r["swb"][64:128, :]),
                 reads=[r["rak"], r["swk"]], writes=[r["cbk"]])
        def E_stat(r):
            p.op("scalar", lambda s, r=r: s.activation(sq[:, 0:512], r["cbf_"], AF.Copy),
                 reads=[r["cbk"]], writes=["sq.0"])
            p.op("scalar", lambda s, r=r: s.activation(sq[:, 512:1024], r["cbf_"], AF.Square),
                 reads=[r["cbk"]], writes=["sq.1"])
            def f_st3(t, r=r):
                t.matmul(r["stA"][0:1, :], ones128b[:, :], sq[:, 0:512], start=True, stop=True)
                return t.matmul(r["stB"][0:1, :], ones128b[:, :], sq[:, 512:1024], start=True, stop=True)
            p.op("tensor", f_st3, reads=["sq.0", "sq.1", "ones128b"], writes=[r["stAk"], r["stBk"]])
            p.op("scalar", lambda s, r=r: s.activation(
                    ROWB[0:1, r["rowo"]:r["rowo"] + 512], r["stA"][0:1, :], AF.Copy),
                 reads=[r["stAk"]], writes=["ROWA"])
            p.op("scalar", lambda s, r=r: s.activation(
                    ROWB[0:1, 1024 + r["rowo"]:1024 + r["rowo"] + 512], r["stB"][0:1, :], AF.Copy),
                 reads=[r["stBk"]], writes=["ROWA"])
            def f_bc3(t, r=r):
                t.matmul(r["stA"][:, :], invrowb[:, :], ROWB[0:1, r["rowo"]:r["rowo"] + 512],
                         start=True, stop=True)
                return t.matmul(r["stB"][:, :], invrowb[:, :],
                                ROWB[0:1, 1024 + r["rowo"]:1024 + r["rowo"] + 512],
                                start=True, stop=True)
            p.op("tensor", f_bc3, reads=["ROWA", "invrowb"], writes=[r["stAk"], r["stBk"]])
        def E_fin(r):
            p.op("scalar", lambda s, r=r: s.activation(r["mbs_"], r["stA"][:, :], AF.Copy),
                 reads=[r["stAk"]], writes=[r["prk"]])
            p.op("gpsimd", lambda g, r=r: g.tensor_sub(r["cbf_"], r["cbf_"], r["mbs_"]),
                 reads=[r["cbk"], r["prk"]], writes=[r["cbk"]])
            p.op("scalar", lambda s, r=r: s.activation(r["vr_"], r["mbs_"], AF.Square),
                 reads=[r["prk"]], writes=[r["prk"]])
            p.op("vector", lambda v, r=r: v.tensor_sub(r["vr_"], r["vr_"], r["stB"][:, :]),
                 reads=[r["prk"], r["stBk"]], writes=[r["prk"]])
            p.op("scalar", lambda s, r=r: s.activation(r["vr_"], r["vr_"], AF.Ln,
                                                       bias=consts[:, 0:1], scale=-1.0),
                 reads=[r["prk"], "consts"], writes=[r["prk"]])
            p.op("scalar", lambda s, r=r: s.activation(r["vr_"], r["vr_"], AF.Exp, scale=-0.5),
                 reads=[r["prk"]], writes=[r["prk"]])
            p.op("vector", lambda v, r=r: v.tensor_mul(r["cbf_"], r["cbf_"], r["vr_"]),
                 reads=[r["cbk"], r["prk"]], writes=[r["cbk"]])
            p.op("vector", lambda v, r=r: v.tensor_scalar(
                    r["dst"][:, r["nq"] * 512:(r["nq"] + 1) * 512], r["cbf_"],
                    lnss[:, 0:1], lnss[:, 1:2], ALU.mult, ALU.add),
                 reads=[r["cbk"], "lnss"], writes=[r["dks"][r["nq"] // 2]])
        chunks = [qk_res(side, nq) for side in range(2) for nq in range(4)]
        E_mm(chunks[0]); E_rope(chunks[0]); E_stat(chunks[0])
        for ci in range(8):
            if ci + 1 < 8:
                E_mm(chunks[ci + 1])
                E_rope(chunks[ci + 1])
            E_fin(chunks[ci])
            if ci + 1 < 8:
                E_stat(chunks[ci + 1])
        for qs in range(4):
            nk = qs * 4 + 4
            def emit_S(kt, qs=qs):
                spar = kt % 2
                p.op("tensor", lambda t, qs=qs, kt=kt, spar=spar: t.matmul(
                        pb[spar][:, :], kbD[:, kt * 128:(kt + 1) * 128],
                        qbD[:, qs * 512:(qs + 1) * 512], start=True, stop=True),
                     reads=[kk[kt // 8], kq[qs // 2]], writes=[f"pb.{spar}"])
            def emit_pow(kt, qs=qs):
                spar = kt % 2
                sa_ = SC[:, 0:512] if spar == 0 else SC[:, 1536:2048]
                sak = "SC.0" if spar == 0 else "SC.1"
                a4_ = sq[:, 0:512] if spar == 0 else sq[:, 512:1024]
                a4k = f"sq.{spar}"
                p.op("scalar", lambda s, spar=spar, sa_=sa_: s.activation(sa_, pb[spar][:, :], AF.Square),
                     reads=[f"pb.{spar}"], writes=[sak])
                if kt >= qs * 4:
                    off = kt - qs * 4
                    p.op("gpsimd", lambda g, off=off, sa_=sa_: g.tensor_mul(
                            sa_, sa_, mask4D[:, off * 512:(off + 1) * 512]),
                         reads=[sak] + kmk, writes=[sak])
                p.op("gpsimd", lambda g, sa_=sa_, a4_=a4_: g.tensor_mul(a4_, sa_, sa_),
                     reads=[sak], writes=[a4k])
            def emit_acc(kt, qs=qs, hl=hl):
                spar = kt % 2
                a4_ = sq[:, 0:512] if spar == 0 else sq[:, 512:1024]
                a4k = f"sq.{spar}"
                def f_acc(t, hl=hl, qs=qs, kt=kt, a4_=a4_):
                    t.matmul(pb[2][0:1, :], ones128b[:, :], a4_,
                             start=(kt == 0), stop=(kt == qs * 4 + 3))
                    return t.matmul(pb[3][:, :], RA[:, (kt * 8 + hl) * 128:(kt * 8 + hl + 1) * 128],
                                    a4_, start=(kt == 0), stop=(kt == qs * 4 + 3))
                p.op("tensor", f_acc, reads=[a4k, "ones128b", f"v.{hl // 4}.{kt}"],
                     writes=["pb.2", "pb.3"])
            emit_S(0)
            emit_pow(0)
            for kt in range(nk):
                if kt + 1 < nk:
                    emit_S(kt + 1)
                emit_acc(kt)
                if kt + 1 < nk:
                    emit_pow(kt + 1)
            p.op("scalar", lambda s: s.activation(ROWA.bitcast(BF16)[0:1, 0:512], pb[2][0:1, :], AF.Copy),
                 reads=["pb.2"], writes=["ROWA"])
            p.op("tensor", lambda t: t.matmul(pb[4][:, :], onesrowb[:, :],
                                              ROWA.bitcast(BF16)[0:1, 0:512], start=True, stop=True),
                 reads=["ROWA", "onesrowb"], writes=["pb.4"])
            p.op("scalar", lambda s: s.activation(SC[:, 0:512], pb[4][:, :], AF.Ln),
                 reads=["pb.4"], writes=["SC.0"])
            p.op("scalar", lambda s: s.activation(SC[:, 0:512], SC[:, 0:512], AF.Exp, scale=-1.0),
                 reads=["SC.0"], writes=["SC.0"])
            p.op("vector", lambda v, qs=qs: v.tensor_mul(
                    attnD[:, qs * 512:(qs + 1) * 512], pb[3][:, :], SC[:, 0:512]),
                 reads=["pb.3", "SC.0"], writes=[kat[qs // 2]])
        p.dma("sync", lambda s, hl=hl: s.dma_start(
            at_in.ap()[hl * 128:(hl + 1) * 128, :], attnD),
            reads=kat, writes=[f"ato.{hl}"], sem="ati")

    p.coll(lambda g: g.collective_compute(
        "AllGather", ALU.bypass, replica_groups=RG,
        ins=[at_in.ap().opt()], outs=[at_out.ap().opt()]),
        reads=[f"ato.{hl}" for hl in range(8)], writes=["at_out"], sem="cc4")
    def f_ab(g):
        val = g.value_load(hhoffs[:1, :1])
        return g.dma_start(
            RZ[:, :].rearrange("p (kc t) -> p kc t", t=T),
            at_out.ap().rearrange("(kc p) t -> p kc t", p=128)[:, :, bass.ds(val, T)])
    p.dma("gpsimd", f_ab, reads=["at_out", "hhoffs"],
          writes=[kRZ(j) for j in range(DC)], sem="abl")
    abuf_keys = [kRZ(j) for j in range(DC)]

    for oc in range(4):
        gpar = oc % 2

        for kcg in range(2):
            par = kcg % 2
            wsl_ = WS[:, par * 4096:(par + 1) * 4096]
            p.dma("gpsimd", lambda g, oc=oc, kcg=kcg, wsl_=wsl_: g.dma_start(
                wsl_.rearrange("p (kc o) -> p kc o", o=512),
                w_out1[oc, kcg * 8:(kcg + 1) * 8].rearrange("kc k o -> k kc o")),
                writes=[kWS(par * 2), kWS(par * 2 + 1)], sem=f"ws{par}")
            def f_mm2(t, oc=oc, kcg=kcg, wsl_=wsl_):
                for kci in range(8):
                    kc = kcg * 8 + kci
                    for tt in range(TTI):
                        inst = t.matmul(
                            pb[tt][:, :],
                            RZ[:, kc * T + tt * 128: kc * T + (tt + 1) * 128],
                            wsl_[:, kci * 512:(kci + 1) * 512],
                            start=(kc == 0), stop=(kc == DC - 1))
                return inst
            p.op("tensor", f_mm2,
                 reads=[kWS(par * 2), kWS(par * 2 + 1)] + [kRZ(kcg * 8 + i) for i in range(8)],
                 writes=[f"pb.{tt}" for tt in range(TTI)])
        for tt in range(TTI):
            slot = tt % 2
            pslot = (tt // 2) % 2
            xsl = SC[:, slot * 1024: slot * 1024 + 1024]
            kx0 = f"SC.{slot}"
            if tt % 2 == 0:
                th = tt // 2
                p.dma("gpsimd", lambda g, oc=oc, th=th, pslot=pslot: g.dma_start(
                    PR[:, pslot * 1024:(pslot + 1) * 1024].rearrange("p (q o) -> p q o", o=512),
                    g1_sp.ap().rearrange("(q p) d -> p q d", p=128)[:, th * 2:(th + 1) * 2, oc * 512:(oc + 1) * 512]),
                    reads=[f"g1o.{o2}.{q}" for o2 in range(4) for q in range(TTI)],
                    writes=[f"PR.{pslot}"], sem=f"g1l{pslot}")
            p.dma("gpsimd", lambda g, oc=oc, tt=tt, xsl=xsl: g.dma_start(
                xsl[:, 0:512], x1_sp.ap()[tt * 128:(tt + 1) * 128, oc * 512:(oc + 1) * 512]),
                reads=[f"x1o.{q}" for q in range(TTI)], writes=[kx0], sem=f"x1l{slot}")
            p.op("vector", lambda v, tt=tt, xsl=xsl, pslot=pslot: v.tensor_mul(
                    xsl[:, 512:1024], pb[tt][:, :],
                    PR[:, pslot * 1024 + (tt % 2) * 512: pslot * 1024 + (tt % 2 + 1) * 512]),
                 reads=[f"pb.{tt}", f"PR.{pslot}", kx0], writes=[kx0])
            p.op("vector", lambda v, xsl=xsl: v.tensor_add(
                    xsl[:, 0:512], xsl[:, 0:512], xsl[:, 512:1024]),
                 reads=[kx0], writes=[kx0])
            p.dma("sync", lambda s, oc=oc, tt=tt, xsl=xsl: s.dma_start(
                x2[tt * 128:(tt + 1) * 128, oc * 512:(oc + 1) * 512], xsl[:, 0:512]),
                reads=[kx0], writes=[f"x2o.{oc}.{tt}"], sem=f"x2o{slot}")

    p.barrier_all("sync")
    p.finalize()
    ctx.close()
    return nc


def prep_fused(inputs):
    hp = {}
    hp["ident"] = np.eye(128, dtype=np.float32)
    hp["w_in"] = tile_w_out_in(np.ascontiguousarray(np.asarray(inputs["L0_attn_W_in"], np.float32)[:8192]))
    hp["w_gu"] = tile_w_out_in(np.asarray(inputs["L0_ffn_w_gate_up"], np.float32))
    hp["w_d"] = tile_w_out_in_wide(np.asarray(inputs["L0_ffn_w_down"], np.float32), 512)
    hp["w_out"] = tile_w_out_in_wide(np.asarray(inputs["L0_attn_W_out"], np.float32), 512)
    hp["w1"] = tile_w_out_in_wide(np.asarray(inputs["L1_ffn_weight"], np.float32), 512)
    hp["w_out1"] = tile_w_in_out_wide(np.asarray(inputs["L1_attn_w_out"], np.float32), 512)
    conv_w = np.asarray(inputs["L0_attn_conv_w"], np.float32)
    conv_b = np.asarray(inputs["L0_attn_conv_b"], np.float32)
    hp["cw"] = np.ascontiguousarray(conv_w[:, 0, :].reshape(48, 128, 3).transpose(1, 0, 2).reshape(128, 144))
    hp["cb"] = np.ascontiguousarray(conv_b.reshape(48, 128).T)
    A_log_rep = np.repeat(np.asarray(inputs["L0_attn_A_log"], np.float32), 64)
    hp["alog"] = np.ascontiguousarray(A_log_rep.reshape(NT, 128).T)
    hp["mask_ssd"] = np.vstack([np.triu(np.ones((64, 64), np.float32))] * 2)
    hp["rmsw"] = np.ascontiguousarray(np.asarray(inputs["L0_attn_rms_w"], np.float32).reshape(DC, 128).T)
    lnw = np.asarray(inputs["L1_ffn_ln_weight"], np.float32)
    lnb = np.asarray(inputs["L1_ffn_ln_bias"], np.float32)
    hp["lnwb"] = np.ascontiguousarray(np.concatenate([lnw.reshape(DC, 128).T, lnb.reshape(DC, 128).T], 1))
    hp["gamrow"] = np.concatenate([np.asarray(inputs["L1_ffn_gamma"], np.float32),
                                   np.asarray(inputs["L1_ffn_beta"], np.float32)])[None, :]
    hp["lns"] = np.ascontiguousarray(np.stack([np.asarray(inputs["L1_attn_ln_scale"], np.float32),
                                               np.asarray(inputs["L1_attn_ln_bias"], np.float32)], 1))
    inv = 1.0 / 10000.0 ** (np.arange(0, 128, 2) / 128)
    tg = np.arange(SEQ)[:, None] * inv[None, :]
    hp["cs2"] = np.ascontiguousarray(np.hstack([np.vstack([np.cos(tg).T] * 2),
                                                np.vstack([np.sin(tg).T] * 2)]).astype(np.float32))
    mask4 = np.zeros((128, 4 * 512), np.float32)
    for off in range(4):
        for pz in range(128):
            mask4[pz, off * 512 + off * 128 + pz:(off + 1) * 512] = 1.0
    hp["mask4"] = mask4
    hp["swp"] = np.roll(np.eye(128, dtype=np.float32), 64, axis=1)
    hp["cidx"] = np.ascontiguousarray(np.tile(np.arange(NCHL, dtype=np.float32), (128, 1)))
    w_qkv = np.asarray(inputs["L1_attn_w_qkv"], np.float32)
    tiles = np.ascontiguousarray(w_qkv.reshape(DC, 128, 48, 128).transpose(2, 0, 1, 3))
    hp["wqkv_h"] = []
    hp["wv_h"] = []
    for hh in range(2):
        wsel = np.stack([np.stack([tiles[hh * 8 + hl], tiles[16 + hh * 8 + hl]])
                         for hl in range(8)])
        hp["wqkv_h"].append(np.ascontiguousarray(wsel))
        wv = np.stack([np.concatenate([tiles[32 + hh * 8 + grp * 4 + l] for l in range(4)], axis=-1)
                       for grp in range(2)])
        hp["wv_h"].append(np.ascontiguousarray(wv))
    return hp


def fused_in_maps(inputs, hp):
    x = np.asarray(inputs["L0_attn_u"], np.float32)
    im = []
    for core in range(NCORES):
        b, h = core // 2, core % 2
        bm = np.zeros((128, 4), np.float32)
        bm[:, 0] = 1.0 if h == 1 else 0.0
        bm[:, 1] = 1.0 if h == 0 else 0.0
        im.append({
            "xh": np.ascontiguousarray(x[b, h * 1024:(h + 1) * 1024]),
            "w_in": hp["w_in"], "w_gu": hp["w_gu"], "w_d": hp["w_d"],
            "w_out": hp["w_out"], "w1": hp["w1"], "wqkv": hp["wqkv_h"][h],
            "w_out1": hp["w_out1"], "cw": hp["cw"], "cb": hp["cb"],
            "wv_b": hp["wv_h"][h],
            "alog": hp["alog"], "ident": hp["ident"], "mask_ssd": hp["mask_ssd"],
            "rmsw": hp["rmsw"], "lnwb": hp["lnwb"], "gamrow": hp["gamrow"],
            "lns": hp["lns"], "cs2": hp["cs2"], "mask4": hp["mask4"],
            "swp": hp["swp"], "cidx": hp["cidx"], "bmask": bm,
            "hhoff": np.array([[h * 1024]], np.int32),
        })
    return im


def kernel_fused(**inputs):
    hp = prep_fused(inputs)
    im = fused_in_maps(inputs, hp)
    nc = build_fused()
    res = _run(nc, im)
    out = np.zeros((BATCH, SEQ, D), dtype=np.float32)
    for core in range(NCORES):
        b, h = core // 2, core % 2
        out[b, h * 1024:(h + 1) * 1024] = res[core]["x2"]
    return out


# version 12
# speedup vs baseline: 1.1176x; 1.1176x over previous
import numpy as np
import ml_dtypes
from contextlib import ExitStack

import concourse.bass as bass
import concourse.mybir as mybir
from concourse.bass_utils import run_bass_kernel_spmd

F32 = mybir.dt.float32
BF16 = mybir.dt.bfloat16
AF = mybir.ActivationFunctionType
ALU = mybir.AluOpType
AX = mybir.AxisListType

NCORES = 8
D = 2048
BATCH = 4
SEQ = 2048
T = 1024
DC = D // 128
TTI = T // 128


class Prog:
    ENGS = ("tensor", "vector", "scalar", "gpsimd", "sync")
    DMA_ENGS = ("gpsimd", "sync", "scalar")

    STRICT_SAME_ENGINE = True

    def __init__(self, nc, ctx):
        self.nc = nc
        self.ctx = ctx
        self.qsem = {e: ctx.enter_context(nc.semaphore(f"q_{e}")) for e in self.ENGS}
        self.qc = {e: 0 for e in self.ENGS}
        self.dsem = {}
        self.queue = {e: [] for e in self.ENGS}
        self.wr = {}
        self.rd = {}
        self.clock = {e: {} for e in self.ENGS}

    def _semh(self, kk):
        return self.qsem[kk[1]] if kk[0] == "q" else self.dsem[kk[1]][0]

    def _needs(self, eng, dma, reads, writes, extra):
        deps = []
        for k in reads:
            for ev in self.wr.get(k, ()):
                deps.append((ev, False))
        for k in writes:
            for ev in self.wr.get(k, ()):
                deps.append((ev, True))
            for rev in self.rd.get(k, {}).values():
                deps.append((rev, True))
        for ev in extra:
            deps.append((ev, False))
        waits = {}
        myclock = self.clock[eng]
        for (kk, cnt, evclock), order_only in deps:
            if (order_only and kk == ("q", eng) and not dma
                    and not self.STRICT_SAME_ENGINE):
                continue
            if myclock.get(kk, 0) >= cnt:
                continue
            waits[kk] = max(waits.get(kk, 0), cnt)
            for ck, cv in evclock.items():
                if myclock.get(ck, 0) < cv:
                    myclock[ck] = cv
            myclock[kk] = max(myclock.get(kk, 0), cnt)
        return waits

    def _record(self, eng, kk, cnt, reads, writes):
        evclock = dict(self.clock[eng])
        evclock[kk] = cnt
        ev = (kk, cnt, evclock)
        for k in writes:
            evs = [e for e in self.wr.get(k, ()) if e[0] != kk]
            evs.append(ev)
            self.wr[k] = evs
            self.rd[k] = {}
        for k in reads:
            self.rd.setdefault(k, {})[kk] = ev
        return ev

    def op(self, eng, fn, reads=(), writes=(), extra=()):
        waits = self._needs(eng, False, reads, writes, extra)
        self.qc[eng] += 1
        ev = self._record(eng, ("q", eng), self.qc[eng], reads, writes)
        self.queue[eng].append((waits, fn, ("q", eng), 1))
        return ev

    def dma(self, eng, fn, reads=(), writes=(), extra=(), n=1, sem=None):
        assert sem is not None, "dma ops need an explicit sem stream name"
        waits = self._needs(eng, True, reads, writes, extra)
        if sem not in self.dsem:
            self.dsem[sem] = [self.ctx.enter_context(self.nc.semaphore(f"dma_{sem}")), 0]
        self.dsem[sem][1] += 16 * n
        ev = self._record(eng, ("dma", sem), self.dsem[sem][1], reads, writes)
        self.queue[eng].append((waits, fn, ("dma", sem), n))
        return ev

    def coll(self, fn, reads=(), writes=(), extra=(), sem=None):
        assert sem is not None
        eng = "gpsimd"
        waits = self._needs(eng, True, reads, writes, extra)
        if sem not in self.dsem:
            self.dsem[sem] = [self.ctx.enter_context(self.nc.semaphore(f"dma_{sem}")), 0]
        self.dsem[sem][1] += 1
        ev = self._record(eng, ("dma", sem), self.dsem[sem][1], reads, writes)
        self.queue[eng].append((waits, fn, ("dmac", sem), 1))
        return ev

    def barrier_all(self, eng="sync"):
        waits = {}
        for e in self.ENGS:
            if self.qc[e] and e != eng:
                waits[("q", e)] = self.qc[e]
        for name, (h, cnt) in self.dsem.items():
            waits[("dma", name)] = cnt
        self.queue[eng].append((waits, None, ("q", eng), 0))

    def finalize(self):
        nc = self.nc
        with nc.Block() as block:
            for ename in self.ENGS:
                if not self.queue[ename]:
                    continue
                def make_body(ename):
                    def body(eng):
                        for waits, fn, kk, n in self.queue[ename]:
                            for wkk, v in sorted(waits.items()):
                                eng.wait_ge(self._semh(wkk), v)
                            if fn is None:
                                continue
                            insts = fn(eng)
                            if kk[0] == "dmac":
                                insts.then_inc(self.dsem[kk[1]][0], 1)
                            elif kk[0] == "dma":
                                if not isinstance(insts, (list, tuple)):
                                    insts = [insts]
                                assert len(insts) == n, f"{ename}: dma op declared n={n} got {len(insts)}"
                                for inst in insts:
                                    inst.then_inc(self.dsem[kk[1]][0], 16)
                            else:
                                if isinstance(insts, (list, tuple)):
                                    insts = insts[-1]
                                insts.then_inc(self.qsem[ename], 1)
                    return body
                getattr(block, ename)(make_body(ename))


def tile_w_out_in(w):
    O, K = w.shape
    return np.ascontiguousarray(
        w.reshape(O // 128, 128, K // 128, 128).transpose(0, 2, 3, 1))


def tile_w_in_out(w):
    K, O = w.shape
    return np.ascontiguousarray(
        w.reshape(K // 128, 128, O // 128, 128).transpose(2, 0, 1, 3))


def tile_w_out_in_wide(w, ow=512):
    O, K = w.shape
    return np.ascontiguousarray(
        w.reshape(O // ow, ow, K // 128, 128).transpose(0, 2, 3, 1))


def emit_ln_tm(p, xs, xkeys, out_ap, out_key, stats, i, sq, eps_col, n=D,
               sqk="ln.sq", pref="ln"):
    if isinstance(sqk, str):
        sqk = [sqk]
    sqw = sq.shape[1]
    """LayerNorm token-major: xs [128, n] fp32 -> out_ap (bf16 or f32).
    stats: dict of sbuf [128, >=ntiles] fp32 buffers m0,s0,m,var,rstd; i = stat column.
    sq: [128, n] f32 scratch."""
    m0 = stats["m0"]; s0 = stats["s0"]; m = stats["m"]; var = stats["var"]; rstd = stats["rstd"]
    p.op("vector", lambda v: v.reduce_sum(m0[:, i:i + 1], xs, axis=AX.X),
         reads=xkeys, writes=[f"{pref}.m0.{i}"])
    nh_ = (n + sqw - 1) // sqw
    for h_ in range(nh_):
        lo, hi = h_ * sqw, min((h_ + 1) * sqw, n)
        p.op("scalar", lambda s, lo=lo, hi=hi: s.activation(sq[:, 0:hi - lo], xs[:, lo:hi], AF.Square),
             reads=xkeys, writes=list(sqk))
        dstc = s0 if h_ == 0 else var
        p.op("vector", lambda v, dstc=dstc, hi=hi, lo=lo: v.reduce_sum(
                dstc[:, i:i + 1], sq[:, 0:hi - lo], axis=AX.X),
             reads=list(sqk), writes=[f"{pref}.s0.{i}" if dstc is s0 else f"{pref}.s0b.{i}"])
        if h_ > 0:
            p.op("vector", lambda v: v.tensor_add(s0[:, i:i + 1], s0[:, i:i + 1], var[:, i:i + 1]),
                 reads=[f"{pref}.s0.{i}", f"{pref}.s0b.{i}"], writes=[f"{pref}.s0.{i}"])
    p.op("vector", lambda v: v.tensor_scalar_mul(m[:, i:i + 1], m0[:, i:i + 1], 1.0 / n),
         reads=[f"{pref}.m0.{i}"], writes=[f"{pref}.m.{i}"])
    p.op("vector", lambda v: v.tensor_scalar(
            var[:, i:i + 1], s0[:, i:i + 1], 1.0 / n, None, ALU.mult),
         reads=[f"{pref}.s0.{i}"], writes=[f"{pref}.va.{i}"])
    p.op("vector", lambda v: v.scalar_tensor_tensor(
            var[:, i:i + 1], m[:, i:i + 1], m[:, i:i + 1], var[:, i:i + 1],
            ALU.mult, ALU.subtract),
         reads=[f"{pref}.m.{i}", f"{pref}.va.{i}"], writes=[f"{pref}.var.{i}"])
    p.op("scalar", lambda s: s.activation(rstd[:, i:i + 1], var[:, i:i + 1], AF.Sqrt,
                                          bias=eps_col, scale=-1.0),
         reads=[f"{pref}.var.{i}", "consts"], writes=[f"{pref}.std.{i}"])
    p.op("vector", lambda v: v.reciprocal(rstd[:, i:i + 1], rstd[:, i:i + 1]),
         reads=[f"{pref}.std.{i}"], writes=[f"{pref}.rstd.{i}"])
    p.op("vector", lambda v: v.tensor_scalar(
            out_ap, xs, m[:, i:i + 1], rstd[:, i:i + 1], ALU.subtract, ALU.mult),
         reads=list(xkeys) + [f"{pref}.m.{i}", f"{pref}.rstd.{i}"], writes=[out_key])


def emit_transpose_tm_to_fm(p, src_tm, src_keys, dst_fm, dst_key, ident, pbanks, pbank_keys,
                            n_ttiles, n_dchunks, t_per=1024, ident_key="identb"):
    grp = 8
    nb = len(pbanks)
    g = 0
    for i in range(n_ttiles):
        for kc0 in range(0, n_dchunks, grp):
            bank = pbanks[g % nb]
            bkey = pbank_keys[g % nb]
            kcs = list(range(kc0, min(kc0 + grp, n_dchunks)))
            def f_tr(t, i=i, kcs=kcs, bank=bank):
                insts = []
                for j, kc in enumerate(kcs):
                    insts.append(t.transpose(
                        bank[:, j * 128:(j + 1) * 128],
                        src_tm[:, i * (128 * n_dchunks) + kc * 128: i * (128 * n_dchunks) + (kc + 1) * 128],
                        ident[:, :]))
                return insts
            p.op("tensor", f_tr, reads=[src_keys(i), ident_key], writes=[bkey], extra=())
            def f_ev(v, i=i, kcs=kcs, bank=bank):
                dst3 = dst_fm[:, :].rearrange("p (kc t) -> p kc t", t=t_per)
                dst = dst3[:, kcs[0]:kcs[0] + len(kcs), i * 128:(i + 1) * 128]
                return v.tensor_copy(dst, bank[:, 0:len(kcs) * 128].rearrange("p (kc t) -> p kc t", t=128))
            p.op("vector", f_ev, reads=[bkey], writes=[f"{dst_key}.ev.{i}.{kc0}"])
            g += 1


def build_launch_A():
    nc = bass.Bass("TRN2", target_bir_lowering=False)
    xh = nc.dram_tensor("xh", [T, D], F32, kind="ExternalInput").ap()
    w_in = nc.dram_tensor("w_in", [64, DC, 128, 128], F32, kind="ExternalInput").ap()
    ident = nc.dram_tensor("ident", [128, 128], F32, kind="ExternalInput").ap()
    z_fm = nc.dram_tensor("z_fm", [D, T], BF16, kind="ExternalOutput").ap()
    xbc_fm = nc.dram_tensor("xbc_fm", [3 * D, T], BF16, kind="ExternalOutput").ap()
    xn0b = nc.dram_tensor("xn0b", [D, T], BF16, kind="ExternalOutput").ap()

    ctx = ExitStack()
    xt = ctx.enter_context(nc.sbuf_tensor("xt", [128, TTI * D], F32))
    xnb = ctx.enter_context(nc.sbuf_tensor("xnb", [128, TTI * D], BF16))
    xn_fm = ctx.enter_context(nc.sbuf_tensor("xn_fm", [128, DC * T], BF16))
    identb = ctx.enter_context(nc.sbuf_tensor("identb", [128, 128], BF16))
    sq = ctx.enter_context(nc.sbuf_tensor("sq", [128, D], F32))
    stats = {k: ctx.enter_context(nc.sbuf_tensor(f"st_{k}", [128, TTI], F32))
             for k in ("m0", "s0", "m", "var", "rstd")}
    wsl = [ctx.enter_context(nc.sbuf_tensor(f"wsl{j}", [128, DC * 128], BF16)) for j in range(2)]
    stage = [ctx.enter_context(nc.sbuf_tensor(f"stg{j}", [128, T], BF16)) for j in range(2)]
    ptr = [ctx.enter_context(nc.psum_tensor(f"ptr{j}", [128, 512], F32)) for j in range(2)]
    pmm = [ctx.enter_context(nc.psum_tensor(f"pmm{j}", [128, 512], F32)) for j in range(4)]

    consts = ctx.enter_context(nc.sbuf_tensor("consts", [128, 4], F32))

    p = Prog(nc, ctx)
    p.op("vector", lambda v: v.memset(consts[:, 0:1], 1e-5), writes=["consts"])

    p.dma("gpsimd", lambda g: g.dma_start(identb[:, :], ident[:, :]), writes=["identb"], sem="identb")
    p.dma("gpsimd", lambda g: g.dma_start(
        xt[:, :].rearrange("p (tt d) -> p tt d", d=D),
        xh.rearrange("(tt p) d -> p tt d", p=128)), writes=[f"xt.{i}" for i in range(TTI)], sem="xt")

    for i in range(TTI):
        emit_ln_tm(p, xt[:, i * D:(i + 1) * D], [f"xt.{i}"],
                   xnb[:, i * D:(i + 1) * D], f"ln.out.{i}", stats, i, sq,
                   eps_col=consts[:, 0:1])

    ptr_b = [b.bitcast(BF16) for b in ptr]
    emit_transpose_tm_to_fm(p, xnb, lambda i: f"ln.out.{i}", xn_fm, "xn_fm",
                            identb, ptr_b, ["ptrb0", "ptrb1"], TTI, DC, t_per=T)
    xn_fm_ready = [f"xn_fm.ev.{i}.{kc0}" for i in range(TTI) for kc0 in (0, 8)]

    p.dma("sync", lambda s: s.dma_start(
        xn0b.rearrange("(kc p) t -> p kc t", p=128),
        xn_fm[:, :].rearrange("p (kc t) -> p kc t", t=T)),
        reads=xn_fm_ready, writes=["xn0b_out"], sem="xn0b_out")

    for oc in range(64):
        par = oc % 2
        p.dma("gpsimd", lambda g, oc=oc, par=par: g.dma_start(
            wsl[par][:, :].rearrange("p (kc o) -> p kc o", o=128),
            w_in[oc].rearrange("kc k o -> k kc o")),
            reads=[], writes=[f"wsl.{par}"], sem=f"wsl{par}")
        pbs = [pmm[par * 2], pmm[par * 2 + 1]]
        pbks = [f"pmm.{par * 2}", f"pmm.{par * 2 + 1}"]
        def f_mm(t, oc=oc, par=par, pbs=pbs):
            for kc in range(DC):
                for nh in range(2):
                    inst = t.matmul(
                        pbs[nh][:, :],
                        wsl[par][:, kc * 128:(kc + 1) * 128],
                        xn_fm[:, kc * T + nh * 512: kc * T + (nh + 1) * 512],
                        start=(kc == 0), stop=(kc == DC - 1))
            return inst
        p.op("tensor", f_mm, reads=[f"wsl.{par}"] + xn_fm_ready, writes=pbks)
        for nh in range(2):
            p.op("vector", lambda v, par=par, nh=nh, pb=pbs[nh]: v.tensor_copy(
                stage[par][:, nh * 512:(nh + 1) * 512], pb[:, :]),
                reads=[pbks[nh]], writes=[f"stage.{par}.{nh}"])
        dst = z_fm if oc < 16 else xbc_fm
        row = oc * 128 if oc < 16 else (oc - 16) * 128
        p.dma("sync", lambda s, dst=dst, row=row, par=par: s.dma_start(
            dst[row:row + 128, :], stage[par][:, :]),
            reads=[f"stage.{par}.0", f"stage.{par}.1"], writes=[f"out.{oc}"], sem=f"st{par}")

    p.barrier_all("sync")
    p.finalize()
    ctx.close()
    return nc


def _run(nc, in_maps):
    return run_bass_kernel_spmd(nc, in_maps, core_ids=list(range(NCORES))).results


def build_launch_A2():
    nc = bass.Bass("TRN2", target_bir_lowering=False)
    xn0b = nc.dram_tensor("xn0b", [D, T], BF16, kind="ExternalInput").ap()
    w_gu = nc.dram_tensor("w_gu", [128, DC, 128, 128], F32, kind="ExternalInput").ap()
    w_d = nc.dram_tensor("w_d", [4, 64, 128, 512], F32, kind="ExternalInput").ap()
    g0 = nc.dram_tensor("g0", [T, D], BF16, kind="ExternalOutput").ap()

    ctx = ExitStack()
    xn_fm = ctx.enter_context(nc.sbuf_tensor("xn_fm", [128, DC * T], BF16))
    h = ctx.enter_context(nc.sbuf_tensor("h", [128, 64 * T], BF16))
    wsl = [ctx.enter_context(nc.sbuf_tensor(f"wsl{j}", [128, DC * 128], BF16)) for j in range(4)]
    wdsl = [ctx.enter_context(nc.sbuf_tensor(f"wdsl{j}", [128, 8 * 512], BF16)) for j in range(2)]
    sg = ctx.enter_context(nc.sbuf_tensor("sg", [128, T], F32))
    stage = [ctx.enter_context(nc.sbuf_tensor(f"stg{j}", [128, 512], BF16)) for j in range(8)]
    pmm = [ctx.enter_context(nc.psum_tensor(f"pmm{j}", [128, 512], F32)) for j in range(8)]

    p = Prog(nc, ctx)
    p.dma("gpsimd", lambda g: g.dma_start(
        xn_fm[:, :].rearrange("p (kc t) -> p kc t", t=T),
        xn0b.rearrange("(kc p) t -> p kc t", p=128)), writes=["xn_fm"], sem="xn_fm")

    for i in range(64):
        par = i % 2
        wg, wu = wsl[par * 2], wsl[par * 2 + 1]
        p.dma("gpsimd", lambda g, i=i, wg=wg: g.dma_start(
            wg[:, :].rearrange("p (kc o) -> p kc o", o=128),
            w_gu[i].rearrange("kc k o -> k kc o")), writes=[f"wg.{par}"], sem=f"wg{par}")
        p.dma("gpsimd", lambda g, i=i, wu=wu: g.dma_start(
            wu[:, :].rearrange("p (kc o) -> p kc o", o=128),
            w_gu[64 + i].rearrange("kc k o -> k kc o")), writes=[f"wu.{par}"], sem=f"wu{par}")
        pg = [pmm[par * 4], pmm[par * 4 + 1]]
        pu = [pmm[par * 4 + 2], pmm[par * 4 + 3]]
        pgk = [f"pmm.{par * 4}", f"pmm.{par * 4 + 1}"]
        puk = [f"pmm.{par * 4 + 2}", f"pmm.{par * 4 + 3}"]
        def f_mm(t, wg=wg, wu=wu, pg=pg, pu=pu):
            for kc in range(DC):
                for nh in range(2):
                    rhs = xn_fm[:, kc * T + nh * 512: kc * T + (nh + 1) * 512]
                    t.matmul(pg[nh][:, :], wg[:, kc * 128:(kc + 1) * 128], rhs,
                             start=(kc == 0), stop=(kc == DC - 1))
                    inst = t.matmul(pu[nh][:, :], wu[:, kc * 128:(kc + 1) * 128], rhs,
                                    start=(kc == 0), stop=(kc == DC - 1))
            return inst
        p.op("tensor", f_mm, reads=[f"wg.{par}", f"wu.{par}", "xn_fm"], writes=pgk + puk)
        for nh in range(2):
            p.op("scalar", lambda s, nh=nh, pg=pg: s.activation(
                sg[:, nh * 512:(nh + 1) * 512], pg[nh][:, :], AF.Silu),
                reads=[pgk[nh]], writes=[f"sg.{nh}"])
            p.op("vector", lambda v, i=i, nh=nh, pu=pu: v.tensor_mul(
                h[:, i * T + nh * 512: i * T + (nh + 1) * 512],
                sg[:, nh * 512:(nh + 1) * 512], pu[nh][:, :]),
                reads=[f"sg.{nh}", puk[nh]], writes=[f"h.{i}"])

    for oc in range(4):
        for icg in range(8):
            par = icg % 2
            p.dma("gpsimd", lambda g, oc=oc, icg=icg, par=par: g.dma_start(
                wdsl[par][:, :].rearrange("p (ic o) -> p ic o", o=512),
                w_d[oc, icg * 8:(icg + 1) * 8].rearrange("ic k o -> k ic o")),
                writes=[f"wd.{par}"], sem=f"wd{par}")
            def f_dn(t, oc=oc, icg=icg, par=par):
                for ici in range(8):
                    ic = icg * 8 + ici
                    for tt in range(TTI):
                        inst = t.matmul(
                            pmm[tt][:, :],
                            h[:, ic * T + tt * 128:(ic * T) + (tt + 1) * 128],
                            wdsl[par][:, ici * 512:(ici + 1) * 512],
                            start=(ic == 0), stop=(ic == 63))
                return inst
            p.op("tensor", f_dn,
                 reads=[f"wd.{par}"] + [f"h.{icg * 8 + j}" for j in range(8)],
                 writes=[f"pmm.{tt}" for tt in range(TTI)])
        for tt in range(TTI):
            p.op("scalar", lambda s, tt=tt: s.activation(
                stage[tt][:, :], pmm[tt][:, :], AF.Silu),
                reads=[f"pmm.{tt}"], writes=[f"stage.{tt}"])
            p.dma("sync", lambda s, oc=oc, tt=tt: s.dma_start(
                g0[tt * 128:(tt + 1) * 128, oc * 512:(oc + 1) * 512], stage[tt][:, :]),
                reads=[f"stage.{tt}"], writes=[f"g0out.{oc}.{tt}"], sem=f"g0st{tt}")

    p.barrier_all("sync")
    p.finalize()
    ctx.close()
    return nc


def prep_host(inputs):
    hp = {}
    hp["ident"] = np.eye(128, dtype=np.float32)
    hp["w_in"] = tile_w_out_in(np.ascontiguousarray(inputs["L0_attn_W_in"][:8192]))
    hp["w_gu"] = tile_w_out_in(inputs["L0_ffn_w_gate_up"])
    hp["w_d"] = tile_w_out_in_wide(inputs["L0_ffn_w_down"], ow=512)
    return hp


def run_phase1(inputs, upto="Z"):
    x = np.asarray(inputs["L0_attn_u"])
    hp = prep_host(inputs)
    inter = {}

    def core_tok(core):
        b, h = core // 2, core % 2
        return b, slice(h * 1024, (h + 1) * 1024)

    nc = build_launch_A()
    im = []
    for core in range(NCORES):
        b, sl = core_tok(core)
        im.append({"xh": np.ascontiguousarray(x[b, sl]), "w_in": hp["w_in"],
                   "ident": hp["ident"]})
    ra = _run(nc, im)
    inter["A"] = ra
    if upto == "A":
        return inter

    nc = build_launch_A2()
    im = []
    for core in range(NCORES):
        im.append({"xn0b": np.ascontiguousarray(ra[core]["xn0b"]),
                   "w_gu": hp["w_gu"], "w_d": hp["w_d"]})
    ra2 = _run(nc, im)
    inter["A2"] = ra2
    if upto == "A2":
        return inter

    A_log = np.asarray(inputs["L0_attn_A_log"], np.float32)
    conv_w = np.asarray(inputs["L0_attn_conv_w"], np.float32)
    conv_b = np.asarray(inputs["L0_attn_conv_b"], np.float32)
    A_log_rep = np.repeat(A_log, 64)
    mask_ssd = np.vstack([np.triu(np.ones((64, 64), np.float32))] * 2)
    xbc_b = [np.concatenate([ra[2 * b]["xbc_fm"], ra[2 * b + 1]["xbc_fm"]], axis=1)
             for b in range(BATCH)]
    nc = build_launch_B()
    im = []
    for core in range(NCORES):
        b, hh = core // 2, core % 2
        ch = np.concatenate([np.arange(hh * 1024, hh * 1024 + 1024),
                             2048 + np.arange(hh * 1024, hh * 1024 + 1024),
                             4096 + np.arange(hh * 1024, hh * 1024 + 1024)])
        cw_dev = np.ascontiguousarray(
            conv_w[ch, 0, :].reshape(24, 128, 3).transpose(1, 0, 2).reshape(128, 72))
        cb_dev = np.ascontiguousarray(conv_b[ch].reshape(24, 128).T)
        alog_dev = np.ascontiguousarray(
            A_log_rep[hh * 1024:(hh + 1) * 1024].reshape(8, 128).T)
        im.append({
            "xp": np.ascontiguousarray(xbc_b[b][hh * 1024:(hh + 1) * 1024]),
            "Bp": np.ascontiguousarray(xbc_b[b][2048 + hh * 1024: 2048 + (hh + 1) * 1024]),
            "Cp": np.ascontiguousarray(xbc_b[b][4096 + hh * 1024: 4096 + (hh + 1) * 1024]),
            "cw": cw_dev, "cb": cb_dev, "alog": alog_dev,
            "ident": hp["ident"], "mask": mask_ssd,
        })
    rb = _run(nc, im)
    inter["B"] = rb
    if upto == "B":
        return inter

    w_out_t = tile_w_out_in_wide(np.asarray(inputs["L0_attn_W_out"], np.float32), 512)
    rmsw_dev = np.ascontiguousarray(
        np.asarray(inputs["L0_attn_rms_w"], np.float32).reshape(DC, 128).T)
    y_full = [np.concatenate([rb[2 * b]["y_fm"], rb[2 * b + 1]["y_fm"]], axis=0)
              for b in range(BATCH)]
    nc = build_launch_C()
    im = []
    for core in range(NCORES):
        b, h = core // 2, core % 2
        sl = slice(h * 1024, (h + 1) * 1024)
        im.append({
            "y_sl": np.ascontiguousarray(y_full[b][:, sl]),
            "z_fm": np.ascontiguousarray(ra[core]["z_fm"]),
            "xh": np.ascontiguousarray(x[b, sl]),
            "g0": np.ascontiguousarray(ra2[core]["g0"]),
            "w_out": w_out_t, "rmsw": rmsw_dev, "ident": hp["ident"],
        })
    rc = _run(nc, im)
    inter["C"] = rc
    if upto == "C":
        return inter

    lnw = np.asarray(inputs["L1_ffn_ln_weight"], np.float32)
    lnb = np.asarray(inputs["L1_ffn_ln_bias"], np.float32)
    lnwb_dev = np.ascontiguousarray(np.concatenate(
        [lnw.reshape(DC, 128).T, lnb.reshape(DC, 128).T], 1))
    gamrow = np.concatenate([np.asarray(inputs["L1_ffn_gamma"], np.float32),
                             np.asarray(inputs["L1_ffn_beta"], np.float32)])[None, :]
    w1t = tile_w_out_in_wide(np.asarray(inputs["L1_ffn_weight"], np.float32), 512)
    nc = build_launch_C2()
    im = [{"xn1b_fm": np.ascontiguousarray(rc[core]["xn1b_fm"]), "w1": w1t,
           "lnwb": lnwb_dev, "gamrow": gamrow, "ident": hp["ident"]}
          for core in range(NCORES)]
    rc2 = _run(nc, im)
    inter["C2"] = rc2
    if upto == "C2":
        return inter

    w_qkv = np.asarray(inputs["L1_attn_w_qkv"], np.float32)
    tiles = np.ascontiguousarray(w_qkv.reshape(DC, 128, 48, 128).transpose(2, 0, 1, 3))
    ln_s = np.asarray(inputs["L1_attn_ln_scale"], np.float32)
    ln_b = np.asarray(inputs["L1_attn_ln_bias"], np.float32)
    inv = 1.0 / 10000.0 ** (np.arange(0, 128, 2) / 128)
    tgrid = np.arange(SEQ)[:, None] * inv[None, :]
    cos2 = np.vstack([np.cos(tgrid).T] * 2).astype(np.float32)
    sin2 = np.vstack([np.sin(tgrid).T] * 2).astype(np.float32)
    cs2 = np.ascontiguousarray(np.hstack([cos2, sin2]))
    mask4 = np.zeros((128, 4 * 512), np.float32)
    for off in range(4):
        for pz in range(128):
            mask4[pz, off * 512 + off * 128 + pz:(off + 1) * 512] = 1.0
    xn1_full = [np.concatenate([rc[2 * b]["xn1b_fm"], rc[2 * b + 1]["xn1b_fm"]], axis=1)
                for b in range(BATCH)]
    nc = build_launch_D()
    im = []
    for core in range(NCORES):
        b, hh = core // 2, core % 2
        wsel = np.stack([np.stack([tiles[hh * 8 + hl], tiles[16 + hh * 8 + hl],
                                   tiles[32 + hh * 8 + hl]]) for hl in range(8)])
        im.append({"xn1_fm": np.ascontiguousarray(xn1_full[b]),
                   "wqkv": np.ascontiguousarray(wsel),
                   "lns": np.ascontiguousarray(np.stack([ln_s, ln_b], 1)),
                   "cs2": cs2, "mask4": mask4,
                   "swp": np.roll(np.eye(128, dtype=np.float32), 64, axis=1)})
    rd = _run(nc, im)
    inter["D"] = rd
    if upto == "D":
        return inter

    wout1t = tile_w_in_out_wide(np.asarray(inputs["L1_attn_w_out"], np.float32), 512)
    attn_full = [np.concatenate([rd[2 * b]["attn"], rd[2 * b + 1]["attn"]], axis=0)
                 for b in range(BATCH)]
    nc = build_launch_E()
    im = []
    for core in range(NCORES):
        b, h = core // 2, core % 2
        sl = slice(h * 1024, (h + 1) * 1024)
        im.append({"attn_sl": np.ascontiguousarray(attn_full[b][:, sl]),
                   "w_out1": wout1t,
                   "x1": np.ascontiguousarray(rc[core]["x1"]),
                   "g1": np.ascontiguousarray(rc2[core]["g1"])})
    re_ = _run(nc, im)
    inter["E"] = re_
    return inter


def kernel(**inputs):
    return kernel_fused(**inputs)


NT_B = 8
NCH = 32
CH = 64


def build_launch_B(debug=False, stage=3):
    nc = bass.Bass("TRN2", target_bir_lowering=False)
    xp = nc.dram_tensor("xp", [1024, SEQ], BF16, kind="ExternalInput").ap()
    Bp = nc.dram_tensor("Bp", [1024, SEQ], BF16, kind="ExternalInput").ap()
    Cp = nc.dram_tensor("Cp", [1024, SEQ], BF16, kind="ExternalInput").ap()
    cw = nc.dram_tensor("cw", [128, 24 * 3], F32, kind="ExternalInput").ap()
    cb = nc.dram_tensor("cb", [128, 24], F32, kind="ExternalInput").ap()
    alog = nc.dram_tensor("alog", [128, NT_B], F32, kind="ExternalInput").ap()
    ident = nc.dram_tensor("ident", [128, 128], F32, kind="ExternalInput").ap()
    mask = nc.dram_tensor("mask", [128, 64], F32, kind="ExternalInput").ap()
    y_fm = nc.dram_tensor("y_fm", [1024, SEQ], BF16, kind="ExternalOutput").ap()
    if debug:
        xs_dbg = nc.dram_tensor("xs_dbg", [1024, SEQ], BF16, kind="ExternalOutput").ap()
        bs_dbg = nc.dram_tensor("bs_dbg", [1024, SEQ], BF16, kind="ExternalOutput").ap()
        cs_dbg = nc.dram_tensor("cs_dbg", [128, NT_B * NCH], F32, kind="ExternalOutput").ap()
        prev_dbg = nc.dram_tensor("prev_dbg", [128, NT_B * (NCH + 1)], F32, kind="ExternalOutput").ap()
        xt_dbg = nc.dram_tensor("xt_dbg", [128, NT_B * SEQ], BF16, kind="ExternalOutput").ap()

    ctx = ExitStack()
    xs = ctx.enter_context(nc.sbuf_tensor("xs", [128, NT_B * SEQ], BF16))
    Bs = ctx.enter_context(nc.sbuf_tensor("Bs", [128, NT_B * SEQ], BF16))
    Cs = ctx.enter_context(nc.sbuf_tensor("Cs", [128, NT_B * SEQ], BF16))
    xT = ctx.enter_context(nc.sbuf_tensor("xT", [128, NT_B * SEQ], BF16))
    cws = ctx.enter_context(nc.sbuf_tensor("cws", [128, 24 * 3], F32))
    cbs = ctx.enter_context(nc.sbuf_tensor("cbs", [128, 24], F32))
    alogs = ctx.enter_context(nc.sbuf_tensor("alogs", [128, NT_B], F32))
    decay = ctx.enter_context(nc.sbuf_tensor("decay", [128, NT_B], F32))
    identb = ctx.enter_context(nc.sbuf_tensor("identb", [128, 128], BF16))
    maskb = ctx.enter_context(nc.sbuf_tensor("maskb", [128, 64], F32))
    acc_a = ctx.enter_context(nc.sbuf_tensor("acc_a", [128, SEQ], F32))
    acc_b = ctx.enter_context(nc.sbuf_tensor("acc_b", [128, SEQ], F32))
    prod = [ctx.enter_context(nc.sbuf_tensor(f"prod{j}", [128, SEQ], BF16)) for j in range(2)]
    cs_all = ctx.enter_context(nc.sbuf_tensor("cs_all", [128, NT_B * NCH], F32))
    prevb = ctx.enter_context(nc.sbuf_tensor("prevb", [128, NT_B * (NCH + 1)], F32))
    onesb = ctx.enter_context(nc.sbuf_tensor("onesb", [128, 1], BF16))
    onesrow = ctx.enter_context(nc.sbuf_tensor("onesrow", [1, 128], BF16))
    gt_sb = [ctx.enter_context(nc.sbuf_tensor(f"gt{j}", [128, 512], BF16)) for j in range(2)]
    yoff_sb = [ctx.enter_context(nc.sbuf_tensor(f"yoff{j}", [1, 512], BF16)) for j in range(4)]
    y_st = [ctx.enter_context(nc.sbuf_tensor(f"yst{j}", [128, 512], BF16)) for j in range(2)]
    pb = [ctx.enter_context(nc.psum_tensor(f"pb{j}", [128, 512], F32)) for j in range(8)]

    p = Prog(nc, ctx)

    for name, dst, src in (("xs", xs, xp), ("Bs", Bs, Bp), ("Cs", Cs, Cp)):
        p.dma("gpsimd", lambda g, dst=dst, src=src: g.dma_start(
            dst[:, :].rearrange("p (t s) -> p t s", s=SEQ),
            src.rearrange("(t p) s -> p t s", p=128)),
            writes=[f"{name}.pre.{t}" for t in range(NT_B)], sem=f"in_{name}")
    p.dma("gpsimd", lambda g: g.dma_start(cws[:, :], cw[:, :]), writes=["cws"], sem="cws")
    p.dma("gpsimd", lambda g: g.dma_start(cbs[:, :], cb[:, :]), writes=["cbs"], sem="cbs")
    p.dma("gpsimd", lambda g: g.dma_start(alogs[:, :], alog[:, :]), writes=["alogs"], sem="alogs")
    p.dma("gpsimd", lambda g: g.dma_start(identb[:, :], ident[:, :]), writes=["identb"], sem="identb")
    p.dma("gpsimd", lambda g: g.dma_start(maskb[:, :], mask[:, :]), writes=["maskb"], sem="maskb")
    p.op("vector", lambda v: v.memset(onesb[:, :], 1.0), writes=["onesb"])
    p.op("vector", lambda v: v.memset(onesrow[:, :], 1.0), writes=["onesrow"])
    p.op("vector", lambda v: v.memset(pb[0][:, :], 0.0), writes=["pb.0"])
    p.op("vector", lambda v: v.memset(pb[1][:, :], 0.0), writes=["pb.1"])
    p.op("vector", lambda v: v.memset(prevb[:, :], 0.0), writes=["prevb.z"])
    p.op("scalar", lambda s: s.activation(decay[:, :], alogs[:, :], AF.Exp, scale=63.0),
         reads=["alogs"], writes=["decay"])

    def conv_tile(part, buf, key, t, gidx):
        src = buf[:, t * SEQ:(t + 1) * SEQ]
        w0 = cws[:, gidx * 3 + 0: gidx * 3 + 1]
        w1 = cws[:, gidx * 3 + 1: gidx * 3 + 2]
        w2 = cws[:, gidx * 3 + 2: gidx * 3 + 3]
        bcol = cbs[:, gidx:gidx + 1]
        kpre = f"{key}.pre.{t}"
        p.op("vector", lambda v: v.memset(acc_a[:, 0:1], 0.0), writes=["acc_a"])
        p.op("vector", lambda v: v.tensor_scalar_mul(acc_a[:, 1:SEQ], src[:, 0:SEQ - 1], w0),
             reads=[kpre, "cws"], writes=["acc_a"])
        p.op("vector", lambda v: v.scalar_tensor_tensor(
                acc_b[:, :], src, w1, acc_a[:, :], ALU.mult, ALU.add),
             reads=[kpre, "cws", "acc_a"], writes=["acc_b"])
        p.op("vector", lambda v: v.tensor_copy(acc_a[:, SEQ - 1:SEQ], acc_b[:, SEQ - 1:SEQ]),
             reads=["acc_b"], writes=["acc_a"])
        p.op("vector", lambda v: v.scalar_tensor_tensor(
                acc_a[:, 0:SEQ - 1], src[:, 1:SEQ], w2, acc_b[:, 0:SEQ - 1],
                ALU.mult, ALU.add),
             reads=[kpre, "cws", "acc_b"], writes=["acc_a"])
        p.op("scalar", lambda s: s.activation(src, acc_a[:, :], AF.Silu, bias=bcol),
             reads=["acc_a", "cbs", kpre], writes=[f"{key}.{t}"])

    for t in range(NT_B):
        conv_tile("x", xs, "xs", t, t)
    for t in range(NT_B):
        conv_tile("B", Bs, "Bs", t, 8 + t)
    for t in range(NT_B):
        conv_tile("C", Cs, "Cs", t, 16 + t)

    for j in range(NT_B):
        pj = prod[j % 2]
        p.op("vector", lambda v, j=j, pj=pj: v.tensor_tensor(
                pj[:, :], xs[:, j * SEQ:(j + 1) * SEQ], Bs[:, j * SEQ:(j + 1) * SEQ], ALU.mult),
             reads=[f"xs.{j}", f"Bs.{j}"], writes=[f"prodbuf.{j % 2}"])
        p.op("vector", lambda v, j=j, pj=pj: v.reduce_sum(
                cs_all[:, j * NCH:(j + 1) * NCH].unsqueeze(2),
                pj[:, :].rearrange("p (c s) -> p c s", s=CH), axis=AX.X),
             reads=[f"prodbuf.{j % 2}"], writes=[f"cs.{j}"])
        p.op("vector", lambda v, j=j: v.tensor_tensor_scan(
                prevb[:, j * (NCH + 1) + 1: j * (NCH + 1) + 1 + NCH],
                decay[:, j:j + 1].to_broadcast([128, NCH]),
                cs_all[:, j * NCH:(j + 1) * NCH],
                0.0, ALU.mult, ALU.add),
             reads=[f"cs.{j}", "decay", "prevb.z"], writes=[f"prev.{j}"])

    grp = 8
    for j in range(NT_B):
        for sb0 in (0, 8):
            bank = pb[6 + (sb0 // 8) % 2].bitcast(BF16)
            bkey = f"pb.{6 + (sb0 // 8) % 2}"
            def f_tr(t, j=j, sb0=sb0, bank=bank):
                insts = []
                for i in range(grp):
                    sb = sb0 + i
                    insts.append(t.transpose(
                        bank[:, i * 128:(i + 1) * 128],
                        xs[:, j * SEQ + sb * 128: j * SEQ + (sb + 1) * 128],
                        identb[:, :]))
                return insts
            p.op("tensor", f_tr, reads=[f"xs.{j}", "identb"], writes=[bkey])
            def f_ev(v, j=j, sb0=sb0, bank=bank):
                dst = xT[:, :].rearrange("p (b ch) -> p b ch", ch=128)[:, j * 16 + sb0: j * 16 + sb0 + grp, :]
                return v.tensor_copy(dst, bank[:, :].rearrange("p (b ch) -> p b ch", ch=128))
            p.op("vector", f_ev, reads=[bkey], writes=[f"xT.{j}.{sb0}"])

    for j in range(NT_B if stage >= 3 else 0):
        pj = prod[j % 2]
        prev_j = prevb[:, j * (NCH + 1): j * (NCH + 1) + NCH]
        p.op("vector", lambda v, j=j, pj=pj, prev_j=prev_j: v.tensor_tensor(
                pj[:, :].rearrange("p (c s) -> p c s", s=CH),
                prev_j.unsqueeze(2).to_broadcast([128, NCH, CH]),
                Cs[:, j * SEQ:(j + 1) * SEQ].rearrange("p (c s) -> p c s", s=CH),
                ALU.mult),
             reads=[f"prev.{j}", f"Cs.{j}", "prevb.z"], writes=[f"prodbuf.{j % 2}"])
        for g in range(4):
            gpar = (j * 4 + g) % 2
            yb = pb[2 + gpar]
            ybk = f"pb.{2 + gpar}"
            ysb = y_st[gpar]
            for e in range(2):
                ob = pb[4 + e]
                obk = f"pb.{4 + e}"
                yo = yoff_sb[gpar * 2 + e]
                yok = f"yoff.{gpar * 2 + e}"
                p.op("tensor", lambda t, j=j, g=g, e=e, ob=ob: t.matmul(
                        ob[0:1, :], onesb[e * 64:e * 64 + 64, :],
                        prod[j % 2][e * 64:e * 64 + 64, g * 512:(g + 1) * 512],
                        start=True, stop=True),
                     reads=[f"prodbuf.{j % 2}", "onesb"], writes=[obk])
                p.op("vector", lambda v, yo=yo, ob=ob: v.tensor_copy(yo[:, :], ob[0:1, :]),
                     reads=[obk], writes=[yok])
                gb = pb[e]
                gbk = f"pb.{e}"
                def f_g(t, j=j, g=g, e=e, gb=gb):
                    for ci in range(8):
                        c = g * 8 + ci
                        rh = (c % 2) * 64
                        inst = t.matmul(
                            gb[rh:rh + 64, ci * 64:(ci + 1) * 64],
                            Bs[e * 64:e * 64 + 64, j * SEQ + c * CH: j * SEQ + (c + 1) * CH],
                            Cs[e * 64:e * 64 + 64, j * SEQ + c * CH: j * SEQ + (c + 1) * CH],
                            start=True, stop=True)
                    return inst
                p.op("tensor", f_g, reads=[f"Bs.{j}", f"Cs.{j}"], writes=[gbk])
                p.op("vector", lambda v, e=e, gb=gb: v.tensor_tensor(
                        gt_sb[e][:, :].rearrange("p (ci l) -> p ci l", l=64),
                        gb[:, :].rearrange("p (ci l) -> p ci l", l=64),
                        maskb[:, :].unsqueeze(1).to_broadcast([128, 8, 64]),
                        ALU.mult),
                     reads=[gbk, "maskb"], writes=[f"gts.{e}"])
            def f_y(t, j=j, g=g, gpar=gpar, yb=yb):
                for ci in range(8):
                    c = g * 8 + ci
                    sb = c // 2
                    mrow = (c % 2) * 64
                    for e in range(2):
                        t.matmul(
                            yb[e * 64:e * 64 + 64, ci * 64:(ci + 1) * 64],
                            xT[mrow:mrow + 64, (j * 16 + sb) * 128 + e * 64: (j * 16 + sb) * 128 + e * 64 + 64],
                            gt_sb[e][mrow:mrow + 64, ci * 64:(ci + 1) * 64],
                            start=True, stop=False)
                        inst = t.matmul(
                            yb[e * 64:e * 64 + 64, ci * 64:(ci + 1) * 64],
                            onesrow[0:1, 0:64],
                            yoff_sb[gpar * 2 + e][0:1, ci * 64:(ci + 1) * 64],
                            start=False, stop=True)
                return inst
            p.op("tensor", f_y,
                 reads=["gts.0", "gts.1", f"xT.{j}.0", f"xT.{j}.8", "onesrow",
                        f"yoff.{gpar * 2}", f"yoff.{gpar * 2 + 1}"],
                 writes=[ybk])
            p.op("vector", lambda v, yb=yb, ysb=ysb: v.tensor_copy(ysb[:, :], yb[:, :]),
                 reads=[ybk], writes=[f"yst.{gpar}"])
            p.dma("sync", lambda s, j=j, g=g, ysb=ysb: s.dma_start(
                    y_fm[j * 128:(j + 1) * 128, g * 512:(g + 1) * 512], ysb[:, :]),
                reads=[f"yst.{gpar}"], writes=[f"yout.{j}.{g}"], sem=f"yout{gpar}")

    if debug:
        p.dma("sync", lambda s: s.dma_start(
            xs_dbg.rearrange("(t p) s -> p t s", p=128),
            xs[:, :].rearrange("p (t s) -> p t s", s=SEQ)),
            reads=[f"xs.{t}" for t in range(NT_B)], writes=["dbg1"], sem="dbg")
        p.dma("sync", lambda s: s.dma_start(
            bs_dbg.rearrange("(t p) s -> p t s", p=128),
            Bs[:, :].rearrange("p (t s) -> p t s", s=SEQ)),
            reads=[f"Bs.{t}" for t in range(NT_B)], writes=["dbg1b"], sem="dbg")
        p.dma("sync", lambda s: s.dma_start(cs_dbg, cs_all[:, :]),
            reads=[f"cs.{t}" for t in range(NT_B)], writes=["dbg2"], sem="dbg")
        p.dma("sync", lambda s: s.dma_start(prev_dbg, prevb[:, :]),
            reads=[f"prev.{t}" for t in range(NT_B)], writes=["dbg3"], sem="dbg")
        p.dma("sync", lambda s: s.dma_start(xt_dbg, xT[:, :]),
            reads=[f"xT.{t}.{s0}" for t in range(NT_B) for s0 in (0, 8)], writes=["dbg4"], sem="dbg")

    p.barrier_all("sync")
    p.finalize()
    ctx.close()
    return nc


def build_launch_C():
    nc = bass.Bass("TRN2", target_bir_lowering=False)
    y_sl = nc.dram_tensor("y_sl", [D, T], BF16, kind="ExternalInput").ap()
    z_fm = nc.dram_tensor("z_fm", [D, T], BF16, kind="ExternalInput").ap()
    xh = nc.dram_tensor("xh", [T, D], F32, kind="ExternalInput").ap()
    g0 = nc.dram_tensor("g0", [T, D], BF16, kind="ExternalInput").ap()
    w_out = nc.dram_tensor("w_out", [4, DC, 128, 512], F32, kind="ExternalInput").ap()
    rmsw = nc.dram_tensor("rmsw", [128, DC], F32, kind="ExternalInput").ap()
    ident = nc.dram_tensor("ident", [128, 128], F32, kind="ExternalInput").ap()
    x1 = nc.dram_tensor("x1", [T, D], F32, kind="ExternalOutput").ap()
    xn1b_fm = nc.dram_tensor("xn1b_fm", [D, T], BF16, kind="ExternalOutput").ap()

    ctx = ExitStack()
    ybuf = ctx.enter_context(nc.sbuf_tensor("ybuf", [128, DC * T], BF16))
    zbuf = ctx.enter_context(nc.sbuf_tensor("zbuf", [128, DC * T], BF16))
    xt = ctx.enter_context(nc.sbuf_tensor("xt", [128, TTI * D], F32))
    g0sl = [ctx.enter_context(nc.sbuf_tensor(f"g0sl{j}", [128, TTI * 512], BF16)) for j in range(2)]
    wsl = [ctx.enter_context(nc.sbuf_tensor(f"wsl{j}", [128, 8 * 512], BF16)) for j in range(2)]
    zs = ctx.enter_context(nc.sbuf_tensor("zs", [128, T], F32))
    sqb = [ctx.enter_context(nc.sbuf_tensor(f"sqb{j}", [128, T], BF16)) for j in range(2)]
    ones128b = ctx.enter_context(nc.sbuf_tensor("ones128b", [128, 1], BF16))
    onesrow = ctx.enter_context(nc.sbuf_tensor("onesrow", [1, 128], F32))
    rrow = ctx.enter_context(nc.sbuf_tensor("ROWA", [1, T], F32))
    rb = ctx.enter_context(nc.sbuf_tensor("rb", [128, T], F32))
    rmsws = ctx.enter_context(nc.sbuf_tensor("rmsws", [128, DC], F32))
    identb = ctx.enter_context(nc.sbuf_tensor("identb", [128, 128], BF16))
    consts = ctx.enter_context(nc.sbuf_tensor("consts", [128, 4], F32))
    tmp = ctx.enter_context(nc.sbuf_tensor("tmp", [128, 512], F32))
    xnb_s = ctx.enter_context(nc.sbuf_tensor("xnb_s", [128, D], BF16))
    xn1st = ctx.enter_context(nc.sbuf_tensor("xn1st", [128, D], BF16))
    sq = ctx.enter_context(nc.sbuf_tensor("sq", [128, D], F32))
    stats = {k: ctx.enter_context(nc.sbuf_tensor(f"st_{k}", [128, TTI], F32))
             for k in ("m0", "s0", "m", "var", "rstd")}
    pb = [ctx.enter_context(nc.psum_tensor(f"pb{j}", [128, 512], F32)) for j in range(8)]

    p = Prog(nc, ctx)
    p.op("vector", lambda v: v.memset(consts[:, 0:1], 1.1920929e-07), writes=["consts"])
    p.op("vector", lambda v: v.memset(consts[:, 1:2], 1e-5), writes=["consts"])
    p.op("vector", lambda v: v.memset(ones128b[:, :], 1.0), writes=["ones128b"])
    p.op("vector", lambda v: v.memset(onesrow[:, :], 1.0), writes=["onesrow"])
    p.dma("gpsimd", lambda g: g.dma_start(
        ybuf[:, :].rearrange("p (kc t) -> p kc t", t=T),
        y_sl.rearrange("(kc p) t -> p kc t", p=128)),
        writes=[f"y.{kc}" for kc in range(DC)], sem="y")
    p.dma("gpsimd", lambda g: g.dma_start(
        zbuf[:, :].rearrange("p (kc t) -> p kc t", t=T),
        z_fm.rearrange("(kc p) t -> p kc t", p=128)),
        writes=[f"z.{kc}" for kc in range(DC)], sem="z")
    p.dma("gpsimd", lambda g: g.dma_start(
        xt[:, :].rearrange("p (tt d) -> p tt d", d=D),
        xh.rearrange("(tt p) d -> p tt d", p=128)),
        writes=[f"xt.{tt}" for tt in range(TTI)], sem="xt")
    p.dma("gpsimd", lambda g: g.dma_start(rmsws[:, :], rmsw[:, :]), writes=["rmsws"], sem="rmsws")
    p.dma("gpsimd", lambda g: g.dma_start(identb[:, :], ident[:, :]), writes=["identb"], sem="identb")

    for kc in range(DC):
        par = kc % 2
        p.op("scalar", lambda s, kc=kc: s.activation(
                zs[:, :], zbuf[:, kc * T:(kc + 1) * T], AF.Silu),
             reads=[f"z.{kc}"], writes=["zs"])
        p.op("vector", lambda v, kc=kc: v.tensor_mul(
                zbuf[:, kc * T:(kc + 1) * T], ybuf[:, kc * T:(kc + 1) * T], zs[:, :]),
             reads=[f"y.{kc}", "zs"], writes=[f"z.{kc}"])
        p.op("scalar", lambda s, kc=kc, par=par: s.activation(
                sqb[par][:, :], zbuf[:, kc * T:(kc + 1) * T], AF.Square),
             reads=[f"z.{kc}"], writes=[f"sqb.{par}"])
        def f_st(t, kc=kc, par=par):
            insts = []
            for nh in range(2):
                insts.append(t.matmul(
                    pb[nh][0:1, :], ones128b[:, :], sqb[par][:, nh * 512:(nh + 1) * 512],
                    start=(kc == 0), stop=(kc == DC - 1)))
            return insts[-1]
        p.op("tensor", f_st, reads=[f"sqb.{par}", "ones128b"], writes=["pb.0", "pb.1"])
    for nh in range(2):
        p.op("vector", lambda v, nh=nh: v.tensor_copy(rrow[:, nh * 512:(nh + 1) * 512], pb[nh][0:1, :]),
             reads=[f"pb.{nh}"], writes=["ROWA"])
    p.op("vector", lambda v: v.tensor_scalar_mul(rrow[:, :], rrow[:, :], 1.0 / D),
         reads=["ROWA"], writes=["ROWA"])
    p.op("scalar", lambda s: s.activation(rrow[:, :], rrow[:, :], AF.Sqrt, bias=consts[0:1, 0:1]),
         reads=["ROWA", "consts"], writes=["ROWA"])
    p.op("vector", lambda v: v.reciprocal(rrow[:, :], rrow[:, :]),
         reads=["ROWA"], writes=["ROWA"])
    def f_bc(t):
        for nh in range(2):
            inst = t.matmul(pb[2 + nh][:, :], onesrow[:, :], rrow[:, nh * 512:(nh + 1) * 512],
                            start=True, stop=True)
        return inst
    p.op("tensor", f_bc, reads=["ROWA", "onesrow"], writes=["pb.2", "pb.3"])
    for nh in range(2):
        p.op("vector", lambda v, nh=nh: v.tensor_copy(rb[:, nh * 512:(nh + 1) * 512], pb[2 + nh][:, :]),
             reads=[f"pb.{2 + nh}"], writes=["rb"])
    for kc in range(DC):
        p.op("vector", lambda v, kc=kc: v.scalar_tensor_tensor(
                ybuf[:, kc * T:(kc + 1) * T], zbuf[:, kc * T:(kc + 1) * T],
                rmsws[:, kc:kc + 1], rb[:, :], ALU.mult, ALU.mult),
             reads=[f"z.{kc}", "rmsws", "rb"], writes=[f"yn.{kc}"])

    for oc in range(4):
        gpar = oc % 2
        p.dma("gpsimd", lambda g, oc=oc, gpar=gpar: g.dma_start(
            g0sl[gpar][:, :].rearrange("p (tt o) -> p tt o", o=512),
            g0.rearrange("(tt p) d -> p tt d", p=128)[:, :, oc * 512:(oc + 1) * 512]),
            writes=[f"g0sl.{gpar}"], sem=f"g0sl{gpar}")
        for kcg in range(2):
            par = kcg % 2
            p.dma("gpsimd", lambda g, oc=oc, kcg=kcg, par=par: g.dma_start(
                wsl[par][:, :].rearrange("p (kc o) -> p kc o", o=512),
                w_out[oc, kcg * 8:(kcg + 1) * 8].rearrange("kc k o -> k kc o")),
                writes=[f"wsl.{par}"], sem=f"wsl{par}")
            def f_mm(t, oc=oc, kcg=kcg, par=par):
                for kci in range(8):
                    kc = kcg * 8 + kci
                    for tt in range(TTI):
                        inst = t.matmul(
                            pb[tt][:, :],
                            ybuf[:, kc * T + tt * 128: kc * T + (tt + 1) * 128],
                            wsl[par][:, kci * 512:(kci + 1) * 512],
                            start=(kc == 0), stop=(kc == DC - 1))
                return inst
            p.op("tensor", f_mm,
                 reads=[f"wsl.{par}"] + [f"yn.{kcg * 8 + j}" for j in range(8)],
                 writes=[f"pb.{tt}" for tt in range(TTI)])
        for tt in range(TTI):
            p.op("vector", lambda v, oc=oc, tt=tt: v.tensor_mul(
                    tmp[:, :], pb[tt][:, :],
                    g0sl[oc % 2][:, tt * 512:(tt + 1) * 512]),
                 reads=[f"pb.{tt}", f"g0sl.{oc % 2}"], writes=["tmp"])
            p.op("vector", lambda v, oc=oc, tt=tt: v.tensor_add(
                    xt[:, tt * D + oc * 512: tt * D + (oc + 1) * 512],
                    xt[:, tt * D + oc * 512: tt * D + (oc + 1) * 512], tmp[:, :]),
                 reads=["tmp", f"xt.{tt}"], writes=[f"xt.{tt}"])

    for tt in range(TTI):
        p.dma("sync", lambda s, tt=tt: s.dma_start(
            x1[tt * 128:(tt + 1) * 128, :], xt[:, tt * D:(tt + 1) * D]),
            reads=[f"xt.{tt}"], writes=[f"x1out.{tt}"], sem="x1out")
        emit_ln_tm(p, xt[:, tt * D:(tt + 1) * D], [f"xt.{tt}"],
                   xnb_s[:, :], "xnb_s", stats, tt, sq,
                   eps_col=consts[:, 1:2], pref="ln1")
        for half in range(2):
            bank = pb[half].bitcast(BF16)
            bkey = f"pb.{half}"
            def f_tr(t, tt=tt, half=half, bank=bank):
                insts = []
                for j in range(8):
                    kc = half * 8 + j
                    insts.append(t.transpose(
                        bank[:, j * 128:(j + 1) * 128],
                        xnb_s[:, kc * 128:(kc + 1) * 128], identb[:, :]))
                return insts
            p.op("tensor", f_tr, reads=["xnb_s", "identb"], writes=[bkey])
            p.op("vector", lambda v, half=half, bank=bank: v.tensor_copy(
                    xn1st[:, half * 1024:(half + 1) * 1024], bank[:, 0:1024]),
                 reads=[bkey], writes=[f"xn1st.{half}"])
        p.dma("sync", lambda s, tt=tt: s.dma_start(
            xn1b_fm.rearrange("(kc p) t -> p kc t", p=128)[:, :, tt * 128:(tt + 1) * 128],
            xn1st[:, :].rearrange("p (kc t) -> p kc t", t=128)),
            reads=["xn1st.0", "xn1st.1"], writes=[f"xn1out.{tt}"], sem="xn1out")

    p.barrier_all("sync")
    p.finalize()
    ctx.close()
    return nc


def tile_w_in_out_wide(w, ow=512):
    K, O = w.shape
    return np.ascontiguousarray(
        w.reshape(K // 128, 128, O // ow, ow).transpose(2, 0, 1, 3))


def build_launch_C2():
    nc = bass.Bass("TRN2", target_bir_lowering=False)
    xn1b_fm = nc.dram_tensor("xn1b_fm", [D, T], BF16, kind="ExternalInput").ap()
    w1 = nc.dram_tensor("w1", [4, DC, 128, 512], F32, kind="ExternalInput").ap()
    lnwb = nc.dram_tensor("lnwb", [128, 2 * DC], F32, kind="ExternalInput").ap()
    gamrow = nc.dram_tensor("gamrow", [1, 2 * D], F32, kind="ExternalInput").ap()
    g1 = nc.dram_tensor("g1", [T, D], BF16, kind="ExternalOutput").ap()

    ctx = ExitStack()
    xfm = ctx.enter_context(nc.sbuf_tensor("xfm", [128, DC * T], BF16))
    lnx = ctx.enter_context(nc.sbuf_tensor("lnx", [128, DC * T], BF16))
    wsl = [ctx.enter_context(nc.sbuf_tensor(f"wsl{j}", [128, 8 * 512], BF16)) for j in range(2)]
    ssl = [ctx.enter_context(nc.sbuf_tensor(f"ssl{j}", [128, 8 * 512], BF16)) for j in range(2)]
    lnwbs = ctx.enter_context(nc.sbuf_tensor("lnwbs", [128, 2 * DC], F32))
    grows = ctx.enter_context(nc.sbuf_tensor("grows", [1, 2 * D], F32))
    grep = ctx.enter_context(nc.sbuf_tensor("grep", [128, D], F32))
    brep = ctx.enter_context(nc.sbuf_tensor("brep", [128, D], F32))
    sq = ctx.enter_context(nc.sbuf_tensor("sq", [128, T], BF16))
    ones128b = ctx.enter_context(nc.sbuf_tensor("ones128b", [128, 1], BF16))
    onesrow = ctx.enter_context(nc.sbuf_tensor("onesrow", [1, 128], F32))
    srow = ctx.enter_context(nc.sbuf_tensor("srow", [1, 2 * T], F32))
    r2row = ctx.enter_context(nc.sbuf_tensor("ROWA", [1, T], F32))
    mrrow = ctx.enter_context(nc.sbuf_tensor("ROWA", [1, T], F32))
    r2b = ctx.enter_context(nc.sbuf_tensor("r2b", [128, T], F32))
    mrb = ctx.enter_context(nc.sbuf_tensor("mrb", [128, T], F32))
    amaxc = ctx.enter_context(nc.sbuf_tensor("amaxc", [128, 1], F32))
    amrow = ctx.enter_context(nc.sbuf_tensor("ROWA", [1, 128], F32))
    scal1 = ctx.enter_context(nc.sbuf_tensor("ROWA", [1, 1], F32))
    scalc = ctx.enter_context(nc.sbuf_tensor("scalc", [128, 1], F32))
    identf = ctx.enter_context(nc.sbuf_tensor("identf", [128, 128], F32))
    ident = nc.dram_tensor("ident", [128, 128], F32, kind="ExternalInput").ap()
    consts = ctx.enter_context(nc.sbuf_tensor("consts", [128, 4], F32))
    tmp = ctx.enter_context(nc.sbuf_tensor("tmp", [128, T], F32))
    tmp2 = ctx.enter_context(nc.sbuf_tensor("tmp2", [128, 512], F32))
    stage = ctx.enter_context(nc.sbuf_tensor("stage", [128, 512], BF16))
    pb = [ctx.enter_context(nc.psum_tensor(f"pb{j}", [128, 512], F32)) for j in range(8)]

    p = Prog(nc, ctx)
    p.op("vector", lambda v: v.memset(consts[:, 0:1], 1e-5), writes=["consts"])
    p.op("vector", lambda v: v.memset(ones128b[:, :], 1.0), writes=["ones128b"])
    p.op("vector", lambda v: v.memset(onesrow[:, :], 1.0), writes=["onesrow"])
    p.op("vector", lambda v: v.memset(amaxc[:, :], 0.0), writes=["amaxc"])
    p.dma("gpsimd", lambda g: g.dma_start(
        xfm[:, :].rearrange("p (kc t) -> p kc t", t=T),
        xn1b_fm.rearrange("(kc p) t -> p kc t", p=128)),
        writes=[f"x.{kc}" for kc in range(DC)], sem="x")
    p.dma("gpsimd", lambda g: g.dma_start(lnwbs[:, :], lnwb[:, :]), writes=["lnwbs"], sem="lnwbs")
    p.dma("gpsimd", lambda g: g.dma_start(grows[:, :], gamrow[:, :]), writes=["grows"], sem="grows")
    p.dma("gpsimd", lambda g: g.dma_start(identf[:, :], ident[:, :]), writes=["identf"], sem="identf")

    def f_grep(t):
        for i in range(4):
            t.matmul(pb[i % 4][:, :], onesrow[:, :], grows[0:1, i * 512:(i + 1) * 512],
                     start=True, stop=True)
            inst = t.matmul(pb[4 + i % 4][:, :], onesrow[:, :], grows[0:1, D + i * 512: D + (i + 1) * 512],
                            start=True, stop=True)
        return inst
    p.op("tensor", f_grep, reads=["grows", "onesrow"], writes=[f"pb.{j}" for j in range(8)])
    for i in range(4):
        p.op("vector", lambda v, i=i: v.tensor_copy(grep[:, i * 512:(i + 1) * 512], pb[i][:, :]),
             reads=[f"pb.{i}"], writes=["grep"])
        p.op("vector", lambda v, i=i: v.tensor_copy(brep[:, i * 512:(i + 1) * 512], pb[4 + i][:, :]),
             reads=[f"pb.{4 + i}"], writes=["brep"])

    for kc in range(DC):
        p.op("scalar", lambda s, kc=kc: s.activation(sq[:, :], xfm[:, kc * T:(kc + 1) * T], AF.Square),
             reads=[f"x.{kc}"], writes=["sq"])
        def f_st(t, kc=kc):
            for nh in range(2):
                t.matmul(pb[nh][0:1, :], ones128b[:, :],
                         xfm[:, kc * T + nh * 512: kc * T + (nh + 1) * 512],
                         start=(kc == 0), stop=(kc == DC - 1))
                inst = t.matmul(pb[2 + nh][0:1, :], ones128b[:, :], sq[:, nh * 512:(nh + 1) * 512],
                                start=(kc == 0), stop=(kc == DC - 1))
            return inst
        p.op("tensor", f_st, reads=[f"x.{kc}", "sq", "ones128b"],
             writes=["pb.0", "pb.1", "pb.2", "pb.3"])
    for nh in range(2):
        p.op("vector", lambda v, nh=nh: v.tensor_copy(srow[:, nh * 512:(nh + 1) * 512], pb[nh][0:1, :]),
             reads=[f"pb.{nh}"], writes=["srow"])
        p.op("vector", lambda v, nh=nh: v.tensor_copy(srow[:, T + nh * 512: T + (nh + 1) * 512], pb[2 + nh][0:1, :]),
             reads=[f"pb.{2 + nh}"], writes=["srow"])
    p.op("vector", lambda v: v.tensor_scalar_mul(srow[:, :], srow[:, :], 1.0 / D),
         reads=["srow"], writes=["srow"])
    p.op("vector", lambda v: v.tensor_mul(r2row[:, :], srow[:, 0:T], srow[:, 0:T]),
         reads=["srow"], writes=["ROWA"])
    p.op("vector", lambda v: v.tensor_sub(r2row[:, :], r2row[:, :], srow[:, T:2 * T]),
         reads=["srow", "ROWA"], writes=["ROWA"])
    p.op("scalar", lambda s: s.activation(r2row[:, :], r2row[:, :], AF.Sqrt,
                                          bias=consts[0:1, 0:1], scale=-1.0),
         reads=["ROWA", "consts"], writes=["ROWA"])
    p.op("vector", lambda v: v.reciprocal(r2row[:, :], r2row[:, :]),
         reads=["ROWA"], writes=["ROWA"])
    p.op("vector", lambda v: v.tensor_mul(mrrow[:, :], srow[:, 0:T], r2row[:, :]),
         reads=["srow", "ROWA"], writes=["ROWA"])
    def f_bc(t):
        for nh in range(2):
            t.matmul(pb[4 + nh][:, :], onesrow[:, :], r2row[0:1, nh * 512:(nh + 1) * 512],
                     start=True, stop=True)
            inst = t.matmul(pb[6 + nh][:, :], onesrow[:, :], mrrow[0:1, nh * 512:(nh + 1) * 512],
                            start=True, stop=True)
        return inst
    p.op("tensor", f_bc, reads=["ROWA", "ROWA", "onesrow"],
         writes=["pb.4", "pb.5", "pb.6", "pb.7"])
    for nh in range(2):
        p.op("vector", lambda v, nh=nh: v.tensor_copy(r2b[:, nh * 512:(nh + 1) * 512], pb[4 + nh][:, :]),
             reads=[f"pb.{4 + nh}"], writes=["r2b"])
        p.op("vector", lambda v, nh=nh: v.tensor_copy(mrb[:, nh * 512:(nh + 1) * 512], pb[6 + nh][:, :]),
             reads=[f"pb.{6 + nh}"], writes=["mrb"])

    for kc in range(DC):
        p.op("vector", lambda v, kc=kc: v.tensor_mul(tmp[:, :], xfm[:, kc * T:(kc + 1) * T], r2b[:, :]),
             reads=[f"x.{kc}", "r2b"], writes=["tmpA"])
        p.op("vector", lambda v, kc=kc: v.tensor_sub(tmp[:, :], tmp[:, :], mrb[:, :]),
             reads=["tmpA", "mrb"], writes=["tmpA"])
        p.op("vector", lambda v, kc=kc: v.tensor_scalar(
                lnx[:, kc * T:(kc + 1) * T], tmp[:, :],
                lnwbs[:, kc:kc + 1], lnwbs[:, DC + kc: DC + kc + 1], ALU.mult, ALU.add),
             reads=["tmpA", "lnwbs"], writes=[f"lnx.{kc}"])

    raw = ctx.enter_context(nc.sbuf_tensor("raw", [128, TTI * D], BF16))
    amx_t = ctx.enter_context(nc.sbuf_tensor("amx_t", [128, 1], F32))
    for oc in range(4):
        for kcg in range(2):
            par = kcg % 2
            p.dma("gpsimd", lambda g, oc=oc, kcg=kcg, par=par: g.dma_start(
                wsl[par][:, :].rearrange("p (kc o) -> p kc o", o=512),
                w1[oc, kcg * 8:(kcg + 1) * 8].rearrange("kc k o -> k kc o")),
                writes=[f"wsl.{par}"], sem=f"wsl{par}")
            p.op("vector", lambda v, par=par: v.reduce_max(
                    amx_t[:, :], wsl[par][:, :], axis=AX.X, apply_absolute_value=True),
                 reads=[f"wsl.{par}"], writes=["amx_t"])
            p.op("vector", lambda v: v.tensor_max(amaxc[:, :], amaxc[:, :], amx_t[:, :]),
                 reads=["amx_t", "amaxc"], writes=["amaxc"])
            p.op("scalar", lambda s, par=par: s.activation(ssl[par][:, :], wsl[par][:, :], AF.Sign),
                 reads=[f"wsl.{par}"], writes=[f"ssl.{par}"])
            def f_mm(t, oc=oc, kcg=kcg, par=par):
                for kci in range(8):
                    kc = kcg * 8 + kci
                    for tt in range(TTI):
                        inst = t.matmul(
                            pb[tt][:, :],
                            lnx[:, kc * T + tt * 128: kc * T + (tt + 1) * 128],
                            ssl[par][:, kci * 512:(kci + 1) * 512],
                            start=(kc == 0), stop=(kc == DC - 1))
                return inst
            p.op("tensor", f_mm,
                 reads=[f"ssl.{par}"] + [f"lnx.{kcg * 8 + j}" for j in range(8)],
                 writes=[f"pb.{tt}" for tt in range(TTI)])
        for tt in range(TTI):
            p.op("vector", lambda v, oc=oc, tt=tt: v.tensor_copy(
                    raw[:, tt * D + oc * 512: tt * D + (oc + 1) * 512], pb[tt][:, :]),
                 reads=[f"pb.{tt}"], writes=[f"raw.{oc}.{tt}"])

    p.op("tensor", lambda t: t.transpose(pb[0][0:1, 0:128], amaxc[:, 0:1], identf[:, :]),
         reads=["amaxc", "identf"], writes=["pb.0"])
    p.op("vector", lambda v: v.tensor_copy(amrow[:, :], pb[0][0:1, 0:128]),
         reads=["pb.0"], writes=["ROWA"])
    p.op("vector", lambda v: v.reduce_max(scal1[:, :], amrow[:, :], axis=AX.X),
         reads=["ROWA"], writes=["ROWA"])
    p.op("tensor", lambda t: t.matmul(pb[1][:, 0:1], onesrow[:, :], scal1[:, :], start=True, stop=True),
         reads=["ROWA", "onesrow"], writes=["pb.1"])
    p.op("vector", lambda v: v.tensor_copy(scalc[:, :], pb[1][:, 0:1]),
         reads=["pb.1"], writes=["scalc"])
    p.op("vector", lambda v: v.tensor_scalar_mul(grep[:, :], grep[:, :], scalc[:, 0:1]),
         reads=["grep", "scalc"], writes=["grep"])

    for tt in range(TTI):
        for oc in range(4):
            p.op("vector", lambda v, tt=tt, oc=oc: v.tensor_mul(
                    tmp2[:, :], raw[:, tt * D + oc * 512: tt * D + (oc + 1) * 512],
                    grep[:, oc * 512:(oc + 1) * 512]),
                 reads=[f"raw.{oc}.{tt}", "grep"], writes=["tmp2"])
            p.op("vector", lambda v, oc=oc: v.tensor_add(
                    tmp2[:, :], tmp2[:, :], brep[:, oc * 512:(oc + 1) * 512]),
                 reads=["tmp2", "brep"], writes=["tmp2"])
            p.op("scalar", lambda s: s.activation(stage[:, :], tmp2[:, :], AF.Silu),
                 reads=["tmp2"], writes=["stage"])
            p.dma("sync", lambda s, tt=tt, oc=oc: s.dma_start(
                g1[tt * 128:(tt + 1) * 128, oc * 512:(oc + 1) * 512], stage[:, :]),
                reads=["stage"], writes=[f"g1out.{tt}.{oc}"], sem="g1out")

    p.barrier_all("sync")
    p.finalize()
    ctx.close()
    return nc


def build_launch_D():
    nc = bass.Bass("TRN2", target_bir_lowering=False)
    xn1_fm = nc.dram_tensor("xn1_fm", [D, SEQ], BF16, kind="ExternalInput").ap()
    wqkv = nc.dram_tensor("wqkv", [8, 3, DC, 128, 128], F32, kind="ExternalInput").ap()
    lns_i = nc.dram_tensor("lns", [128, 2], F32, kind="ExternalInput").ap()
    cs2 = nc.dram_tensor("cs2", [128, 2 * SEQ], F32, kind="ExternalInput").ap()
    mask4 = nc.dram_tensor("mask4", [128, 4 * 512], F32, kind="ExternalInput").ap()
    swp = nc.dram_tensor("swp", [128, 128], F32, kind="ExternalInput").ap()
    attn = nc.dram_tensor("attn", [1024, SEQ], BF16, kind="ExternalOutput").ap()

    ctx = ExitStack()
    xfm = ctx.enter_context(nc.sbuf_tensor("xfm", [128, DC * SEQ], BF16))
    v_all = ctx.enter_context(nc.sbuf_tensor("v_all", [128, 16 * 8 * 128], BF16))
    qb = ctx.enter_context(nc.sbuf_tensor("qb", [128, SEQ], BF16))
    kb = ctx.enter_context(nc.sbuf_tensor("kb", [128, SEQ], BF16))
    wslb = [ctx.enter_context(nc.sbuf_tensor(f"wslb{j}", [128, 3 * DC * 128], BF16)) for j in range(2)]
    cs2s = ctx.enter_context(nc.sbuf_tensor("cs2s", [128, 2 * SEQ], F32))
    mask4s = ctx.enter_context(nc.sbuf_tensor("mask4s", [128, 4 * 512], F32))
    lnss = ctx.enter_context(nc.sbuf_tensor("lnss", [128, 2], F32))
    ra = ctx.enter_context(nc.sbuf_tensor("ra", [128, 512], F32))
    rb_ = ctx.enter_context(nc.sbuf_tensor("rb_", [128, 512], BF16))
    swps = ctx.enter_context(nc.sbuf_tensor("swps", [128, 128], BF16))
    cbf = ctx.enter_context(nc.sbuf_tensor("cbf", [128, 512], F32))
    cbb = ctx.enter_context(nc.sbuf_tensor("cbb", [128, 512], BF16))
    sqh = ctx.enter_context(nc.sbuf_tensor("sqh", [128, 512], BF16))
    srow2 = ctx.enter_context(nc.sbuf_tensor("ROWA", [1, 1024], F32))
    rrow2 = ctx.enter_context(nc.sbuf_tensor("ROWA", [1, 512], F32))
    mrrow2 = ctx.enter_context(nc.sbuf_tensor("ROWA", [1, 512], F32))
    r2b = ctx.enter_context(nc.sbuf_tensor("r2b", [128, 512], F32))
    mrb = ctx.enter_context(nc.sbuf_tensor("mrb", [128, 512], F32))
    sa = ctx.enter_context(nc.sbuf_tensor("sa", [128, 512], F32))
    a4b = ctx.enter_context(nc.sbuf_tensor("a4b", [128, 512], BF16))
    drow = ctx.enter_context(nc.sbuf_tensor("ROWA", [1, 512], F32))
    rbq = ctx.enter_context(nc.sbuf_tensor("rbq", [128, 512], F32))
    attnb = ctx.enter_context(nc.sbuf_tensor("attnb", [128, SEQ], BF16))
    ones128b = ctx.enter_context(nc.sbuf_tensor("ones128b", [128, 1], BF16))
    onesrow = ctx.enter_context(nc.sbuf_tensor("onesrow", [1, 128], F32))
    consts = ctx.enter_context(nc.sbuf_tensor("consts", [128, 4], F32))
    pb = [ctx.enter_context(nc.psum_tensor(f"pb{j}", [128, 512], F32)) for j in range(8)]

    p = Prog(nc, ctx)
    p.op("vector", lambda v: v.memset(consts[:, 0:1], 1e-5), writes=["consts"])
    p.op("vector", lambda v: v.memset(ones128b[:, :], 1.0), writes=["ones128b"])
    p.op("vector", lambda v: v.memset(onesrow[:, :], 1.0), writes=["onesrow"])
    p.dma("gpsimd", lambda g: g.dma_start(
        xfm[:, :].rearrange("p (kc t) -> p kc t", t=SEQ),
        xn1_fm.rearrange("(kc p) t -> p kc t", p=128)),
        writes=[f"x.{kc}" for kc in range(DC)], sem="x")
    p.dma("gpsimd", lambda g: g.dma_start(cs2s[:, :], cs2[:, :]), writes=["cs2s"], sem="cs2s")
    p.dma("gpsimd", lambda g: g.dma_start(mask4s[:, :], mask4[:, :]), writes=["mask4s"], sem="mask4s")
    p.dma("gpsimd", lambda g: g.dma_start(lnss[:, :], lns_i[:, :]), writes=["lnss"], sem="lnss")
    p.dma("gpsimd", lambda g: g.dma_start(swps[:, :], swp[:, :]), writes=["swps"], sem="swps")

    for hl in range(8):
        par = hl % 2
        p.dma("gpsimd", lambda g, hl=hl, par=par: g.dma_start(
            wslb[par][:, :].rearrange("p (s kc o) -> p s kc o", kc=DC, o=128).rearrange("p s kc o -> p s kc o"),
            wqkv[hl].rearrange("s kc k o -> k s kc o")),
            writes=[f"wq.{par}"], sem=f"wq{par}")
        for tth in range(2):
            def f_v(t, hl=hl, par=par, tth=tth):
                for kc in range(DC):
                    for tt in range(TTI):
                        inst = t.matmul(
                            pb[tt][:, 0:128],
                            xfm[:, kc * SEQ + (tth * 8 + tt) * 128: kc * SEQ + (tth * 8 + tt + 1) * 128],
                            wslb[par][:, (2 * DC + kc) * 128:(2 * DC + kc + 1) * 128],
                            start=(kc == 0), stop=(kc == DC - 1))
                return inst
            p.op("tensor", f_v, reads=[f"wq.{par}"] + [f"x.{kc}" for kc in range(DC)],
                 writes=[f"pb.{tt}" for tt in range(TTI)])
            for tt in range(TTI):
                p.op("vector", lambda v, hl=hl, tth=tth, tt=tt: v.tensor_copy(
                        v_all[:, ((tth * 8 + tt) * 8 + hl) * 128: ((tth * 8 + tt) * 8 + hl + 1) * 128],
                        pb[tt][:, 0:128]),
                     reads=[f"pb.{tt}"], writes=[f"v.{hl}.{tth * 8 + tt}"])
        for side, dst in ((0, qb), (1, kb)):
            dk = "qb" if side == 0 else "kb"
            for nq in range(4):
                def f_qk(t, hl=hl, par=par, side=side, nq=nq):
                    for kc in range(DC):
                        inst = t.matmul(
                            pb[4][:, :],
                            wslb[par][:, (side * DC + kc) * 128:(side * DC + kc + 1) * 128],
                            xfm[:, kc * SEQ + nq * 512: kc * SEQ + (nq + 1) * 512],
                            start=(kc == 0), stop=(kc == DC - 1))
                    return inst
                p.op("tensor", f_qk, reads=[f"wq.{par}"] + [f"x.{kc}" for kc in range(DC)],
                     writes=["pb.4"])
                p.op("vector", lambda v, nq=nq: v.tensor_mul(
                        ra[:, :], pb[4][:, :], cs2s[:, nq * 512:(nq + 1) * 512]),
                     reads=["pb.4", "cs2s"], writes=["ra"])
                p.op("vector", lambda v, nq=nq: v.tensor_mul(
                        rb_[:, :], pb[4][:, :], cs2s[:, SEQ + nq * 512: SEQ + (nq + 1) * 512]),
                     reads=["pb.4", "cs2s"], writes=["rb_"])
                p.op("tensor", lambda t: t.matmul(pb[7][:, :], swps[:, :], rb_[:, :],
                                                  start=True, stop=True),
                     reads=["rb_", "swps"], writes=["pb.7"])
                p.op("vector", lambda v: v.tensor_sub(cbf[0:64, :], ra[0:64, :], pb[7][0:64, :]),
                     reads=["ra", "pb.7"], writes=["cbf"])
                p.op("vector", lambda v: v.tensor_add(cbf[64:128, :], ra[64:128, :], pb[7][64:128, :]),
                     reads=["ra", "pb.7"], writes=["cbf"])
                p.op("vector", lambda v: v.tensor_copy(cbb[:, :], cbf[:, :]),
                     reads=["cbf"], writes=["cbb"])
                p.op("scalar", lambda s: s.activation(sqh[:, :], cbf[:, :], AF.Square),
                     reads=["cbf"], writes=["sqh"])
                def f_st(t):
                    t.matmul(pb[5][0:1, :], ones128b[:, :], cbb[:, :], start=True, stop=True)
                    return t.matmul(pb[6][0:1, :], ones128b[:, :], sqh[:, :], start=True, stop=True)
                p.op("tensor", f_st, reads=["cbb", "sqh", "ones128b"], writes=["pb.5", "pb.6"])
                p.op("vector", lambda v: v.tensor_copy(srow2[:, 0:512], pb[5][0:1, :]),
                     reads=["pb.5"], writes=["ROWA"])
                p.op("vector", lambda v: v.tensor_copy(srow2[:, 512:1024], pb[6][0:1, :]),
                     reads=["pb.6"], writes=["ROWA"])
                p.op("vector", lambda v: v.tensor_scalar_mul(srow2[:, :], srow2[:, :], 1.0 / 128),
                     reads=["ROWA"], writes=["ROWA"])
                p.op("vector", lambda v: v.tensor_mul(rrow2[:, :], srow2[:, 0:512], srow2[:, 0:512]),
                     reads=["ROWA"], writes=["ROWA"])
                p.op("vector", lambda v: v.tensor_sub(rrow2[:, :], rrow2[:, :], srow2[:, 512:1024]),
                     reads=["ROWA", "ROWA"], writes=["ROWA"])
                p.op("scalar", lambda s: s.activation(rrow2[:, :], rrow2[:, :], AF.Sqrt,
                                                      bias=consts[0:1, 0:1], scale=-1.0),
                     reads=["ROWA", "consts"], writes=["ROWA"])
                p.op("vector", lambda v: v.reciprocal(rrow2[:, :], rrow2[:, :]),
                     reads=["ROWA"], writes=["ROWA"])
                p.op("vector", lambda v: v.tensor_mul(mrrow2[:, :], srow2[:, 0:512], rrow2[:, :]),
                     reads=["ROWA", "ROWA"], writes=["ROWA"])
                def f_bc(t):
                    t.matmul(pb[5][:, :], onesrow[:, :], rrow2[0:1, :], start=True, stop=True)
                    return t.matmul(pb[6][:, :], onesrow[:, :], mrrow2[0:1, :], start=True, stop=True)
                p.op("tensor", f_bc, reads=["ROWA", "ROWA", "onesrow"], writes=["pb.5", "pb.6"])
                p.op("vector", lambda v: v.tensor_copy(r2b[:, :], pb[5][:, :]),
                     reads=["pb.5"], writes=["r2b"])
                p.op("vector", lambda v: v.tensor_copy(mrb[:, :], pb[6][:, :]),
                     reads=["pb.6"], writes=["mrb"])
                p.op("vector", lambda v: v.tensor_mul(cbf[:, :], cbf[:, :], r2b[:, :]),
                     reads=["cbf", "r2b"], writes=["cbf"])
                p.op("vector", lambda v: v.tensor_sub(cbf[:, :], cbf[:, :], mrb[:, :]),
                     reads=["cbf", "mrb"], writes=["cbf"])
                p.op("vector", lambda v, dst=dst, nq=nq: v.tensor_scalar(
                        dst[:, nq * 512:(nq + 1) * 512], cbf[:, :],
                        lnss[:, 0:1], lnss[:, 1:2], ALU.mult, ALU.add),
                     reads=["cbf", "lnss"], writes=[f"{dk}.{nq}"])
        for qs in range(4):
            for kt in range(qs * 4 + 4):
                spar = kt % 2
                p.op("tensor", lambda t, hl=hl, qs=qs, kt=kt, spar=spar: t.matmul(
                        pb[spar][:, :], kb[:, kt * 128:(kt + 1) * 128],
                        qb[:, qs * 512:(qs + 1) * 512], start=True, stop=True),
                     reads=[f"kb.{kt // 4}", f"qb.{qs}"], writes=[f"pb.{spar}"])
                p.op("scalar", lambda s, spar=spar: s.activation(sa[:, :], pb[spar][:, :], AF.Square),
                     reads=[f"pb.{spar}"], writes=["sa"])
                if kt >= qs * 4:
                    off = kt - qs * 4
                    p.op("vector", lambda v, off=off: v.tensor_mul(
                            sa[:, :], sa[:, :], mask4s[:, off * 512:(off + 1) * 512]),
                         reads=["sa", "mask4s"], writes=["sa"])
                p.op("vector", lambda v: v.tensor_mul(a4b[:, :], sa[:, :], sa[:, :]),
                     reads=["sa"], writes=["a4b"])
                def f_acc(t, hl=hl, qs=qs, kt=kt):
                    t.matmul(pb[2][0:1, :], ones128b[:, :], a4b[:, :],
                             start=(kt == 0), stop=(kt == qs * 4 + 3))
                    return t.matmul(pb[3][:, :], v_all[:, (kt * 8 + hl) * 128:(kt * 8 + hl + 1) * 128],
                                    a4b[:, :], start=(kt == 0), stop=(kt == qs * 4 + 3))
                p.op("tensor", f_acc, reads=["a4b", "ones128b", f"v.{hl}.{kt}"],
                     writes=["pb.2", "pb.3"])
            p.op("vector", lambda v: v.tensor_copy(drow[:, :], pb[2][0:1, :]),
                 reads=["pb.2"], writes=["ROWA"])
            p.op("vector", lambda v: v.reciprocal(drow[:, :], drow[:, :]),
                 reads=["ROWA"], writes=["ROWA"])
            p.op("tensor", lambda t: t.matmul(pb[4][:, :], onesrow[:, :], drow[0:1, :],
                                              start=True, stop=True),
                 reads=["ROWA", "onesrow"], writes=["pb.4"])
            p.op("vector", lambda v: v.tensor_copy(rbq[:, :], pb[4][:, :]),
                 reads=["pb.4"], writes=["rbq"])
            p.op("vector", lambda v, qs=qs: v.tensor_mul(
                    attnb[:, qs * 512:(qs + 1) * 512], pb[3][:, :], rbq[:, :]),
                 reads=["pb.3", "rbq"], writes=[f"attnb.{qs}"])
        p.dma("sync", lambda s, hl=hl: s.dma_start(
            attn[hl * 128:(hl + 1) * 128, :], attnb[:, :]),
            reads=[f"attnb.{qs}" for qs in range(4)], writes=[f"attnout.{hl}"], sem="attnout")

    p.barrier_all("sync")
    p.finalize()
    ctx.close()
    return nc


def build_launch_E():
    nc = bass.Bass("TRN2", target_bir_lowering=False)
    attn_sl = nc.dram_tensor("attn_sl", [D, T], BF16, kind="ExternalInput").ap()
    w_out1 = nc.dram_tensor("w_out1", [4, DC, 128, 512], F32, kind="ExternalInput").ap()
    x1 = nc.dram_tensor("x1", [T, D], F32, kind="ExternalInput").ap()
    g1 = nc.dram_tensor("g1", [T, D], BF16, kind="ExternalInput").ap()
    x2 = nc.dram_tensor("x2", [T, D], F32, kind="ExternalOutput").ap()

    ctx = ExitStack()
    abuf = ctx.enter_context(nc.sbuf_tensor("abuf", [128, DC * T], BF16))
    xt = ctx.enter_context(nc.sbuf_tensor("xt", [128, TTI * D], F32))
    g1sl = [ctx.enter_context(nc.sbuf_tensor(f"g1sl{j}", [128, TTI * 512], BF16)) for j in range(2)]
    wsl = [ctx.enter_context(nc.sbuf_tensor(f"wsl{j}", [128, 8 * 512], BF16)) for j in range(2)]
    tmp = ctx.enter_context(nc.sbuf_tensor("tmp", [128, 512], F32))
    pb = [ctx.enter_context(nc.psum_tensor(f"pb{j}", [128, 512], F32)) for j in range(8)]

    p = Prog(nc, ctx)
    p.dma("gpsimd", lambda g: g.dma_start(
        abuf[:, :].rearrange("p (kc t) -> p kc t", t=T),
        attn_sl.rearrange("(kc p) t -> p kc t", p=128)),
        writes=[f"a.{kc}" for kc in range(DC)], sem="a")
    p.dma("gpsimd", lambda g: g.dma_start(
        xt[:, :].rearrange("p (tt d) -> p tt d", d=D),
        x1.rearrange("(tt p) d -> p tt d", p=128)),
        writes=[f"xt.{tt}" for tt in range(TTI)], sem="xt")

    for oc in range(4):
        gpar = oc % 2
        p.dma("gpsimd", lambda g, oc=oc, gpar=gpar: g.dma_start(
            g1sl[gpar][:, :].rearrange("p (tt o) -> p tt o", o=512),
            g1.rearrange("(tt p) d -> p tt d", p=128)[:, :, oc * 512:(oc + 1) * 512]),
            writes=[f"g1sl.{gpar}"], sem=f"g1sl{gpar}")
        for kcg in range(2):
            par = kcg % 2
            p.dma("gpsimd", lambda g, oc=oc, kcg=kcg, par=par: g.dma_start(
                wsl[par][:, :].rearrange("p (kc o) -> p kc o", o=512),
                w_out1[oc, kcg * 8:(kcg + 1) * 8].rearrange("kc k o -> k kc o")),
                writes=[f"wsl.{par}"], sem=f"wsl{par}")
            def f_mm(t, oc=oc, kcg=kcg, par=par):
                for kci in range(8):
                    kc = kcg * 8 + kci
                    for tt in range(TTI):
                        inst = t.matmul(
                            pb[tt][:, :],
                            abuf[:, kc * T + tt * 128: kc * T + (tt + 1) * 128],
                            wsl[par][:, kci * 512:(kci + 1) * 512],
                            start=(kc == 0), stop=(kc == DC - 1))
                return inst
            p.op("tensor", f_mm,
                 reads=[f"wsl.{par}"] + [f"a.{kcg * 8 + j}" for j in range(8)],
                 writes=[f"pb.{tt}" for tt in range(TTI)])
        for tt in range(TTI):
            p.op("vector", lambda v, oc=oc, tt=tt: v.tensor_mul(
                    tmp[:, :], pb[tt][:, :],
                    g1sl[oc % 2][:, tt * 512:(tt + 1) * 512]),
                 reads=[f"pb.{tt}", f"g1sl.{oc % 2}"], writes=["tmp"])
            p.op("vector", lambda v, oc=oc, tt=tt: v.tensor_add(
                    xt[:, tt * D + oc * 512: tt * D + (oc + 1) * 512],
                    xt[:, tt * D + oc * 512: tt * D + (oc + 1) * 512], tmp[:, :]),
                 reads=["tmp", f"xt.{tt}"], writes=[f"xt.{tt}"])
    for tt in range(TTI):
        p.dma("sync", lambda s, tt=tt: s.dma_start(
            x2[tt * 128:(tt + 1) * 128, :], xt[:, tt * D:(tt + 1) * D]),
            reads=[f"xt.{tt}"], writes=[f"x2out.{tt}"], sem="x2out")

    p.barrier_all("sync")
    p.finalize()
    ctx.close()
    return nc


NT = 16
NCHL = 16


def build_fused():
    nc = bass.Bass("TRN2", target_bir_lowering=False)
    I32 = mybir.dt.int32
    ap = lambda n, s, d=F32: nc.dram_tensor(n, s, d, kind="ExternalInput").ap()
    xh = ap("xh", [T, D])
    w_in = ap("w_in", [64, DC, 128, 128])
    w_gu = ap("w_gu", [128, DC, 128, 128])
    w_d = ap("w_d", [4, 64, 128, 512])
    w_out = ap("w_out", [4, DC, 128, 512])
    w1 = ap("w1", [4, DC, 128, 512])
    wqkv = ap("wqkv", [8, 2, DC, 128, 128])
    wv_b = ap("wv_b", [2, DC, 128, 512])
    w_out1 = ap("w_out1", [4, DC, 128, 512])
    cw = ap("cw", [128, NT * 3 * 3])
    cb = ap("cb", [128, NT * 3])
    alog = ap("alog", [128, NT])
    ident = ap("ident", [128, 128])
    mask_ssd = ap("mask_ssd", [128, 64])
    rmsw = ap("rmsw", [128, DC])
    lnwb = ap("lnwb", [128, 2 * DC])
    gamrow = ap("gamrow", [1, 2 * D])
    lns_i = ap("lns", [128, 2])
    cs2 = ap("cs2", [128, 2 * SEQ])
    mask4 = ap("mask4", [128, 4 * 512])
    swp = ap("swp", [128, 128])
    cidx = ap("cidx", [128, NCHL])
    bmask = ap("bmask", [128, 4])
    hhoff = nc.dram_tensor("hhoff", [1, 1], I32, kind="ExternalInput").ap()
    x2 = nc.dram_tensor("x2", [T, D], F32, kind="ExternalOutput").ap()
    idt = lambda n, s, d: nc.dram_tensor(n, s, d)
    hb_in = idt("hb_in", [2, 3 * D], BF16)
    hb_out = idt("hb_out", [4, 3 * D], BF16)
    st_in = idt("st_in", [128, NT], BF16)
    st_out = idt("st_out", [256, NT], BF16)
    xn_in = idt("xn_in", [D, T], BF16)
    xn_out = idt("xn_out", [2 * D, T], BF16)
    at_in = idt("at_in", [1024, SEQ], BF16)
    at_out = idt("at_out", [2048, SEQ], BF16)
    x1_sp = idt("x1_sp", [T, D], F32)
    s0_sp = idt("s0_sp", [T, D], BF16)
    g0_sp = idt("g0_sp", [T, D], BF16)
    g1_sp = idt("g1_sp", [T, D], BF16)
    RG = [[0, 1], [2, 3], [4, 5], [6, 7]]

    ctx = ExitStack()
    R1 = ctx.enter_context(nc.sbuf_tensor("R1", [128, 2 * DC * T], BF16))
    RA = ctx.enter_context(nc.sbuf_tensor("RA", [128, DC * T], BF16))
    RB = ctx.enter_context(nc.sbuf_tensor("RB", [128, DC * T], BF16))
    RZ = ctx.enter_context(nc.sbuf_tensor("RZ", [128, DC * T], BF16))
    WS = ctx.enter_context(nc.sbuf_tensor("WS", [128, 8192], BF16))
    SC = ctx.enter_context(nc.sbuf_tensor("SC", [128, 2048], F32))
    PR = ctx.enter_context(nc.sbuf_tensor("PR", [128, 2048], BF16))
    identb = ctx.enter_context(nc.sbuf_tensor("identb", [128, 128], BF16))
    maskb = ctx.enter_context(nc.sbuf_tensor("maskb", [128, 64], BF16))
    swps = ctx.enter_context(nc.sbuf_tensor("swps", [128, 128], BF16))
    cwsb = ctx.enter_context(nc.sbuf_tensor("cwsb", [128, NT * 9], F32))
    cbsb = ctx.enter_context(nc.sbuf_tensor("cbsb", [128, NT * 3], F32))
    alogs = ctx.enter_context(nc.sbuf_tensor("alogs", [128, NT], F32))
    decay = ctx.enter_context(nc.sbuf_tensor("decay", [128, NT], F32))
    cidxs = ctx.enter_context(nc.sbuf_tensor("cidxs", [128, NCHL], F32))
    bmasks = ctx.enter_context(nc.sbuf_tensor("bmasks", [128, 4], F32))
    hhoffs = ctx.enter_context(nc.sbuf_tensor("hhoffs", [1, 1], I32))
    rmsws = ctx.enter_context(nc.sbuf_tensor("rmsws", [128, DC], F32))
    lnwbs = ctx.enter_context(nc.sbuf_tensor("lnwbs", [128, 2 * DC], F32))
    lnss = ctx.enter_context(nc.sbuf_tensor("lnss", [128, 2], F32))
    ROWA = ctx.enter_context(nc.sbuf_tensor("ROWA", [1, 3072], F32))
    hlbuf = ctx.enter_context(nc.sbuf_tensor("hlbuf", [128, 2 * NT * 3], F32))
    hstage = ctx.enter_context(nc.sbuf_tensor("hstage", [128, 112], BF16))
    prevb = ctx.enter_context(nc.sbuf_tensor("prevb", [128, NT * (NCHL + 1)], F32))
    cs_all = ctx.enter_context(nc.sbuf_tensor("cs_all", [128, NT * NCHL], F32))
    s0f = ctx.enter_context(nc.sbuf_tensor("s0f", [128, NT], F32))
    powm = ctx.enter_context(nc.sbuf_tensor("powm", [128, NCHL], F32))
    ones128b = ctx.enter_context(nc.sbuf_tensor("ones128b", [128, 1], BF16))
    onesrowb = ctx.enter_context(nc.sbuf_tensor("onesrowb", [1, 128], BF16))
    invrowb = ctx.enter_context(nc.sbuf_tensor("invrowb", [1, 128], BF16))
    consts = ctx.enter_context(nc.sbuf_tensor("consts", [128, 2], F32))
    rrow = ROWA[:, 0:2048]
    r2row = ROWA[:, 2048:3072]
    mrrow = ROWA[:, 0:1024]
    srow2 = ROWA[:, 0:1024]
    rrow2 = ROWA[:, 1024:1536]
    mrrow2 = ROWA[:, 1536:2048]
    drow = ROWA[:, 2048:2560]
    amrow = ROWA[:, 0:128]
    scal1 = ROWA[:, 128:129]
    amaxc = ctx.enter_context(nc.sbuf_tensor("amaxc", [128, 1], BF16))
    amx_t = ctx.enter_context(nc.sbuf_tensor("amx_t", [128, 1], BF16))
    scalc = ctx.enter_context(nc.sbuf_tensor("scalc", [128, 1], F32))
    stats = {k: ctx.enter_context(nc.sbuf_tensor(f"st_{k}", [128, TTI], F32))
             for k in ("m0", "s0", "m", "var", "rstd")}
    sq = ctx.enter_context(nc.sbuf_tensor("sqs", [128, 1024], BF16))
    pb = [ctx.enter_context(nc.psum_tensor(f"pb{j}", [128, 512], F32)) for j in range(8)]
    R1b = R1

    SCb = SC.bitcast(BF16)
    gt_sb = [WS[:, 0:512], WS[:, 2048:2560]]
    r2b7 = R1[:, 3 * T:4 * T]
    r2bb = RZ[:, 0:T]
    mrbb = RZ[:, T:2 * T]
    ROWAb2 = ROWA.bitcast(BF16)
    yoff_sb = [ROWAb2[:, i * 512:(i + 1) * 512] for i in range(4)]

    p = Prog(nc, ctx)

    def kR1(j):  return f"R1.{j}"
    def kRA(j):  return f"RA.{j}"
    def kRB(j):  return f"RB.{j}"
    def kRZ(j):  return f"RZ.{j}"
    def kWS(j):  return f"WS.{j}"
    def r1t(j):  return R1[:, j * T:(j + 1) * T]
    def rat(j):  return RA[:, j * T:(j + 1) * T]
    def rbt(j):  return RB[:, j * T:(j + 1) * T]
    def rzt(j):  return RZ[:, j * T:(j + 1) * T]

    p.op("vector", lambda v: v.memset(consts[:, 0:1], 1e-5), writes=["consts"])
    p.op("vector", lambda v: v.memset(consts[:, 1:2], 1.1920929e-07), writes=["consts"])
    p.op("vector", lambda v: v.memset(ones128b[:, :], 1.0), writes=["ones128b"])
    p.op("vector", lambda v: v.memset(onesrowb[:, :], 1.0), writes=["onesrowb"])
    p.op("vector", lambda v: v.memset(invrowb[:, :], 1.0 / 128), writes=["invrowb"])
    p.op("vector", lambda v: v.memset(amaxc[:, :], 0.0), writes=["amaxc"])
    p.op("vector", lambda v: v.memset(prevb[:, :], 0.0), writes=["prevb.z"])
    p.op("vector", lambda v: v.memset(pb[0][:, :], 0.0), writes=["pb.0"])
    p.op("vector", lambda v: v.memset(pb[1][:, :], 0.0), writes=["pb.1"])
    for nm, dst, src in (("identb", identb, ident),
                         ("maskb", maskb, mask_ssd), ("swps", swps, swp),
                         ("cwsb", cwsb, cw), ("cbsb", cbsb, cb), ("alogs", alogs, alog),
                         ("cidxs", cidxs, cidx), ("bmasks", bmasks, bmask),
                         ("rmsws", rmsws, rmsw), ("lnwbs", lnwbs, lnwb),
                         ("lnss", lnss, lns_i),
                         ("hhoffs", hhoffs, hhoff)):
        p.dma("gpsimd", lambda g, dst=dst, src=src: g.dma_start(dst[:, :], src[:, :]),
              writes=[nm], sem=nm)
    p.op("scalar", lambda s: s.activation(decay[:, :], alogs[:, :], AF.Exp, scale=63.0),
         reads=["alogs"], writes=["decay"])

    for tt in range(TTI):
        slot = tt % 2
        xsl = SC[:, 0:2048]
        lnout = R1[:, slot * 2048:(slot + 1) * 2048]
        lnkeys = [kR1(slot * 2), kR1(slot * 2 + 1)]
        p.dma("gpsimd", lambda g, tt=tt, xsl=xsl: g.dma_start(
            xsl, xh[tt * 128:(tt + 1) * 128, :]),
            writes=["SC.0", "SC.1"], sem="xl0")
        emit_ln_tm(p, xsl, ["SC.0", "SC.1"], lnout, lnkeys[0], stats, tt, sq,
                   eps_col=consts[:, 0:1], pref="ln0", sqk=("sq.0", "sq.1"))
        p.wr[lnkeys[1]] = list(p.wr.get(lnkeys[0], ()))
        for half in range(2):
            bank = pb[half].bitcast(BF16)
            bkey = f"pb.{half}"
            def f_tr(t, tt=tt, half=half, bank=bank, lnout=lnout):
                insts = []
                for j in range(8):
                    kc = half * 8 + j
                    insts.append(t.transpose(
                        bank[:, j * 128:(j + 1) * 128],
                        lnout[:, kc * 128:(kc + 1) * 128], identb[:, :]))
                return insts
            p.op("tensor", f_tr, reads=lnkeys + ["identb"], writes=[bkey])
            def f_ev(v, tt=tt, half=half, bank=bank):
                dst = RA[:, :].rearrange("p (kc t) -> p kc t", t=T)[
                    :, half * 8:(half + 1) * 8, tt * 128:(tt + 1) * 128]
                return v.tensor_copy(dst, bank[:, 0:1024].rearrange("p (kc t) -> p kc t", t=128))
            p.op("vector", f_ev, reads=[bkey],
                 writes=[kRA(half * 8 + j) for j in range(8)])
    xn0_keys = [kRA(j) for j in range(DC)]

    def inproj_dst(oc):
        if oc < 16:   return rzt(oc), kRZ(oc)
        return (r1t(oc - 16), kR1(oc - 16)) if oc < 48 else (rbt(oc - 48), kRB(oc - 48))
    def f_gb(v):
        for r, c in ((0, 0), (1, T - 1)):
            v.tensor_copy(hstage[:, r * 48: r * 48 + 32].unsqueeze(2),
                          R1[:, 0:32 * T].rearrange("p (j t) -> p j t", t=T)[:, :, c:c + 1])
            inst = v.tensor_copy(hstage[:, r * 48 + 32: r * 48 + 48].unsqueeze(2),
                                 RB[:, :].rearrange("p (j t) -> p j t", t=T)[:, :, c:c + 1])
        return inst
    for oc in list(range(16, 64)) + list(range(16)):
        par = oc % 2
        wsl_ = WS[:, par * 2048:(par + 1) * 2048]
        p.dma("gpsimd", lambda g, oc=oc, wsl_=wsl_: g.dma_start(
            wsl_.rearrange("p (kc o) -> p kc o", o=128),
            w_in[oc].rearrange("kc k o -> k kc o")),
            writes=[kWS(par)], sem=f"ws{par}")
        pbs, pbks = [pb[par * 2], pb[par * 2 + 1]], [f"pb.{par * 2}", f"pb.{par * 2 + 1}"]
        def f_mm(t, oc=oc, par=par, wsl_=wsl_, pbs=pbs):
            for kc in range(DC):
                for nh in range(2):
                    inst = t.matmul(
                        pbs[nh][:, :], wsl_[:, kc * 128:(kc + 1) * 128],
                        RA[:, kc * T + nh * 512: kc * T + (nh + 1) * 512],
                        start=(kc == 0), stop=(kc == DC - 1))
            return inst
        p.op("tensor", f_mm, reads=[kWS(par)] + xn0_keys, writes=pbks)
        dst, dkey = inproj_dst(oc)
        for nh in range(2):
            p.op("vector", lambda v, dst=dst, nh=nh, pbs=pbs: v.tensor_copy(
                dst[:, nh * 512:(nh + 1) * 512], pbs[nh][:, :]),
                reads=[pbks[nh]], writes=[dkey])
        if oc == 63:
            p.op("vector", f_gb,
                 reads=[kR1(j) for j in range(32)] + [kRB(j) for j in range(16)],
                 writes=["hstage"])
            p.dma("sync", lambda g: g.dma_start(
                hb_in.ap().rearrange("r (p t) -> p r t", p=128),
                hstage[:, 0:96].rearrange("p (r t) -> p r t", t=48)),
                reads=["hstage"], writes=["hb_in"], sem="hbi")
    p.coll(lambda g: g.collective_compute(
        "AllGather", ALU.bypass, replica_groups=RG,
        ins=[hb_in.ap().opt()], outs=[hb_out.ap().opt()]),
        reads=["hb_in"], writes=["hb_out"], sem="cc1")
    p.dma("gpsimd", lambda g: g.dma_start(
        hlbuf[:, :].rearrange("p (r t) -> p r t", t=48),
        hb_out.ap()[1:3, :].rearrange("r (p t) -> p r t", p=128)),
        reads=["hb_out"], writes=["hlbuf"], sem="hbo")
    p.op("vector", lambda v: v.tensor_scalar_mul(hlbuf[:, 0:48], hlbuf[:, 0:48], bmasks[:, 0:1]),
         reads=["hlbuf", "bmasks"], writes=["hlbuf"])
    p.op("vector", lambda v: v.tensor_scalar_mul(hlbuf[:, 48:96], hlbuf[:, 48:96], bmasks[:, 1:2]),
         reads=["hlbuf", "bmasks"], writes=["hlbuf"])

    acc_a, acc_b = SC[:, 0:1024], SC[:, 1024:2048]
    def conv_tile_f(buf_ap, key, gidx):
        src = buf_ap
        w0 = cwsb[:, gidx * 3 + 0: gidx * 3 + 1]
        w1_ = cwsb[:, gidx * 3 + 1: gidx * 3 + 2]
        w2 = cwsb[:, gidx * 3 + 2: gidx * 3 + 3]
        bcol = cbsb[:, gidx:gidx + 1]
        hl = hlbuf[:, gidx:gidx + 1]
        hr = hlbuf[:, 48 + gidx: 48 + gidx + 1]
        p.op("vector", lambda v: v.tensor_scalar_mul(acc_a[:, 0:1], hl, w0),
             reads=["hlbuf", "cwsb"], writes=["SC.0"])
        p.op("vector", lambda v: v.tensor_scalar_mul(acc_a[:, 1:T], src[:, 0:T - 1], w0),
             reads=[key, "cwsb"], writes=["SC.0"])
        p.op("vector", lambda v: v.scalar_tensor_tensor(
                acc_b[:, :], src, w1_, acc_a[:, :], ALU.mult, ALU.add),
             reads=[key, "cwsb", "SC.0"], writes=["SC.1"])
        p.op("vector", lambda v: v.scalar_tensor_tensor(
                acc_a[:, T - 1:T], hr, w2, acc_b[:, T - 1:T], ALU.mult, ALU.add),
             reads=["hlbuf", "cwsb", "SC.1"], writes=["SC.0"])
        p.op("vector", lambda v: v.scalar_tensor_tensor(
                acc_a[:, 0:T - 1], src[:, 1:T], w2, acc_b[:, 0:T - 1], ALU.mult, ALU.add),
             reads=[key, "cwsb", "SC.1"], writes=["SC.0"])
        p.op("scalar", lambda s: s.activation(src, acc_a[:, :], AF.Silu, bias=bcol),
             reads=["SC.0", "cbsb", key], writes=[key])

    for j in range(NT):
        conv_tile_f(r1t(j), kR1(j), j)
        conv_tile_f(r1t(16 + j), kR1(16 + j), 16 + j)
        pj = PR[:, (j % 2) * T:(j % 2 + 1) * T]
        pjk = f"PR.{j % 2}"
        p.op("vector", lambda v, j=j, pj=pj: v.tensor_tensor(
                pj, r1t(j), r1t(16 + j), ALU.mult),
             reads=[kR1(j), kR1(16 + j)], writes=[pjk])
        p.op("vector", lambda v, j=j, pj=pj: v.reduce_sum(
                cs_all[:, j * NCHL:(j + 1) * NCHL].unsqueeze(2),
                pj.rearrange("p (c s) -> p c s", s=CH), axis=AX.X),
             reads=[pjk], writes=[f"cs.{j}"])
        p.op("vector", lambda v, j=j: v.tensor_tensor_scan(
                prevb[:, j * (NCHL + 1) + 1: j * (NCHL + 1) + 1 + NCHL],
                decay[:, j:j + 1].to_broadcast([128, NCHL]),
                cs_all[:, j * NCHL:(j + 1) * NCHL], 0.0, ALU.mult, ALU.add),
             reads=[f"cs.{j}", "decay", "prevb.z"], writes=[f"prev.{j}"])
        bank = pb[6].bitcast(BF16)
        def f_tr(t, j=j, bank=bank):
            insts = []
            for i in range(8):
                insts.append(t.transpose(
                    bank[:, i * 128:(i + 1) * 128],
                    R1[:, j * T + i * 128: j * T + (i + 1) * 128], identb[:, :]))
            return insts
        p.op("tensor", f_tr, reads=[kR1(j), "identb", pjk], writes=["pb.6"])
        p.op("vector", lambda v, j=j, bank=bank: v.tensor_copy(r1t(j), bank[:, 0:1024]),
             reads=["pb.6"], writes=[kR1(j)])

    p.op("vector", lambda v: v.tensor_copy(
            hstage[:, 96:96 + NT].unsqueeze(2),
            prevb[:, :].rearrange("p (j c) -> p j c", c=NCHL + 1)[:, :, NCHL:NCHL + 1]),
         reads=[f"prev.{j}" for j in range(NT)], writes=["finst"])
    p.dma("sync", lambda g: g.dma_start(st_in.ap(), hstage[:, 96:96 + NT]),
          reads=["finst"], writes=["st_in"], sem="sti")
    p.coll(lambda g: g.collective_compute(
        "AllGather", ALU.bypass, replica_groups=RG,
        ins=[st_in.ap().opt()], outs=[st_out.ap().opt()]),
        reads=["st_in"], writes=["st_out"], sem="cc2")
    p.dma("gpsimd", lambda g: g.dma_start(s0f[:, :], st_out.ap()[0:128, :]),
          reads=["st_out"], writes=["s0f"], sem="sto")
    p.op("vector", lambda v: v.tensor_scalar_mul(s0f[:, :], s0f[:, :], bmasks[:, 0:1]),
         reads=["s0f", "bmasks"], writes=["s0f"])
    for j in range(NT):
        p.op("vector", lambda v, j=j: v.tensor_scalar_mul(
                powm[:, :], cidxs[:, :], alogs[:, j:j + 1]),
             reads=["cidxs", "alogs"], writes=["powm"])
        p.op("scalar", lambda s: s.activation(powm[:, :], powm[:, :], AF.Exp, scale=63.0),
             reads=["powm"], writes=["powm"])
        p.op("vector", lambda v, j=j: v.tensor_scalar_mul(
                powm[:, :], powm[:, :], s0f[:, j:j + 1]),
             reads=["powm", "s0f"], writes=["powm"])
        p.op("vector", lambda v, j=j: v.tensor_add(
                prevb[:, j * (NCHL + 1): j * (NCHL + 1) + NCHL],
                prevb[:, j * (NCHL + 1): j * (NCHL + 1) + NCHL], powm[:, :]),
             reads=["powm", f"prev.{j}"], writes=[f"prev.{j}"])

    for j in range(NT):
        conv_tile_f(rbt(j), kRB(j), 32 + j)
        p.op("scalar", lambda s, j=j: s.activation(SC[:, 0:1024], rzt(j), AF.Silu),
             reads=[kRZ(j)], writes=["SC.0"])
        pj = PR[:, (j % 2) * T:(j % 2 + 1) * T]
        pjk = f"PR.{j % 2}"
        prev_j = prevb[:, j * (NCHL + 1): j * (NCHL + 1) + NCHL]
        p.op("vector", lambda v, j=j, pj=pj, prev_j=prev_j: v.tensor_tensor(
                pj.rearrange("p (c s) -> p c s", s=CH),
                prev_j.unsqueeze(2).to_broadcast([128, NCHL, CH]),
                rbt(j).rearrange("p (c s) -> p c s", s=CH), ALU.mult),
             reads=[f"prev.{j}", kRB(j), "prevb.z"], writes=[pjk])
        for g_ in range(2):
            gpar = (j * 2 + g_) % 2
            yb = pb[2 + gpar]
            ybk = f"pb.{2 + gpar}"
            for e in range(2):
                ob = pb[4 + e]
                obk = f"pb.{4 + e}"
                yo = yoff_sb[gpar * 2 + e]
                p.op("tensor", lambda t, j=j, g_=g_, e=e, ob=ob, pj=pj: t.matmul(
                        ob[0:1, :], ones128b[e * 64:e * 64 + 64, :],
                        pj[e * 64:e * 64 + 64, g_ * 512:(g_ + 1) * 512],
                        start=True, stop=True),
                     reads=[pjk, "ones128b"], writes=[obk])
                p.op("vector", lambda v, yo=yo, ob=ob: v.tensor_copy(yo[:, :], ob[0:1, :]),
                     reads=[obk], writes=["ROWA"])
                gb = pb[e]
                gbk = f"pb.{e}"
                def f_g(t, j=j, g_=g_, e=e, gb=gb):
                    for ci in range(8):
                        c = g_ * 8 + ci
                        rh = (c % 2) * 64
                        inst = t.matmul(
                            gb[rh:rh + 64, ci * 64:(ci + 1) * 64],
                            R1[e * 64:e * 64 + 64, (16 + j) * T + c * CH: (16 + j) * T + (c + 1) * CH],
                            RB[e * 64:e * 64 + 64, j * T + c * CH: j * T + (c + 1) * CH],
                            start=True, stop=True)
                    return inst
                p.op("tensor", f_g, reads=[kR1(16 + j), kRB(j)], writes=[gbk])
                p.op("vector", lambda v, e=e, gb=gb: v.tensor_tensor(
                        gt_sb[e][:, :].rearrange("p (ci l) -> p ci l", l=64),
                        gb[:, :].rearrange("p (ci l) -> p ci l", l=64),
                        maskb[:, :].unsqueeze(1).to_broadcast([128, 8, 64]), ALU.mult),
                     reads=[gbk, "maskb", kWS(e)], writes=[f"gts.{e}", kWS(e)])
            def f_y(t, j=j, g_=g_, gpar=gpar, yb=yb):
                for ci in range(8):
                    c = g_ * 8 + ci
                    sb = c // 2
                    mrow = (c % 2) * 64
                    for e in range(2):
                        t.matmul(
                            yb[e * 64:e * 64 + 64, ci * 64:(ci + 1) * 64],
                            R1[mrow:mrow + 64, j * T + sb * 128 + e * 64: j * T + sb * 128 + e * 64 + 64],
                            gt_sb[e][mrow:mrow + 64, ci * 64:(ci + 1) * 64],
                            start=True, stop=False)
                        inst = t.matmul(
                            yb[e * 64:e * 64 + 64, ci * 64:(ci + 1) * 64],
                            onesrowb[0:1, 0:64],
                            yoff_sb[gpar * 2 + e][0:1, ci * 64:(ci + 1) * 64],
                            start=False, stop=True)
                return inst
            p.op("tensor", f_y,
                 reads=["gts.0", "gts.1", kWS(0), kWS(1), kR1(j), "onesrowb", "ROWA"],
                 writes=[ybk])
            p.op("vector", lambda v, j=j, g_=g_, yb=yb: v.tensor_mul(
                    rzt(j)[:, g_ * 512:(g_ + 1) * 512], yb[:, :],
                    SC[:, g_ * 512:(g_ + 1) * 512]),
                 reads=[ybk, "SC.0"], writes=[kRZ(j)])

    sqb = R1[:, 0:1024]
    for kc in range(DC):
        p.op("scalar", lambda s, kc=kc: s.activation(sqb[:, 0:1024], rzt(kc), AF.Square),
             reads=[kRZ(kc)], writes=[kR1(0)])
        def f_st(t, kc=kc):
            for nh in range(2):
                inst = t.matmul(pb[6 + nh][0:1, :], ones128b[:, :],
                                sqb[:, nh * 512:(nh + 1) * 512],
                                start=(kc == 0), stop=(kc == DC - 1))
            return inst
        p.op("tensor", f_st, reads=[kR1(0), "ones128b"], writes=["pb.6", "pb.7"])
    for nh in range(2):
        p.op("vector", lambda v, nh=nh: v.tensor_copy(rrow[:, nh * 512:(nh + 1) * 512], pb[6 + nh][0:1, :]),
             reads=[f"pb.{6 + nh}"], writes=["ROWA"])
    p.op("vector", lambda v: v.tensor_scalar_mul(rrow[:, 0:T], rrow[:, 0:T], 1.0 / D),
         reads=["ROWA"], writes=["ROWA"])
    p.op("scalar", lambda s: s.activation(rrow[:, 0:T], rrow[:, 0:T], AF.Sqrt, bias=consts[0:1, 1:2]),
         reads=["ROWA", "consts"], writes=["ROWA"])
    p.op("vector", lambda v: v.reciprocal(rrow[:, 0:T], rrow[:, 0:T]),
         reads=["ROWA"], writes=["ROWA"])
    ROWB7 = ROWA.bitcast(BF16)
    p.op("scalar", lambda s: s.activation(ROWB7[0:1, 4096:4096 + T], rrow[0:1, 0:T], AF.Copy),
         reads=["ROWA"], writes=["ROWA"])
    def f_bc(t):
        for nh in range(2):
            inst = t.matmul(pb[6 + nh][:, :], onesrowb[:, :],
                            ROWB7[0:1, 4096 + nh * 512: 4096 + (nh + 1) * 512],
                            start=True, stop=True)
        return inst
    p.op("tensor", f_bc, reads=["ROWA", "onesrowb"], writes=["pb.6", "pb.7"])
    for nh in range(2):
        p.op("vector", lambda v, nh=nh: v.tensor_copy(r2b7[:, nh * 512:(nh + 1) * 512], pb[6 + nh][:, :]),
             reads=[f"pb.{6 + nh}"], writes=[kR1(3)])
    for kc in range(DC):
        p.op("vector", lambda v, kc=kc: v.scalar_tensor_tensor(
                rzt(kc), rzt(kc), rmsws[:, kc:kc + 1], r2b7[:, :], ALU.mult, ALU.mult),
             reads=[kRZ(kc), "rmsws", kR1(3)], writes=[kRZ(kc)])
    for oc in range(4):
        for kcg in range(2):
            par = kcg % 2
            wsl_ = WS[:, par * 4096:(par + 1) * 4096]
            p.dma("gpsimd", lambda g, oc=oc, kcg=kcg, wsl_=wsl_: g.dma_start(
                wsl_.rearrange("p (kc o) -> p kc o", o=512),
                w_out[oc, kcg * 8:(kcg + 1) * 8].rearrange("kc k o -> k kc o")),
                writes=[kWS(par * 2), kWS(par * 2 + 1)], sem=f"ws{par}")
            def f_mm(t, oc=oc, kcg=kcg, wsl_=wsl_):
                for kci in range(8):
                    kc = kcg * 8 + kci
                    for tt in range(TTI):
                        inst = t.matmul(
                            pb[tt][:, :],
                            RZ[:, kc * T + tt * 128: kc * T + (tt + 1) * 128],
                            wsl_[:, kci * 512:(kci + 1) * 512],
                            start=(kc == 0), stop=(kc == DC - 1))
                return inst
            p.op("tensor", f_mm,
                 reads=[kWS(par * 2), kWS(par * 2 + 1)] + [kRZ(kcg * 8 + i) for i in range(8)],
                 writes=[f"pb.{tt}" for tt in range(TTI)])
        for tt in range(TTI):
            spar = tt % 2
            sbuf_st = R1[:, 1024 + spar * 1024: 1024 + spar * 1024 + 512]
            stk = kR1(1 + spar)
            p.op("vector", lambda v, tt=tt, sbuf_st=sbuf_st: v.tensor_copy(
                    sbuf_st, pb[tt][:, :]),
                 reads=[f"pb.{tt}"], writes=[stk])
            p.dma("sync", lambda s, oc=oc, tt=tt, sbuf_st=sbuf_st: s.dma_start(
                s0_sp.ap()[tt * 128:(tt + 1) * 128, oc * 512:(oc + 1) * 512], sbuf_st),
                reads=[stk], writes=[f"s0o.{oc}.{tt}"], sem=f"s0sp{spar}")
    def h_tile(ic):
        if ic < 32:  return r1t(ic), kR1(ic)
        if ic < 48:  return rzt(ic - 32), kRZ(ic - 32)
        return rbt(ic - 48), kRB(ic - 48)
    for i in range(64):
        par = i % 2
        wg = WS[:, par * 2048:(par + 1) * 2048]
        wu = WS[:, 4096 + par * 2048: 4096 + (par + 1) * 2048]
        p.dma("gpsimd", lambda g, i=i, wg=wg: g.dma_start(
            wg.rearrange("p (kc o) -> p kc o", o=128),
            w_gu[i].rearrange("kc k o -> k kc o")),
            writes=[kWS(par)], sem=f"ws{par}")
        p.dma("gpsimd", lambda g, i=i, wu=wu: g.dma_start(
            wu.rearrange("p (kc o) -> p kc o", o=128),
            w_gu[64 + i].rearrange("kc k o -> k kc o")),
            writes=[kWS(2 + par)], sem=f"wu{par}")
        pg = [pb[par * 4], pb[par * 4 + 1]]
        pu = [pb[par * 4 + 2], pb[par * 4 + 3]]
        pgk = [f"pb.{par * 4}", f"pb.{par * 4 + 1}"]
        puk = [f"pb.{par * 4 + 2}", f"pb.{par * 4 + 3}"]
        def f_gu(t, wg=wg, wu=wu, pg=pg, pu=pu):
            for kc in range(DC):
                for nh in range(2):
                    rhs = RA[:, kc * T + nh * 512: kc * T + (nh + 1) * 512]
                    t.matmul(pg[nh][:, :], wg[:, kc * 128:(kc + 1) * 128], rhs,
                             start=(kc == 0), stop=(kc == DC - 1))
                    inst = t.matmul(pu[nh][:, :], wu[:, kc * 128:(kc + 1) * 128], rhs,
                                    start=(kc == 0), stop=(kc == DC - 1))
            return inst
        p.op("tensor", f_gu, reads=[kWS(par), kWS(2 + par)] + xn0_keys, writes=pgk + puk)
        hdst, hkey = h_tile(i)
        for nh in range(2):
            p.op("scalar", lambda s, nh=nh, pg=pg: s.activation(
                    SC[:, nh * 512:(nh + 1) * 512], pg[nh][:, :], AF.Silu),
                 reads=[pgk[nh]], writes=["SC.0"])
            p.op("vector", lambda v, nh=nh, hdst=hdst, pu=pu: v.tensor_mul(
                    hdst[:, nh * 512:(nh + 1) * 512],
                    SC[:, nh * 512:(nh + 1) * 512], pu[nh][:, :]),
                 reads=["SC.0", puk[nh]], writes=[hkey])
    hkeys_all = [h_tile(i)[1] for i in range(64)]
    for oc in range(4):
        for icg in range(8):
            par = icg % 2
            wsl_ = WS[:, par * 4096:(par + 1) * 4096]
            p.dma("gpsimd", lambda g, oc=oc, icg=icg, wsl_=wsl_: g.dma_start(
                wsl_.rearrange("p (ic o) -> p ic o", o=512),
                w_d[oc, icg * 8:(icg + 1) * 8].rearrange("ic k o -> k ic o")),
                writes=[kWS(par * 2), kWS(par * 2 + 1)], sem=f"ws{par}")
            def f_dn(t, oc=oc, icg=icg, wsl_=wsl_):
                for ici in range(8):
                    ic = icg * 8 + ici
                    hsrc, _ = h_tile(ic)
                    for tt in range(TTI):
                        inst = t.matmul(
                            pb[tt][:, :], hsrc[:, tt * 128:(tt + 1) * 128],
                            wsl_[:, ici * 512:(ici + 1) * 512],
                            start=(ic == 0), stop=(ic == 63))
                return inst
            p.op("tensor", f_dn,
                 reads=[kWS(par * 2), kWS(par * 2 + 1)] + [h_tile(icg * 8 + k)[1] for k in range(8)],
                 writes=[f"pb.{tt}" for tt in range(TTI)])
        for tt in range(TTI):
            spar = tt % 2
            sbuf_st = SCb[:, spar * 2048: spar * 2048 + 512]
            stk = f"SC.{spar}"
            p.op("scalar", lambda s, tt=tt, sbuf_st=sbuf_st: s.activation(
                    sbuf_st, pb[tt][:, :], AF.Silu),
                 reads=[f"pb.{tt}"], writes=[stk])
            p.dma("sync", lambda s, oc=oc, tt=tt, sbuf_st=sbuf_st: s.dma_start(
                g0_sp.ap()[tt * 128:(tt + 1) * 128, oc * 512:(oc + 1) * 512], sbuf_st),
                reads=[stk], writes=[f"g0o.{oc}.{tt}"], sem=f"g0sp{spar}")

    for tt in range(TTI):
        slot = tt % 2
        xsl = SC[:, 0:2048]
        kx0, kx1 = "SC.0", "SC.1"
        s0b = R1[:, 4096 + slot * 2048: 4096 + (slot + 1) * 2048]
        ks0 = [kR1(4 + slot * 2), kR1(5 + slot * 2)]
        lnout = R1[:, slot * 2048:(slot + 1) * 2048]
        lnkeys = [kR1(slot * 2), kR1(slot * 2 + 1)]
        p.dma("gpsimd", lambda g, tt=tt, xsl=xsl: g.dma_start(
            xsl, xh[tt * 128:(tt + 1) * 128, :]),
            writes=[kx0, kx1], sem="xl0")
        p.dma("gpsimd", lambda g, tt=tt, s0b=s0b: g.dma_start(
            s0b, s0_sp.ap()[tt * 128:(tt + 1) * 128, :]),
            reads=[f"s0o.{oc}.{q}" for oc in range(4) for q in range(TTI)],
            writes=ks0, sem=f"s0l{slot}")
        p.dma("gpsimd", lambda g, tt=tt: g.dma_start(
            PR[:, :], g0_sp.ap()[tt * 128:(tt + 1) * 128, :]),
            reads=[f"g0o.{oc}.{q}" for oc in range(4) for q in range(TTI)],
            writes=["PR.0", "PR.1"], sem="g0l")
        for h_ in range(2):
            p.op("vector", lambda v, h_=h_, s0b=s0b: v.tensor_tensor(
                    sq[:, :], s0b[:, h_ * 1024:(h_ + 1) * 1024],
                    PR[:, h_ * 1024:(h_ + 1) * 1024], ALU.mult),
                 reads=[f"PR.{h_}", "sq.0", "sq.1"] + ks0, writes=["sq.0", "sq.1"])
            p.op("vector", lambda v, h_=h_, xsl=xsl: v.tensor_add(
                    xsl[:, h_ * 1024:(h_ + 1) * 1024],
                    xsl[:, h_ * 1024:(h_ + 1) * 1024], sq[:, :]),
                 reads=["sq.0", "sq.1", kx0, kx1], writes=[kx0, kx1])
        p.dma("sync", lambda s, tt=tt, xsl=xsl: s.dma_start(
            x1_sp.ap()[tt * 128:(tt + 1) * 128, :], xsl),
            reads=[kx0, kx1], writes=[f"x1o.{tt}"], sem="x1sp")
        emit_ln_tm(p, xsl, [kx0, kx1], lnout, lnkeys[0], stats, tt, sq,
                   eps_col=consts[:, 0:1], pref="ln1", sqk=("sq.0", "sq.1"))
        p.wr[lnkeys[1]] = list(p.wr.get(lnkeys[0], ()))
        for half in range(2):
            bank = pb[half].bitcast(BF16)
            bkey = f"pb.{half}"
            def f_tr(t, tt=tt, half=half, bank=bank, lnout=lnout):
                insts = []
                for j in range(8):
                    kc = half * 8 + j
                    insts.append(t.transpose(
                        bank[:, j * 128:(j + 1) * 128],
                        lnout[:, kc * 128:(kc + 1) * 128], identb[:, :]))
                return insts
            p.op("tensor", f_tr, reads=lnkeys + ["identb"], writes=[bkey])
            def f_ev(v, tt=tt, half=half, bank=bank):
                dst = RA[:, :].rearrange("p (kc t) -> p kc t", t=T)[
                    :, half * 8:(half + 1) * 8, tt * 128:(tt + 1) * 128]
                return v.tensor_copy(dst, bank[:, 0:1024].rearrange("p (kc t) -> p kc t", t=128))
            p.op("vector", f_ev, reads=[bkey],
                 writes=[kRA(half * 8 + j) for j in range(8)])
    xn1_keys = [kRA(j) for j in range(DC)]

    p.dma("sync", lambda s: s.dma_start(
        xn_in.ap().rearrange("(kc p) t -> p kc t", p=128),
        RA[:, :].rearrange("p (kc t) -> p kc t", t=T)),
        reads=xn1_keys, writes=["xn_in"], sem="xni")
    p.coll(lambda g: g.collective_compute(
        "AllGather", ALU.bypass, replica_groups=RG,
        ins=[xn_in.ap().opt()], outs=[xn_out.ap().opt()]),
        reads=["xn_in"], writes=["xn_out"], sem="cc3")
    for r_ in range(2):
        p.dma("gpsimd", lambda g, r_=r_: g.dma_start(
            R1[:, :].rearrange("p (kc rt) -> p kc rt", rt=2048)[:, :, r_ * T:(r_ + 1) * T],
            xn_out.ap()[r_ * D:(r_ + 1) * D, :].rearrange("(kc p) t -> p kc t", p=128)),
            reads=["xn_out"], writes=[kR1(j) for j in range(32)], sem="xno")
    xfm_keys = [kR1(j) for j in range(32)]

    for kc in range(DC):
        p.op("scalar", lambda s, kc=kc: s.activation(sq[:, 0:1024], rat(kc), AF.Square),
             reads=[kRA(kc)], writes=["sq.0", "sq.1"])
        def f_st(t, kc=kc):
            for nh in range(2):
                t.matmul(pb[2 + nh][0:1, :], ones128b[:, :],
                         RA[:, kc * T + nh * 512: kc * T + (nh + 1) * 512],
                         start=(kc == 0), stop=(kc == DC - 1))
                inst = t.matmul(pb[4 + nh][0:1, :], ones128b[:, :],
                                sq[:, nh * 512:(nh + 1) * 512],
                                start=(kc == 0), stop=(kc == DC - 1))
            return inst
        p.op("tensor", f_st, reads=[kRA(kc), "sq.0", "sq.1", "ones128b"],
             writes=["pb.2", "pb.3", "pb.4", "pb.5"])
    for nh in range(2):
        p.op("vector", lambda v, nh=nh: v.tensor_copy(rrow[:, nh * 512:(nh + 1) * 512], pb[2 + nh][0:1, :]),
             reads=[f"pb.{2 + nh}"], writes=["ROWA"])
        p.op("vector", lambda v, nh=nh: v.tensor_copy(rrow[:, T + nh * 512: T + (nh + 1) * 512], pb[4 + nh][0:1, :]),
             reads=[f"pb.{4 + nh}"], writes=["ROWA"])
    p.op("vector", lambda v: v.tensor_scalar_mul(rrow[:, :], rrow[:, :], 1.0 / D),
         reads=["ROWA"], writes=["ROWA"])
    p.op("vector", lambda v: v.tensor_mul(r2row[:, :], rrow[:, 0:T], rrow[:, 0:T]),
         reads=["ROWA"], writes=["ROWA"])
    p.op("vector", lambda v: v.tensor_sub(r2row[:, :], r2row[:, :], rrow[:, T:2 * T]),
         reads=["ROWA", "ROWA"], writes=["ROWA"])
    p.op("scalar", lambda s: s.activation(r2row[:, :], r2row[:, :], AF.Sqrt,
                                          bias=consts[0:1, 0:1], scale=-1.0),
         reads=["ROWA", "consts"], writes=["ROWA"])
    p.op("vector", lambda v: v.reciprocal(r2row[:, :], r2row[:, :]),
         reads=["ROWA"], writes=["ROWA"])
    p.op("vector", lambda v: v.tensor_mul(mrrow[:, :], rrow[:, 0:T], r2row[:, :]),
         reads=["ROWA", "ROWA"], writes=["ROWA"])
    ROWB11 = ROWA.bitcast(BF16)
    p.op("scalar", lambda s: s.activation(ROWB11[0:1, 4096:4096 + T], r2row[0:1, :], AF.Copy),
         reads=["ROWA"], writes=["ROWA"])
    p.op("scalar", lambda s: s.activation(ROWB11[0:1, 4096 + T:4096 + 2 * T], mrrow[0:1, :], AF.Copy),
         reads=["ROWA"], writes=["ROWA"])
    def f_bc2(t):
        for nh in range(2):
            t.matmul(pb[2 + nh][:, :], onesrowb[:, :],
                     ROWB11[0:1, 4096 + nh * 512: 4096 + (nh + 1) * 512],
                     start=True, stop=True)
            inst = t.matmul(pb[4 + nh][:, :], onesrowb[:, :],
                            ROWB11[0:1, 4096 + T + nh * 512: 4096 + T + (nh + 1) * 512],
                            start=True, stop=True)
        return inst
    p.op("tensor", f_bc2, reads=["ROWA", "onesrowb"],
         writes=["pb.2", "pb.3", "pb.4", "pb.5"])
    for nh in range(2):
        p.op("vector", lambda v, nh=nh: v.tensor_copy(r2bb[:, nh * 512:(nh + 1) * 512], pb[2 + nh][:, :]),
             reads=[f"pb.{2 + nh}"], writes=[kRZ(0)])
        p.op("vector", lambda v, nh=nh: v.tensor_copy(mrbb[:, nh * 512:(nh + 1) * 512], pb[4 + nh][:, :]),
             reads=[f"pb.{4 + nh}"], writes=[kRZ(1)])
    for kc in range(DC):
        p.op("vector", lambda v, kc=kc: v.tensor_mul(sq[:, 0:T], rat(kc), r2bb[:, :]),
             reads=[kRA(kc), kRZ(0), "sq.0", "sq.1"], writes=["sq.0", "sq.1"])
        p.op("vector", lambda v, kc=kc: v.tensor_sub(sq[:, 0:T], sq[:, 0:T], mrbb[:, :]),
             reads=[kRZ(1), "sq.0", "sq.1"], writes=["sq.0", "sq.1"])
        p.op("vector", lambda v, kc=kc: v.tensor_scalar(
                rbt(kc), sq[:, 0:T], lnwbs[:, kc:kc + 1], lnwbs[:, DC + kc: DC + kc + 1],
                ALU.mult, ALU.add),
             reads=["sq.0", "sq.1", "lnwbs"], writes=[kRB(kc)])
    lnx_keys = [kRB(j) for j in range(DC)]
    for oc in range(4):
        for kcg in range(2):
            par = kcg % 2
            wsl_ = WS[:, par * 4096:(par + 1) * 4096]
            p.dma("gpsimd", lambda g, oc=oc, kcg=kcg, wsl_=wsl_: g.dma_start(
                wsl_.rearrange("p (kc o) -> p kc o", o=512),
                w1[oc, kcg * 8:(kcg + 1) * 8].rearrange("kc k o -> k kc o")),
                writes=[kWS(par * 2), kWS(par * 2 + 1)], sem=f"ws{par}")
            p.op("vector", lambda v, wsl_=wsl_: v.reduce_max(
                    amx_t[:, :], wsl_, axis=AX.X, apply_absolute_value=True),
                 reads=[kWS(par * 2), kWS(par * 2 + 1)], writes=["amx_t"])
            p.op("vector", lambda v: v.tensor_max(amaxc[:, :], amaxc[:, :], amx_t[:, :]),
                 reads=["amx_t", "amaxc"], writes=["amaxc"])
            p.op("scalar", lambda s, wsl_=wsl_: s.activation(wsl_, wsl_, AF.Sign),
                 reads=[kWS(par * 2), kWS(par * 2 + 1)],
                 writes=[kWS(par * 2), kWS(par * 2 + 1)])
            def f_mm1(t, oc=oc, kcg=kcg, wsl_=wsl_):
                for kci in range(8):
                    kc = kcg * 8 + kci
                    for tt in range(TTI):
                        inst = t.matmul(
                            pb[tt][:, :],
                            RB[:, kc * T + tt * 128: kc * T + (tt + 1) * 128],
                            wsl_[:, kci * 512:(kci + 1) * 512],
                            start=(kc == 0), stop=(kc == DC - 1))
                return inst
            p.op("tensor", f_mm1,
                 reads=[kWS(par * 2), kWS(par * 2 + 1)] + [kRB(kcg * 8 + i) for i in range(8)],
                 writes=[f"pb.{tt}" for tt in range(TTI)])
        for tt in range(TTI):
            p.op("vector", lambda v, oc=oc, tt=tt: v.tensor_copy(
                    RZ[:, tt * D + oc * 512: tt * D + (oc + 1) * 512], pb[tt][:, :]),
                 reads=[f"pb.{tt}"], writes=[kRZ((tt * D + oc * 512) // T)])
    p.op("tensor", lambda t: t.transpose(pb[0].bitcast(BF16)[0:1, 0:128], amaxc[:, 0:1], identb[:, :]),
         reads=["amaxc", "identb"], writes=["pb.0"])
    p.op("vector", lambda v: v.tensor_copy(amrow[:, :], pb[0].bitcast(BF16)[0:1, 0:128]),
         reads=["pb.0"], writes=["ROWA"])
    p.op("vector", lambda v: v.reduce_max(scal1[:, :], amrow[:, :], axis=AX.X),
         reads=["ROWA"], writes=["ROWA"])
    p.op("scalar", lambda s: s.activation(ROWA.bitcast(BF16)[0:1, 0:1], scal1[:, :], AF.Copy),
         reads=["ROWA"], writes=["ROWA"])
    p.op("tensor", lambda t: t.matmul(pb[1][:, 0:1], onesrowb[:, :],
                                      ROWA.bitcast(BF16)[0:1, 0:1], start=True, stop=True),
         reads=["ROWA", "onesrowb"], writes=["pb.1"])
    p.op("vector", lambda v: v.tensor_copy(scalc[:, :], pb[1][:, 0:1]),
         reads=["pb.1"], writes=["scalc"])
    grepb = RB[:, 12 * T: 14 * T]
    brepb = RB[:, 14 * T: 16 * T]
    p.dma("gpsimd", lambda g: g.dma_start(
        grepb, gamrow[0:1, 0:D].to_broadcast([128, D])),
        writes=[kRB(12), kRB(13)], sem="grep")
    p.dma("gpsimd", lambda g: g.dma_start(
        brepb, gamrow[0:1, D:2 * D].to_broadcast([128, D])),
        writes=[kRB(14), kRB(15)], sem="brep")
    p.op("vector", lambda v: v.tensor_scalar_mul(grepb, grepb, scalc[:, 0:1]),
         reads=[kRB(12), kRB(13), "scalc"], writes=[kRB(12), kRB(13)])
    for tt in range(TTI):
        for oc in range(4):
            spar = (tt * 4 + oc) % 2
            sbuf_st = PR[:, spar * 1024: spar * 1024 + 512]
            stk = f"PR.{spar}"
            p.op("vector", lambda v, tt=tt, oc=oc: v.tensor_tensor(
                    sq[:, 0:512], RZ[:, tt * D + oc * 512: tt * D + (oc + 1) * 512],
                    grepb[:, oc * 512:(oc + 1) * 512], ALU.mult),
                 reads=[kRZ((tt * D + oc * 512) // T), kRB(12 + oc // 2), "sq.0"],
                 writes=["sq.0"])
            p.op("vector", lambda v, oc=oc: v.tensor_add(
                    sq[:, 0:512], sq[:, 0:512], brepb[:, oc * 512:(oc + 1) * 512]),
                 reads=[kRB(14 + oc // 2), "sq.0"], writes=["sq.0"])
            p.op("scalar", lambda s, sbuf_st=sbuf_st: s.activation(sbuf_st, sq[:, 0:512], AF.Silu),
                 reads=["sq.0"], writes=[stk])
            p.dma("sync", lambda s, tt=tt, oc=oc, sbuf_st=sbuf_st: s.dma_start(
                g1_sp.ap()[tt * 128:(tt + 1) * 128, oc * 512:(oc + 1) * 512], sbuf_st),
                reads=[stk], writes=[f"g1o.{oc}.{tt}"], sem=f"g1sp{spar}")

    cs2D = RZ[:, 0:4096]
    mask4D = RB[:, 8192:10240]
    attnD = RB[:, 10240:12288]
    p.dma("gpsimd", lambda g: g.dma_start(cs2D, cs2[:, :]),
          writes=[kRZ(0), kRZ(1), kRZ(2), kRZ(3)], sem="dcs2")
    p.dma("gpsimd", lambda g: g.dma_start(mask4D, mask4[:, :]),
          writes=[kRB(8), kRB(9)], sem="dmask4")
    kcs = [kRZ(0), kRZ(1), kRZ(2), kRZ(3)]
    kmk = [kRB(8), kRB(9)]
    kat = [kRB(10), kRB(11)]
    wvA, wvB = WS[:, 0:4096], WS[:, 4096:8192]
    for grp in range(2):
        p.dma("gpsimd", lambda g, grp=grp: g.dma_start(
            wvA.rearrange("p (kc o) -> p kc o", o=256),
            wv_b[grp, :, :, 0:256].rearrange("kc k o -> k kc o")),
            writes=[kWS(0), kWS(1)], sem="wva")
        p.dma("gpsimd", lambda g, grp=grp: g.dma_start(
            wvB.rearrange("p (kc o) -> p kc o", o=256),
            wv_b[grp, :, :, 256:512].rearrange("kc k o -> k kc o")),
            writes=[kWS(2), kWS(3)], sem="wvb")
        for tth in range(2):
            def f_v(t, grp=grp, tth=tth):
                for kc in range(DC):
                    for tt in range(TTI):
                        t.matmul(
                            pb[tt][:, 0:256],
                            R1[:, kc * SEQ + (tth * 8 + tt) * 128: kc * SEQ + (tth * 8 + tt + 1) * 128],
                            wvA[:, kc * 256:(kc + 1) * 256],
                            start=(kc == 0), stop=False)
                        inst = t.matmul(
                            pb[tt][:, 256:512],
                            R1[:, kc * SEQ + (tth * 8 + tt) * 128: kc * SEQ + (tth * 8 + tt + 1) * 128],
                            wvB[:, kc * 256:(kc + 1) * 256],
                            start=False, stop=(kc == DC - 1))
                return inst
            p.op("tensor", f_v, reads=[kWS(0), kWS(1), kWS(2), kWS(3)] + xfm_keys,
                 writes=[f"pb.{tt}" for tt in range(TTI)])
            for tt in range(TTI):
                gt = tth * 8 + tt
                p.op("vector", lambda v, grp=grp, gt=gt, tt=tt: v.tensor_copy(
                        RA[:, (gt * 8 + grp * 4) * 128:(gt * 8 + grp * 4 + 4) * 128], pb[tt][:, :]),
                     reads=[f"pb.{tt}"], writes=[f"v.{grp}.{gt}"])
    wq_ = WS[:, 0:4096]
    kwq = [kWS(0), kWS(1)]
    ROWB = ROWA.bitcast(BF16)
    RBF = RB.bitcast(F32)
    RZF = RZ.bitcast(F32)
    def head_bufs(hl):
        hp_ = hl % 2
        return (RB[:, hp_ * 4096: hp_ * 4096 + 2048],
                RB[:, hp_ * 4096 + 2048: hp_ * 4096 + 4096],
                [kRB(hp_ * 4), kRB(hp_ * 4 + 1)],
                [kRB(hp_ * 4 + 2), kRB(hp_ * 4 + 3)])
    def qk_res(hl, side, nq):
        pc = nq % 2
        qbD, kbD, kq, kk = head_bufs(hl)
        return dict(
            pc=pc, side=side, nq=nq,
            qkb=pb[4] if pc == 0 else pb[6], qkk="pb.4" if pc == 0 else "pb.6",
            swb=pb[5] if pc == 0 else pb[7], swk="pb.5" if pc == 0 else "pb.7",
            stA=pb[4] if pc == 0 else pb[6], stB=pb[5] if pc == 0 else pb[7],
            stAk="pb.4" if pc == 0 else "pb.6", stBk="pb.5" if pc == 0 else "pb.7",
            ra_=RZF[:, 3072:3584] if pc == 0 else RBF[:, 6144:6656],
            rak=kRZ(6) if pc == 0 else kRB(12),
            cbf_=RZF[:, 3584:4096] if pc == 0 else RBF[:, 6656:7168],
            cbk=kRZ(7) if pc == 0 else kRB(13),
            rbc=RZ[:, 4096 + pc * 1024: 4096 + pc * 1024 + 512],
            rbk=kRZ(4 + pc),
            sqh=RZ[:, 4096 + pc * 1024 + 512: 4096 + pc * 1024 + 1024],
            mbs_=PR[:, pc * 1024: pc * 1024 + 512],
            vr_=PR[:, pc * 1024 + 512: pc * 1024 + 1024],
            prk=f"PR.{pc}", rowo=pc * 512,
            dst=qbD if side == 0 else kbD, dks=kq if side == 0 else kk)
    def emit_chunk(hl, side, nq):
        r = qk_res(hl, side, nq)
        def f_qk(t, r=r):
            for kc in range(DC):
                inst = t.matmul(
                    r["qkb"][:, :],
                    wq_[:, (r["side"] * DC + kc) * 128:(r["side"] * DC + kc + 1) * 128],
                    R1[:, kc * SEQ + r["nq"] * 512: kc * SEQ + (r["nq"] + 1) * 512],
                    start=(kc == 0), stop=(kc == DC - 1))
            return inst
        p.op("tensor", f_qk, reads=kwq + xfm_keys, writes=[r["qkk"]])
        p.op("vector", lambda v, r=r: v.tensor_mul(
                r["rbc"], r["qkb"][:, :],
                cs2D[:, SEQ + r["nq"] * 512: SEQ + (r["nq"] + 1) * 512]),
             reads=[r["qkk"]] + kcs, writes=[r["rbk"]])
        p.op("tensor", lambda t, r=r: t.matmul(r["swb"][:, :], swps[:, :], r["rbc"],
                                               start=True, stop=True),
             reads=[r["rbk"], "swps"], writes=[r["swk"]])
        p.op("vector", lambda v, r=r: v.tensor_mul(
                r["ra_"], r["qkb"][:, :], cs2D[:, r["nq"] * 512:(r["nq"] + 1) * 512]),
             reads=[r["qkk"]] + kcs, writes=[r["rak"]])
        p.op("vector", lambda v, r=r: v.tensor_sub(
                r["cbf_"][0:64, :], r["ra_"][0:64, :], r["swb"][0:64, :]),
             reads=[r["rak"], r["swk"]], writes=[r["cbk"]])
        p.op("vector", lambda v, r=r: v.tensor_add(
                r["cbf_"][64:128, :], r["ra_"][64:128, :], r["swb"][64:128, :]),
             reads=[r["rak"], r["swk"]], writes=[r["cbk"]])
        p.op("scalar", lambda s, r=r: s.activation(r["rbc"], r["cbf_"], AF.Copy),
             reads=[r["cbk"]], writes=[r["rbk"]])
        p.op("scalar", lambda s, r=r: s.activation(r["sqh"], r["cbf_"], AF.Square),
             reads=[r["cbk"]], writes=[r["rbk"]])
        def f_st3(t, r=r):
            t.matmul(r["stA"][0:1, :], ones128b[:, :], r["rbc"], start=True, stop=True)
            return t.matmul(r["stB"][0:1, :], ones128b[:, :], r["sqh"], start=True, stop=True)
        p.op("tensor", f_st3, reads=[r["rbk"], "ones128b"], writes=[r["stAk"], r["stBk"]])
        p.op("scalar", lambda s, r=r: s.activation(
                ROWB[0:1, r["rowo"]:r["rowo"] + 512], r["stA"][0:1, :], AF.Copy),
             reads=[r["stAk"]], writes=["ROWA"])
        p.op("scalar", lambda s, r=r: s.activation(
                ROWB[0:1, 1024 + r["rowo"]:1024 + r["rowo"] + 512], r["stB"][0:1, :], AF.Copy),
             reads=[r["stBk"]], writes=["ROWA"])
        def f_bc3(t, r=r):
            t.matmul(r["stA"][:, :], invrowb[:, :], ROWB[0:1, r["rowo"]:r["rowo"] + 512],
                     start=True, stop=True)
            return t.matmul(r["stB"][:, :], invrowb[:, :],
                            ROWB[0:1, 1024 + r["rowo"]:1024 + r["rowo"] + 512],
                            start=True, stop=True)
        p.op("tensor", f_bc3, reads=["ROWA", "invrowb"], writes=[r["stAk"], r["stBk"]])
        p.op("scalar", lambda s, r=r: s.activation(r["mbs_"], r["stA"][:, :], AF.Copy),
             reads=[r["stAk"]], writes=[r["prk"]])
        p.op("gpsimd", lambda g, r=r: g.tensor_sub(r["cbf_"], r["cbf_"], r["mbs_"]),
             reads=[r["cbk"], r["prk"]], writes=[r["cbk"]])
        p.op("scalar", lambda s, r=r: s.activation(r["vr_"], r["mbs_"], AF.Square),
             reads=[r["prk"]], writes=[r["prk"]])
        p.op("vector", lambda v, r=r: v.tensor_sub(r["vr_"], r["vr_"], r["stB"][:, :]),
             reads=[r["prk"], r["stBk"]], writes=[r["prk"]])
        p.op("scalar", lambda s, r=r: s.activation(r["vr_"], r["vr_"], AF.Ln,
                                                   bias=consts[:, 0:1], scale=-1.0),
             reads=[r["prk"], "consts"], writes=[r["prk"]])
        p.op("scalar", lambda s, r=r: s.activation(r["vr_"], r["vr_"], AF.Exp, scale=-0.5),
             reads=[r["prk"]], writes=[r["prk"]])
        p.op("vector", lambda v, r=r: v.tensor_mul(r["cbf_"], r["cbf_"], r["vr_"]),
             reads=[r["cbk"], r["prk"]], writes=[r["cbk"]])
        p.op("vector", lambda v, r=r: v.tensor_scalar(
                r["dst"][:, r["nq"] * 512:(r["nq"] + 1) * 512], r["cbf_"],
                lnss[:, 0:1], lnss[:, 1:2], ALU.mult, ALU.add),
             reads=[r["cbk"], "lnss"], writes=[r["dks"][r["nq"] // 2]])
    def emit_wq(hl):
        p.dma("gpsimd", lambda g, hl=hl: g.dma_start(
            wq_.rearrange("p (s kc o) -> p s kc o", kc=DC, o=128),
            wqkv[hl].rearrange("s kc k o -> k s kc o")),
            writes=kwq, sem="wq0")
    def emit_core_qs(hl, qs):
        qbD, kbD, kq, kk = head_bufs(hl)
        nk = qs * 4 + 4
        def emit_S(kt):
            spar = kt % 2
            p.op("tensor", lambda t, qs=qs, kt=kt, spar=spar: t.matmul(
                    pb[spar][:, :], kbD[:, kt * 128:(kt + 1) * 128],
                    qbD[:, qs * 512:(qs + 1) * 512], start=True, stop=True),
                 reads=[kk[kt // 8], kq[qs // 2]], writes=[f"pb.{spar}"])
        def emit_pow(kt):
            spar = kt % 2
            sa_ = SC[:, 0:512] if spar == 0 else SC[:, 1536:2048]
            sak = "SC.0" if spar == 0 else "SC.1"
            a4_ = sq[:, 0:512] if spar == 0 else sq[:, 512:1024]
            a4k = f"sq.{spar}"
            p.op("scalar", lambda s, spar=spar, sa_=sa_: s.activation(sa_, pb[spar][:, :], AF.Square),
                 reads=[f"pb.{spar}"], writes=[sak])
            if kt >= qs * 4:
                off = kt - qs * 4
                p.op("gpsimd", lambda g, off=off, sa_=sa_: g.tensor_mul(
                        sa_, sa_, mask4D[:, off * 512:(off + 1) * 512]),
                     reads=[sak] + kmk, writes=[sak])
            p.op("vector", lambda v, sa_=sa_, a4_=a4_: v.tensor_mul(a4_, sa_, sa_),
                 reads=[sak], writes=[a4k])
        def emit_acc(kt):
            spar = kt % 2
            a4_ = sq[:, 0:512] if spar == 0 else sq[:, 512:1024]
            a4k = f"sq.{spar}"
            def f_acc(t, hl=hl, qs=qs, kt=kt, a4_=a4_):
                t.matmul(pb[2][0:1, :], ones128b[:, :], a4_,
                         start=(kt == 0), stop=(kt == qs * 4 + 3))
                return t.matmul(pb[3][:, :], RA[:, (kt * 8 + hl) * 128:(kt * 8 + hl + 1) * 128],
                                a4_, start=(kt == 0), stop=(kt == qs * 4 + 3))
            p.op("tensor", f_acc, reads=[a4k, "ones128b", f"v.{hl // 4}.{kt}"],
                 writes=["pb.2", "pb.3"])
        emit_S(0)
        emit_pow(0)
        for kt in range(nk):
            if kt + 1 < nk:
                emit_S(kt + 1)
            emit_acc(kt)
            if kt + 1 < nk:
                emit_pow(kt + 1)
        p.op("scalar", lambda s: s.activation(ROWA.bitcast(BF16)[0:1, 4096:4608], pb[2][0:1, :], AF.Copy),
             reads=["pb.2"], writes=["ROWD"])
        p.op("tensor", lambda t: t.matmul(pb[2][:, :], onesrowb[:, :],
                                          ROWA.bitcast(BF16)[0:1, 4096:4608], start=True, stop=True),
             reads=["ROWD", "onesrowb"], writes=["pb.2"])
        p.op("scalar", lambda s: s.activation(SC[:, 0:512], pb[2][:, :], AF.Ln),
             reads=["pb.2"], writes=["SC.0"])
        p.op("scalar", lambda s: s.activation(SC[:, 0:512], SC[:, 0:512], AF.Exp, scale=-1.0),
             reads=["SC.0"], writes=["SC.0"])
        p.op("vector", lambda v, qs=qs: v.tensor_mul(
                attnD[:, qs * 512:(qs + 1) * 512], pb[3][:, :], SC[:, 0:512]),
             reads=["pb.3", "SC.0"], writes=[kat[qs // 2]])
    emit_wq(0)
    for ci in range(8):
        emit_chunk(0, ci // 4, ci % 4)
    for hl in range(8):
        for qs in range(4):
            emit_core_qs(hl, qs)
            if hl + 1 < 8:
                if qs == 0:
                    emit_wq(hl + 1)
                emit_chunk(hl + 1, (qs * 2) // 4, (qs * 2) % 4)
                emit_chunk(hl + 1, (qs * 2 + 1) // 4, (qs * 2 + 1) % 4)
        p.dma("sync", lambda s, hl=hl: s.dma_start(
            at_in.ap()[hl * 128:(hl + 1) * 128, :], attnD),
            reads=kat, writes=[f"ato.{hl}"], sem="attout")

    p.coll(lambda g: g.collective_compute(
        "AllGather", ALU.bypass, replica_groups=RG,
        ins=[at_in.ap().opt()], outs=[at_out.ap().opt()]),
        reads=[f"ato.{hl}" for hl in range(8)], writes=["at_out"], sem="cc4")
    def f_ab(g):
        val = g.value_load(hhoffs[:1, :1])
        return g.dma_start(
            RZ[:, :].rearrange("p (kc t) -> p kc t", t=T),
            at_out.ap().rearrange("(kc p) t -> p kc t", p=128)[:, :, bass.ds(val, T)])
    p.dma("gpsimd", f_ab, reads=["at_out", "hhoffs"],
          writes=[kRZ(j) for j in range(DC)], sem="abl")
    abuf_keys = [kRZ(j) for j in range(DC)]

    for oc in range(4):
        gpar = oc % 2

        for kcg in range(2):
            par = kcg % 2
            wsl_ = WS[:, par * 4096:(par + 1) * 4096]
            p.dma("gpsimd", lambda g, oc=oc, kcg=kcg, wsl_=wsl_: g.dma_start(
                wsl_.rearrange("p (kc o) -> p kc o", o=512),
                w_out1[oc, kcg * 8:(kcg + 1) * 8].rearrange("kc k o -> k kc o")),
                writes=[kWS(par * 2), kWS(par * 2 + 1)], sem=f"ws{par}")
            def f_mm2(t, oc=oc, kcg=kcg, wsl_=wsl_):
                for kci in range(8):
                    kc = kcg * 8 + kci
                    for tt in range(TTI):
                        inst = t.matmul(
                            pb[tt][:, :],
                            RZ[:, kc * T + tt * 128: kc * T + (tt + 1) * 128],
                            wsl_[:, kci * 512:(kci + 1) * 512],
                            start=(kc == 0), stop=(kc == DC - 1))
                return inst
            p.op("tensor", f_mm2,
                 reads=[kWS(par * 2), kWS(par * 2 + 1)] + [kRZ(kcg * 8 + i) for i in range(8)],
                 writes=[f"pb.{tt}" for tt in range(TTI)])
        for tt in range(TTI):
            slot = tt % 2
            pslot = (tt // 2) % 2
            xsl = SC[:, slot * 1024: slot * 1024 + 1024]
            kx0 = f"SC.{slot}"
            if tt % 2 == 0:
                th = tt // 2
                p.dma("gpsimd", lambda g, oc=oc, th=th, pslot=pslot: g.dma_start(
                    PR[:, pslot * 1024:(pslot + 1) * 1024].rearrange("p (q o) -> p q o", o=512),
                    g1_sp.ap().rearrange("(q p) d -> p q d", p=128)[:, th * 2:(th + 1) * 2, oc * 512:(oc + 1) * 512]),
                    reads=[f"g1o.{o2}.{q}" for o2 in range(4) for q in range(TTI)],
                    writes=[f"PR.{pslot}"], sem=f"g1l{pslot}")
            p.dma("gpsimd", lambda g, oc=oc, tt=tt, xsl=xsl: g.dma_start(
                xsl[:, 0:512], x1_sp.ap()[tt * 128:(tt + 1) * 128, oc * 512:(oc + 1) * 512]),
                reads=[f"x1o.{q}" for q in range(TTI)], writes=[kx0], sem=f"x1l{slot}")
            p.op("vector", lambda v, tt=tt, xsl=xsl, pslot=pslot: v.tensor_mul(
                    xsl[:, 512:1024], pb[tt][:, :],
                    PR[:, pslot * 1024 + (tt % 2) * 512: pslot * 1024 + (tt % 2 + 1) * 512]),
                 reads=[f"pb.{tt}", f"PR.{pslot}", kx0], writes=[kx0])
            p.op("vector", lambda v, xsl=xsl: v.tensor_add(
                    xsl[:, 0:512], xsl[:, 0:512], xsl[:, 512:1024]),
                 reads=[kx0], writes=[kx0])
            p.dma("sync", lambda s, oc=oc, tt=tt, xsl=xsl: s.dma_start(
                x2[tt * 128:(tt + 1) * 128, oc * 512:(oc + 1) * 512], xsl[:, 0:512]),
                reads=[kx0], writes=[f"x2o.{oc}.{tt}"], sem=f"x2o{slot}")

    p.barrier_all("sync")
    p.finalize()
    ctx.close()
    return nc


def prep_fused(inputs):
    hp = {}
    hp["ident"] = np.eye(128, dtype=np.float32)
    hp["w_in"] = tile_w_out_in(np.ascontiguousarray(np.asarray(inputs["L0_attn_W_in"], np.float32)[:8192]))
    hp["w_gu"] = tile_w_out_in(np.asarray(inputs["L0_ffn_w_gate_up"], np.float32))
    hp["w_d"] = tile_w_out_in_wide(np.asarray(inputs["L0_ffn_w_down"], np.float32), 512)
    hp["w_out"] = tile_w_out_in_wide(np.asarray(inputs["L0_attn_W_out"], np.float32), 512)
    hp["w1"] = tile_w_out_in_wide(np.asarray(inputs["L1_ffn_weight"], np.float32), 512)
    hp["w_out1"] = tile_w_in_out_wide(np.asarray(inputs["L1_attn_w_out"], np.float32), 512)
    conv_w = np.asarray(inputs["L0_attn_conv_w"], np.float32)
    conv_b = np.asarray(inputs["L0_attn_conv_b"], np.float32)
    hp["cw"] = np.ascontiguousarray(conv_w[:, 0, :].reshape(48, 128, 3).transpose(1, 0, 2).reshape(128, 144))
    hp["cb"] = np.ascontiguousarray(conv_b.reshape(48, 128).T)
    A_log_rep = np.repeat(np.asarray(inputs["L0_attn_A_log"], np.float32), 64)
    hp["alog"] = np.ascontiguousarray(A_log_rep.reshape(NT, 128).T)
    hp["mask_ssd"] = np.vstack([np.triu(np.ones((64, 64), np.float32))] * 2)
    hp["rmsw"] = np.ascontiguousarray(np.asarray(inputs["L0_attn_rms_w"], np.float32).reshape(DC, 128).T)
    lnw = np.asarray(inputs["L1_ffn_ln_weight"], np.float32)
    lnb = np.asarray(inputs["L1_ffn_ln_bias"], np.float32)
    hp["lnwb"] = np.ascontiguousarray(np.concatenate([lnw.reshape(DC, 128).T, lnb.reshape(DC, 128).T], 1))
    hp["gamrow"] = np.concatenate([np.asarray(inputs["L1_ffn_gamma"], np.float32),
                                   np.asarray(inputs["L1_ffn_beta"], np.float32)])[None, :]
    hp["lns"] = np.ascontiguousarray(np.stack([np.asarray(inputs["L1_attn_ln_scale"], np.float32),
                                               np.asarray(inputs["L1_attn_ln_bias"], np.float32)], 1))
    inv = 1.0 / 10000.0 ** (np.arange(0, 128, 2) / 128)
    tg = np.arange(SEQ)[:, None] * inv[None, :]
    hp["cs2"] = np.ascontiguousarray(np.hstack([np.vstack([np.cos(tg).T] * 2),
                                                np.vstack([np.sin(tg).T] * 2)]).astype(np.float32))
    mask4 = np.zeros((128, 4 * 512), np.float32)
    for off in range(4):
        for pz in range(128):
            mask4[pz, off * 512 + off * 128 + pz:(off + 1) * 512] = 1.0
    hp["mask4"] = mask4
    hp["swp"] = np.roll(np.eye(128, dtype=np.float32), 64, axis=1)
    hp["cidx"] = np.ascontiguousarray(np.tile(np.arange(NCHL, dtype=np.float32), (128, 1)))
    w_qkv = np.asarray(inputs["L1_attn_w_qkv"], np.float32)
    tiles = np.ascontiguousarray(w_qkv.reshape(DC, 128, 48, 128).transpose(2, 0, 1, 3))
    hp["wqkv_h"] = []
    hp["wv_h"] = []
    for hh in range(2):
        wsel = np.stack([np.stack([tiles[hh * 8 + hl], tiles[16 + hh * 8 + hl]])
                         for hl in range(8)])
        hp["wqkv_h"].append(np.ascontiguousarray(wsel))
        wv = np.stack([np.concatenate([tiles[32 + hh * 8 + grp * 4 + l] for l in range(4)], axis=-1)
                       for grp in range(2)])
        hp["wv_h"].append(np.ascontiguousarray(wv))
    return hp


def fused_in_maps(inputs, hp):
    x = np.asarray(inputs["L0_attn_u"], np.float32)
    im = []
    for core in range(NCORES):
        b, h = core // 2, core % 2
        bm = np.zeros((128, 4), np.float32)
        bm[:, 0] = 1.0 if h == 1 else 0.0
        bm[:, 1] = 1.0 if h == 0 else 0.0
        im.append({
            "xh": np.ascontiguousarray(x[b, h * 1024:(h + 1) * 1024]),
            "w_in": hp["w_in"], "w_gu": hp["w_gu"], "w_d": hp["w_d"],
            "w_out": hp["w_out"], "w1": hp["w1"], "wqkv": hp["wqkv_h"][h],
            "w_out1": hp["w_out1"], "cw": hp["cw"], "cb": hp["cb"],
            "wv_b": hp["wv_h"][h],
            "alog": hp["alog"], "ident": hp["ident"], "mask_ssd": hp["mask_ssd"],
            "rmsw": hp["rmsw"], "lnwb": hp["lnwb"], "gamrow": hp["gamrow"],
            "lns": hp["lns"], "cs2": hp["cs2"], "mask4": hp["mask4"],
            "swp": hp["swp"], "cidx": hp["cidx"], "bmask": bm,
            "hhoff": np.array([[h * 1024]], np.int32),
        })
    return im


def kernel_fused(**inputs):
    hp = prep_fused(inputs)
    im = fused_in_maps(inputs, hp)
    nc = build_fused()
    res = _run(nc, im)
    out = np.zeros((BATCH, SEQ, D), dtype=np.float32)
    for core in range(NCORES):
        b, h = core // 2, core % 2
        out[b, h * 1024:(h + 1) * 1024] = res[core]["x2"]
    return out
